# version 08
# baseline (speedup 1.0000x reference)
import contextlib
import numpy as np
import ml_dtypes
import concourse.bass as bass
import concourse.mybir as mybir
from concourse.bass_utils import run_bass_kernel_spmd

F32 = mybir.dt.float32
BF16 = mybir.dt.bfloat16
I8 = mybir.dt.int8
AF = mybir.ActivationFunctionType
ALU = mybir.AluOpType
AX = mybir.AxisListType

D = 2048
T = 2048
BPC = 2
NTOK = BPC * T
NSTRIP = NTOK // 512
KT = D // 128

ENGS = ("tensor", "vector", "scalar", "gpsimd", "sync")


class Tok:
    __slots__ = ("eng", "val")

    def __init__(self, eng, val):
        self.eng = eng
        self.val = val


_ENGMAP = {"EngineType.PE": "tensor", "EngineType.DVE": "vector",
           "EngineType.Activation": "scalar", "EngineType.Pool": "gpsimd",
           "EngineType.SP": "sync"}


class Ctx:

    def __init__(self, nc):
        self.nc = nc
        self.sem = {}
        self.cnt = {}
        self._stack = None
        self.eclock = {e: {} for e in ENGS}
        self.snap = {}

    def make_sems(self, stack):
        self._stack = stack
        for e in ENGS:
            self.sem[e] = stack.enter_context(self.nc.semaphore(f"sem_{e}"))
            self.cnt[e] = 0

    def _get(self, name):
        if name not in self.sem:
            self.sem[name] = self._stack.enter_context(self.nc.semaphore(name))
            self.cnt[name] = 0
        return self.sem[name]

    def _engname(self, instr_or_handle):
        e = getattr(instr_or_handle, "engine", None)
        return _ENGMAP.get(str(e), None)

    def inc(self, eng, instr):
        instr.then_inc(self.sem[eng], 1)
        self.cnt[eng] += 1
        snapshot = dict(self.eclock[eng])
        snapshot[eng] = self.cnt[eng]
        self.snap[(eng, self.cnt[eng])] = snapshot
        return Tok(eng, self.cnt[eng])

    def dinc(self, stream, instr):
        key = "D_" + stream
        sem = self._get(key)
        instr.then_inc(sem, 16)
        self.cnt[key] += 16
        eng = self._engname(instr)
        snapshot = dict(self.eclock[eng]) if eng else {}
        snapshot[key] = self.cnt[key]
        self.snap[(key, self.cnt[key])] = snapshot
        return Tok(key, self.cnt[key])

    def wait(self, handle, *toks):
        best = {}

        def add(t):
            if t is None:
                return
            if isinstance(t, (list, tuple)):
                for x in t:
                    add(x)
            else:
                best[t.eng] = max(best.get(t.eng, 0), t.val)
        add(toks)
        eng = self._engname(handle)
        clk = self.eclock[eng] if eng else {}
        for sname, v in best.items():
            if clk.get(sname, 0) >= v:
                continue
            if sname == eng:
                handle.wait_ge(self.sem[sname], v)
                clk[sname] = max(clk.get(sname, 0), v)
                continue
            handle.wait_ge(self.sem[sname], v)
            snapshot = self.snap.get((sname, v))
            if snapshot:
                for k2, v2 in snapshot.items():
                    if k2 == eng:
                        continue
                    if clk.get(k2, 0) < v2:
                        clk[k2] = v2
            else:
                clk[sname] = max(clk.get(sname, 0), v)

    def mark(self, eng):
        return Tok(eng, self.cnt[eng])

    def allmarks(self):
        return [Tok(k, v) for k, v in self.cnt.items()]



def emit_consts(nc, c, st, values=()):
    onescol = st.enter_context(nc.sbuf_tensor("onescol", [128, 1], BF16))
    onesrow = st.enter_context(nc.sbuf_tensor("onesrow", [1, 128], BF16))
    ident = st.enter_context(nc.sbuf_tensor("identc", [128, 128], BF16))
    cvals = list(values)
    ctile = st.enter_context(nc.sbuf_tensor("cvals", [128, max(1, len(cvals))], F32)) \
        if cvals else None
    cmap = {}
    toks = {}
    with nc.Block() as block:
        @block.gpsimd
        def _(g):
            ts = [c.inc("gpsimd", g.memset(onescol[:, :], 1.0)),
                  c.inc("gpsimd", g.memset(onesrow[:, :], 1.0))]
            ti = c.inc("gpsimd", g.memset(ident[:, :], 0.0))
            c.wait(g, ti)
            ts.append(c.inc("gpsimd", g.affine_select(
                out=ident[:, :], in_=ident[:, :], compare_op=ALU.not_equal,
                fill=1.0, base=0, pattern=[[-1, 128]], channel_multiplier=1)))
            for i, v in enumerate(cvals):
                ts.append(c.inc("gpsimd", g.memset(ctile[:, i:i + 1], float(v))))
                cmap[v] = ctile[:, i:i + 1]
            toks["consts"] = ts
    return onescol, onesrow, ident, cmap, toks["consts"]



def emit_ln(nc, c, st, pfx, x_dr, out_dr, onescol, onesrow, cmap, consts_tok, out32_dr=None):
    xf = st.enter_context(nc.sbuf_tensor(pfx + "xf", [128, KT * 512], F32))
    xb = st.enter_context(nc.sbuf_tensor(pfx + "xb", [128, KT * 512], BF16))
    sq = st.enter_context(nc.sbuf_tensor(pfx + "sq", [128, KT * 512], BF16))
    tmp = st.enter_context(nc.sbuf_tensor(pfx + "tmp", [128, KT * 512], F32))
    ob = st.enter_context(nc.sbuf_tensor(pfx + "ob", [128, KT * 512], BF16))
    o32 = st.enter_context(nc.sbuf_tensor(pfx + "o32", [128, KT * 512], F32)) \
        if out32_dr is not None else None
    rows = st.enter_context(nc.sbuf_tensor(pfx + "rows", [1, 512 * 6], F32))
    rowb = st.enter_context(nc.sbuf_tensor(pfx + "rowb", [1, 512 * 2], BF16))
    repr_ = st.enter_context(nc.sbuf_tensor(pfx + "repr", [128, 512], F32))
    repm = st.enter_context(nc.sbuf_tensor(pfx + "repm", [128, 512], F32))
    ps_sum = st.enter_context(nc.psum_tensor(pfx + "psum", [1, 512], F32))
    ps_sq = st.enter_context(nc.psum_tensor(pfx + "psq", [1, 512], F32))
    ps_rr = st.enter_context(nc.psum_tensor(pfx + "prr", [128, 512], F32))
    ps_rm = st.enter_context(nc.psum_tensor(pfx + "prm", [128, 512], F32))

    M_, V_, SR_, RS_, MS_ = (slice(i * 512, (i + 1) * 512) for i in range(5))

    x_t = x_dr.rearrange("(kt p) (s n) -> p kt s n", p=128, n=512)
    o_t = out_dr.rearrange("(kt p) (s n) -> p kt s n", p=128, n=512)
    o32_t = out32_dr.rearrange("(kt p) (s n) -> p kt s n", p=128, n=512) \
        if out32_dr is not None else None

    tk = {}
    prev = {"apply": None, "mm_rep": None, "mm_st": None}
    with nc.Block() as block:
        for s in range(NSTRIP):
            @block.gpsimd
            def _(g, s=s):
                c.wait(g, tk.get(("ap_done", s - 1)))
                tk[("ld", s)] = c.dinc("ln_x", g.dma_start(
                    out=xf.ap().rearrange("p (kt n) -> p kt n", n=512),
                    in_=x_t[:, :, s, :]))

            @block.vector
            def _(v, s=s):
                c.wait(v, tk[("ld", s)], tk.get(("mm_st", s - 1)))
                tk[("xb", s)] = c.inc("vector", v.tensor_copy(xb[:, :], xf[:, :]))

            @block.scalar
            def _(sc, s=s):
                c.wait(sc, tk[("ld", s)], tk.get(("mm_st", s - 1)))
                tk[("sq", s)] = c.inc("scalar", sc.activation(sq[:, :], xf[:, :], AF.Square))

            @block.tensor
            def _(t, s=s):
                c.wait(t, tk[("xb", s)], tk[("sq", s)], consts_tok,
                       tk.get(("rows_done", s - 1)))
                for k in range(KT):
                    last1 = t.matmul(ps_sum[:, :], onescol[:, :], xb[:, k * 512:(k + 1) * 512],
                                     start=(k == 0), stop=(k == KT - 1))
                for k in range(KT):
                    last2 = t.matmul(ps_sq[:, :], onescol[:, :], sq[:, k * 512:(k + 1) * 512],
                                     start=(k == 0), stop=(k == KT - 1))
                tk[("mm_st", s)] = c.inc("tensor", last2)

            @block.vector
            def _(v, s=s):
                c.wait(v, tk[("mm_st", s)])
                t1 = c.inc("vector", v.tensor_scalar(rows[:, M_], ps_sum[:, :], 1.0 / D, None, ALU.mult))
                t2 = c.inc("vector", v.tensor_scalar(rows[:, V_], ps_sq[:, :], 1.0 / D, None, ALU.mult))
                c.wait(v, t1, t2)
                t4 = c.inc("vector", v.tensor_mul(rows[:, SR_], rows[:, M_], rows[:, M_]))
                c.wait(v, t4)
                tk[("var", s)] = c.inc("vector", v.tensor_sub(rows[:, V_], rows[:, V_], rows[:, SR_]))

            @block.scalar
            def _(sc, s=s):
                c.wait(sc, tk[("var", s)])
                tk[("srt", s)] = c.inc("scalar", sc.activation(
                    rows[:, SR_], rows[:, V_], AF.Sqrt, bias=cmap[1e-5][0:1, 0:1]))

            @block.vector
            def _(v, s=s):
                c.wait(v, tk[("srt", s)])
                t1 = c.inc("vector", v.reciprocal(rows[:, RS_], rows[:, SR_]))
                c.wait(v, t1)
                t2 = c.inc("vector", v.tensor_mul(rows[:, MS_], rows[:, M_], rows[:, RS_]))
                t3 = c.inc("vector", v.tensor_copy(rowb[:, 0:512], rows[:, RS_]))
                c.wait(v, t2, t3)
                tk[("rows_done", s)] = c.inc("vector", v.tensor_copy(rowb[:, 512:1024], rows[:, MS_]))

            @block.tensor
            def _(t, s=s):
                c.wait(t, tk[("rows_done", s)], tk.get(("repv", s - 1)))
                t.matmul(ps_rr[:, :], onesrow[:, :], rowb[0:1, 0:512], start=True, stop=True)
                last = t.matmul(ps_rm[:, :], onesrow[:, :], rowb[0:1, 512:1024], start=True, stop=True)
                tk[("mm_rep", s)] = c.inc("tensor", last)

            @block.vector
            def _(v, s=s):
                c.wait(v, tk[("mm_rep", s)])
                t1 = c.inc("vector", v.tensor_copy(repr_[:, :], ps_rr[:, :]))
                t2 = c.inc("vector", v.tensor_copy(repm[:, :], ps_rm[:, :]))
                tk[("repv", s)] = [t1, t2]

            if out32_dr is None:
                @block.vector
                def _(v, s=s):
                    c.wait(v, tk[("repv", s)], tk.get(("ob_store", s - 1)))
                    for k in range(0, KT, 2):
                        t1 = c.inc("vector", v.tensor_mul(
                            tmp[:, k * 512:(k + 1) * 512], xf[:, k * 512:(k + 1) * 512], repr_[:, :]))
                        c.wait(v, t1)
                        c.inc("vector", v.tensor_sub(
                            ob[:, k * 512:(k + 1) * 512], tmp[:, k * 512:(k + 1) * 512], repm[:, :]))
                    tk[("ap_v", s)] = c.mark("vector")

                @block.gpsimd
                def _(g, s=s):
                    c.wait(g, tk[("repv", s)], tk.get(("ob_store", s - 1)))
                    for k in range(1, KT, 2):
                        t1 = c.inc("gpsimd", g.tensor_mul(
                            tmp[:, k * 512:(k + 1) * 512], xf[:, k * 512:(k + 1) * 512], repr_[:, :]))
                        c.wait(g, t1)
                        c.inc("gpsimd", g.tensor_sub(
                            ob[:, k * 512:(k + 1) * 512], tmp[:, k * 512:(k + 1) * 512], repm[:, :]))
                    tk[("ap_g", s)] = c.mark("gpsimd")
                    tk[("ap_done", s)] = [tk[("ap_v", s)], tk[("ap_g", s)]]
            else:
                @block.vector
                def _(v, s=s):
                    c.wait(v, tk[("repv", s)], tk.get(("ob_store", s - 1)), tk.get(("o32_store", s - 1)))
                    for k in range(0, KT, 2):
                        t1 = c.inc("vector", v.tensor_mul(
                            tmp[:, k * 512:(k + 1) * 512], xf[:, k * 512:(k + 1) * 512], repr_[:, :]))
                        c.wait(v, t1)
                        c.inc("vector", v.tensor_sub(
                            o32[:, k * 512:(k + 1) * 512], tmp[:, k * 512:(k + 1) * 512], repm[:, :]))
                    tk[("ap_v", s)] = c.mark("vector")

                @block.gpsimd
                def _(g, s=s):
                    c.wait(g, tk[("repv", s)], tk.get(("ob_store", s - 1)), tk.get(("o32_store", s - 1)))
                    for k in range(1, KT, 2):
                        t1 = c.inc("gpsimd", g.tensor_mul(
                            tmp[:, k * 512:(k + 1) * 512], xf[:, k * 512:(k + 1) * 512], repr_[:, :]))
                        c.wait(g, t1)
                        c.inc("gpsimd", g.tensor_sub(
                            o32[:, k * 512:(k + 1) * 512], tmp[:, k * 512:(k + 1) * 512], repm[:, :]))
                    tk[("ap_g", s)] = c.mark("gpsimd")

                @block.scalar
                def _(sc, s=s):
                    c.wait(sc, tk[("ap_v", s)], tk[("ap_g", s)])
                    t1 = c.inc("scalar", sc.copy(ob[:, 0:8 * 512], o32[:, 0:8 * 512]))
                    t2 = c.inc("scalar", sc.copy(ob[:, 8 * 512:KT * 512], o32[:, 8 * 512:KT * 512]))
                    tk[("ap_done", s)] = [t1, t2]

            @block.sync
            def _(sy, s=s):
                c.wait(sy, tk[("ap_done", s)])
                tk[("ob_store", s)] = c.dinc("ln_o", sy.dma_start(
                    out=o_t[:, :, s, :],
                    in_=ob.ap().rearrange("p (kt n) -> p kt n", n=512)))
                if out32_dr is not None:
                    tk[("o32_store", s)] = c.dinc("ln_o32", sy.dma_start(
                        out=o32_t[:, :, s, :],
                        in_=o32.ap().rearrange("p (kt n) -> p kt n", n=512)))

        @block.sync
        def _(sy):
            c.wait(sy, tk[("ob_store", NSTRIP - 1)])
    return tk[("ob_store", NSTRIP - 1)]


def emit_barrier(nc, c):
    marks = c.allmarks()
    with nc.Block() as block:
        @block.tensor
        def _(t):
            c.wait(t, marks)

        @block.vector
        def _(v):
            c.wait(v, marks)

        @block.scalar
        def _(s):
            c.wait(s, marks)

        @block.gpsimd
        def _(g):
            c.wait(g, marks)

        @block.sync
        def _(sy):
            c.wait(sy, marks)
    return marks



def emit_ffn0(nc, c, st, pfx, xln_dr, wg_dr, wu_dr, wd_dr, sgT_dr, suT_dr, sdT_dr,
              xres_dr, out_dr, fuse_dr):
    IT = 64
    with contextlib.ExitStack() as stA:
        xln = stA.enter_context(nc.sbuf_tensor(pfx + "xln", [128, KT * NTOK], BF16))
        wg = [stA.enter_context(nc.sbuf_tensor(pfx + f"wg{i}", [128, KT * 128], BF16)) for i in range(2)]
        wu = [stA.enter_context(nc.sbuf_tensor(pfx + f"wu{i}", [128, KT * 128], BF16)) for i in range(2)]
        gbuf = stA.enter_context(nc.sbuf_tensor(pfx + "gbuf", [128, NTOK], BF16))
        ubuf = stA.enter_context(nc.sbuf_tensor(pfx + "ubuf", [128, NTOK], BF16))
        fbuf = stA.enter_context(nc.sbuf_tensor(pfx + "fbuf", [128, NTOK], BF16))
        sg = stA.enter_context(nc.sbuf_tensor(pfx + "sg", [128, IT], F32))
        su = stA.enter_context(nc.sbuf_tensor(pfx + "su", [128, IT], F32))
        psg = [stA.enter_context(nc.psum_tensor(pfx + f"psg{i}", [128, 512], F32)) for i in range(2)]
        psu = [stA.enter_context(nc.psum_tensor(pfx + f"psu{i}", [128, 512], F32)) for i in range(2)]

        xln_t = xln_dr.rearrange("(kt p) n -> p kt n", p=128)
        wg_t = wg_dr.rearrange("(kt p) (it m) -> p kt it m", p=128, m=128)
        wu_t = wu_dr.rearrange("(kt p) (it m) -> p kt it m", p=128, m=128)
        fuse_t = fuse_dr.rearrange("(it p) n -> p it n", p=128)

        tk = {}
        with nc.Block() as block:
            @block.gpsimd
            def _(g):
                tk["xln"] = [c.dinc("fa_in", g.dma_start(
                    out=xln[:, q * 4 * NTOK:(q + 1) * 4 * NTOK].rearrange("p (kt n) -> p kt n", n=NTOK),
                    in_=xln_t[:, q * 4:(q + 1) * 4, :])) for q in range(4)]
                tk["sg"] = c.dinc("fa_in", g.dma_start(out=sg[:, :], in_=sgT_dr[:, :]))
                tk["su"] = c.dinc("fa_in", g.dma_start(out=su[:, :], in_=suT_dr[:, :]))
                tk["fa_in"] = Tok("D_fa_in", c.cnt["D_fa_in"])

            for it in range(IT):
                buf = it % 2

                @block.gpsimd
                def _(g, it=it, buf=buf):
                    c.wait(g, tk.get(("mmlast", it - 2)))
                    t1 = c.dinc(f"fa_wg{buf}", g.dma_start(out=wg[buf].ap().rearrange(
                        "p (kt m) -> p kt m", m=128), in_=wg_t[:, :, it, :]))
                    t2 = c.dinc(f"fa_wu{buf}", g.dma_start(out=wu[buf].ap().rearrange(
                        "p (kt m) -> p kt m", m=128), in_=wu_t[:, :, it, :]))
                    tk[("w", it)] = [t1, t2]

                for s in range(NSTRIP):
                    @block.tensor
                    def _(t, it=it, buf=buf, s=s):
                        if s == 0:
                            c.wait(t, tk[("w", it)], tk["fa_in"])
                        c.wait(t, tk.get(("evA", it, s - 2)) or
                               tk.get(("evA", it - 1, NSTRIP - 2 + s)))
                        bk = s % 2
                        for k in range(KT):
                            t.matmul(psg[bk][:, :], wg[buf][:, k * 128:(k + 1) * 128],
                                     xln[:, k * NTOK + s * 512: k * NTOK + (s + 1) * 512],
                                     start=(k == 0), stop=(k == KT - 1))
                        for k in range(KT):
                            last = t.matmul(psu[bk][:, :], wu[buf][:, k * 128:(k + 1) * 128],
                                            xln[:, k * NTOK + s * 512: k * NTOK + (s + 1) * 512],
                                            start=(k == 0), stop=(k == KT - 1))
                        tk[("mm", it, s)] = c.inc("tensor", last)
                        if s == NSTRIP - 1:
                            tk[("mmlast", it)] = tk[("mm", it, s)]

                    @block.scalar
                    def _(sc, it=it, buf=buf, s=s):
                        if s == 0:
                            c.wait(sc, tk["fa_in"], tk.get(("fuse", it - 1)))
                        c.wait(sc, tk[("mm", it, s)])
                        bk = s % 2
                        e1 = c.inc("scalar", sc.activation(
                            gbuf[:, s * 512:(s + 1) * 512], psg[bk][:, :], AF.Silu,
                            scale=sg[:, it:it + 1]))
                        e2 = c.inc("scalar", sc.activation(
                            ubuf[:, s * 512:(s + 1) * 512], psu[bk][:, :], AF.Copy,
                            scale=su[:, it:it + 1]))
                        tk[("evA", it, s)] = [e1, e2]

                @block.vector
                def _(v, it=it):
                    c.wait(v, tk[("evA", it, NSTRIP - 1)], tk.get(("fstore", it - 1)))
                    tk[("fuse", it)] = c.inc("vector", v.tensor_mul(fbuf[:, :], gbuf[:, :], ubuf[:, :]))

                @block.sync
                def _(sy, it=it):
                    c.wait(sy, tk[("fuse", it)])
                    tk[("fstore", it)] = c.dinc("fa_f", sy.dma_start(
                        out=fuse_t[:, it, :], in_=fbuf[:, :]))

            @block.sync
            def _(sy):
                c.wait(sy, tk[("fstore", IT - 1)])

    emit_barrier(nc, c)

    with contextlib.ExitStack() as stB:
        DKT = 64
        fsb = stB.enter_context(nc.sbuf_tensor(pfx + "fsb", [128, DKT * 512], BF16))
        xres = stB.enter_context(nc.sbuf_tensor(pfx + "xres", [128, KT * 512], F32))
        osb = stB.enter_context(nc.sbuf_tensor(pfx + "osb", [128, KT * 512], F32))
        wd = [stB.enter_context(nc.sbuf_tensor(pfx + f"wd{i}", [128, 4 * 8 * 128], BF16)) for i in range(2)]
        sd = stB.enter_context(nc.sbuf_tensor(pfx + "sd", [128, KT], F32))
        psd = [stB.enter_context(nc.psum_tensor(pfx + f"psd{i}", [128, 512], F32)) for i in range(8)]

        fuse_s = fuse_dr.rearrange("(kt p) (s n) -> p kt s n", p=128, n=512)
        x_t = xres_dr.rearrange("(kt p) (s n) -> p kt s n", p=128, n=512)
        o_t = out_dr.rearrange("(kt p) (s n) -> p kt s n", p=128, n=512)
        wd_t = wd_dr.rearrange("(kt p) (mg m) -> p kt mg m", p=128, m=8 * 128)

        tk = {}
        mm_by_j = {}
        with nc.Block() as block:
            @block.gpsimd
            def _(g):
                tk["sd"] = c.dinc("fb_in", g.dma_start(out=sd[:, :], in_=sdT_dr[:, :]))

            for s in range(NSTRIP):
                @block.sync
                def _(sy, s=s):
                    c.wait(sy, tk.get(("ost", s - 1)))
                    c.wait(sy, tk.get(("mmall", s - 1)))
                    tk[("fld", s)] = c.dinc("fb_f", sy.dma_start(
                        out=fsb.ap().rearrange("p (kt n) -> p kt n", n=512),
                        in_=fuse_s[:, :, s, :]))
                    tk[("xld", s)] = c.dinc("fb_x", sy.dma_start(
                        out=xres.ap().rearrange("p (kt n) -> p kt n", n=512),
                        in_=x_t[:, :, s, :]))

                for mg in range(2):
                    for kb in range(0, DKT, 4):
                        j = ((s * 2 + mg) * DKT + kb) // 4

                        @block.gpsimd
                        def _(g, s=s, mg=mg, kb=kb, j=j):
                            c.wait(g, mm_by_j.get(j - 2))
                            tk[("wd", s, mg, kb)] = c.dinc(f"fb_wd{(kb // 4) % 2}", g.dma_start(
                                out=wd[(kb // 4) % 2].ap().rearrange("p (k4 m) -> p k4 m", m=1024),
                                in_=wd_t[:, kb:kb + 4, mg, :]))

                        @block.tensor
                        def _(t, s=s, mg=mg, kb=kb, j=j):
                            c.wait(t, tk[("wd", s, mg, kb)], tk[("fld", s)])
                            if kb == 0:
                                c.wait(t, tk.get(("ev", s, mg - 1)), tk.get(("ev", s - 1, 1)))
                            last = None
                            for k in range(kb, kb + 4):
                                for m in range(8):
                                    last = t.matmul(
                                        psd[m][:, :],
                                        wd[(kb // 4) % 2][:, (k - kb) * 1024 + m * 128:
                                                          (k - kb) * 1024 + (m + 1) * 128],
                                        fsb[:, k * 512:(k + 1) * 512],
                                        start=(k == 0), stop=(k == DKT - 1))
                            mm_by_j[j] = tk[("mmk", s, mg, kb)] = c.inc("tensor", last)

                    @block.vector
                    def _(v, s=s, mg=mg):
                        c.wait(v, tk[("mmk", s, mg, DKT - 4)], tk[("xld", s)], tk["sd"])
                        ev = []
                        for m in range(8):
                            mt = mg * 8 + m
                            ev.append(c.inc("vector", v.scalar_tensor_tensor(
                                osb[:, mt * 512:(mt + 1) * 512], psd[m][:, :],
                                sd[:, mt:mt + 1], xres[:, mt * 512:(mt + 1) * 512],
                                ALU.mult, ALU.add)))
                        tk[("ev", s, mg)] = ev
                tk[("mmall", s)] = tk[("mmk", s, 1, DKT - 4)]

                @block.sync
                def _(sy, s=s):
                    c.wait(sy, tk[("ev", s, 0)], tk[("ev", s, 1)])
                    tk[("ost", s)] = c.dinc("fb_o", sy.dma_start(
                        out=o_t[:, :, s, :],
                        in_=osb.ap().rearrange("p (kt n) -> p kt n", n=512)))

            @block.sync
            def _(sy):
                c.wait(sy, tk[("ost", NSTRIP - 1)])
    return



def emit_moe(nc, c, st, pfx, xln_dr, xln32_dr, gw_dr, fc1_dr, fc1bT_dr, fc2_dr, fc2b_dr,
             alphaR_dr, xres_dr, out_dr, onesrow, ident, ctok):
    NE = 8
    NTT = NTOK // 128
    combTok = st.enter_context(nc.sbuf_tensor(pfx + "combTok", [128, NTOK // 128 * 8], BF16))
    ps_t = st.enter_context(nc.psum_tensor(pfx + "pstr", [1, 128], BF16))
    with contextlib.ExitStack() as stA:
        xls = stA.enter_context(nc.sbuf_tensor(pfx + "xls", [128, KT * 512], F32))
        gw = stA.enter_context(nc.sbuf_tensor(pfx + "gw", [128, KT * NE], F32))
        eb = stA.enter_context(nc.sbuf_tensor(pfx + "eb", [128, NE], F32))
        em = stA.enter_context(nc.sbuf_tensor(pfx + "em", [128, NE], F32))
        c1 = stA.enter_context(nc.sbuf_tensor(pfx + "c1", [128, NE], F32))
        c2 = stA.enter_context(nc.sbuf_tensor(pfx + "c2", [128, NE], F32))
        rowsA = stA.enter_context(nc.sbuf_tensor(pfx + "rowsA", [128, 8], F32))
        alr = stA.enter_context(nc.sbuf_tensor(pfx + "alr", [128, NE], F32))
        combB = stA.enter_context(nc.sbuf_tensor(pfx + "combB", [128, NE], BF16))
        ps_g = stA.enter_context(nc.psum_tensor(pfx + "psgt", [128, NE], F32))

        NM, M2, DEN, S2, M1, TP, S1 = 0, 1, 2, 3, 4, 5, 6

        gw_t = gw_dr.rearrange("(kt p) e -> p kt e", p=128)
        xln_t = xln_dr.rearrange("(kt p) (tt n) -> p kt tt n", p=128, n=128)

        tk = {}
        with nc.Block() as block:
            @block.gpsimd
            def _(g):
                t1 = c.dinc(pfx + "g_in", g.dma_start(out=gw[:, :], in_=gw_t[:, :, :]))
                t2 = c.dinc(pfx + "g_in", g.dma_start(out=alr[:, :], in_=alphaR_dr[:, :]))
                tk["gin"] = Tok("D_" + pfx + "g_in", c.cnt["D_" + pfx + "g_in"])

            for tt in range(NTT):
                sidx, toff = tt // 4, (tt % 4) * 128

                if tt % 4 == 0:
                    @block.gpsimd
                    def _(g, sidx=sidx):
                        c.wait(g, tk.get(("gmm", sidx * 4 - 1)))
                        tk[("xld", sidx)] = c.dinc(pfx + "g_x", g.dma_start(
                            out=xls.ap().rearrange("p (kt n) -> p kt n", n=512),
                            in_=xln32_dr.rearrange("(kt p) n -> p kt n", p=128)[:, :, sidx * 512:(sidx + 1) * 512]))

                @block.tensor
                def _(t, tt=tt, sidx=sidx, toff=toff):
                    c.wait(t, tk[("xld", sidx)], tk["gin"])
                    c.wait(t, tk.get(("comb", tt - 1)))
                    for k in range(KT):
                        last = t.matmul(ps_g[:, :], xls[:, k * 512 + toff: k * 512 + toff + 128],
                                        gw[:, k * NE:(k + 1) * NE], start=(k == 0), stop=(k == KT - 1))
                    tk[("gmm", tt)] = c.inc("tensor", last)

                @block.vector
                def _(v, tt=tt):
                    c.wait(v, tk[("gmm", tt)])
                    t1 = c.inc("vector", v.reduce_max(rowsA[:, M1:M1 + 1], ps_g[:, :], AX.X))
                    c.wait(v, t1)
                    t2 = c.inc("vector", v.tensor_single_scalar(
                        c1[:, :], ps_g[:, :], rowsA[:, M1:M1 + 1], ALU.is_ge))
                    t3 = c.inc("vector", v.tensor_scalar(
                        rowsA[:, NM:NM + 1], rowsA[:, M1:M1 + 1], -1.0, None, ALU.mult))
                    tk[("nm", tt)] = [t2, t3]

                @block.scalar
                def _(sc, tt=tt):
                    c.wait(sc, tk[("nm", tt)])
                    tk[("exp", tt)] = c.inc("scalar", sc.activation(
                        eb[:, :], ps_g[:, :], AF.Exp, bias=rowsA[:, NM:NM + 1]))

                @block.vector
                def _(v, tt=tt):
                    c.wait(v, tk[("exp", tt)])
                    t1 = c.inc("vector", v.tensor_mul(em[:, :], eb[:, :], c1[:, :]))
                    c.wait(v, t1)
                    t2 = c.inc("vector", v.tensor_sub(em[:, :], eb[:, :], em[:, :]))
                    t2b = c.inc("vector", v.reduce_max(rowsA[:, TP:TP + 1], eb[:, :], AX.X))
                    c.wait(v, t2, t2b)
                    t3 = c.inc("vector", v.reduce_max(rowsA[:, M2:M2 + 1], em[:, :], AX.X))
                    c.wait(v, t3)
                    t4 = c.inc("vector", v.tensor_add(
                        rowsA[:, DEN:DEN + 1], rowsA[:, TP:TP + 1], rowsA[:, M2:M2 + 1]))
                    c.wait(v, t4)
                    t5 = c.inc("vector", v.reciprocal(rowsA[:, DEN:DEN + 1], rowsA[:, DEN:DEN + 1]))
                    c.wait(v, t5)
                    t6 = c.inc("vector", v.tensor_mul(rowsA[:, S2:S2 + 1], rowsA[:, M2:M2 + 1],
                                                      rowsA[:, DEN:DEN + 1]))
                    t6b = c.inc("vector", v.tensor_mul(rowsA[:, S1:S1 + 1], rowsA[:, TP:TP + 1],
                                                       rowsA[:, DEN:DEN + 1]))
                    t7 = c.inc("vector", v.tensor_single_scalar(c2[:, :], em[:, :],
                                                                rowsA[:, M2:M2 + 1], ALU.is_ge))
                    c.wait(v, t6, t6b, t7)
                    t8 = c.inc("vector", v.tensor_scalar(c1[:, :], c1[:, :],
                                                         rowsA[:, S1:S1 + 1], None, ALU.mult))
                    t9 = c.inc("vector", v.tensor_scalar(c2[:, :], c2[:, :],
                                                         rowsA[:, S2:S2 + 1], None, ALU.mult))
                    c.wait(v, t8, t9)
                    ta = c.inc("vector", v.tensor_add(c1[:, :], c1[:, :], c2[:, :]))
                    c.wait(v, ta)
                    tb = c.inc("vector", v.tensor_mul(c1[:, :], c1[:, :], alr[:, :]))
                    c.wait(v, tb)
                    tk[("comb", tt)] = c.inc("vector", v.tensor_copy(
                        combTok[:, tt * 8:(tt + 1) * 8], c1[:, :]))

            @block.sync
            def _(sy):
                c.wait(sy, tk[("comb", NTT - 1)])
            tk["combdone_"] = tk[("comb", NTT - 1)]

    bar_tk = tk["combdone_"]
    emit_barrier(nc, c)

    with nc.Block() as block:
        @block.sync
        def _(sy):
            t = c.dinc(pfx + "init", sy.dma_start(out=out_dr[:, :], in_=xres_dr[:, :]))
            c.wait(sy, t)
    init_tok = Tok("D_" + pfx + "init", 16)

    with contextlib.ExitStack() as stB:
        f1 = stB.enter_context(nc.sbuf_tensor(pfx + "f1", [128, KT * D], BF16))
        f2 = stB.enter_context(nc.sbuf_tensor(pfx + "f2", [128, KT * D], BF16))
        b1 = stB.enter_context(nc.sbuf_tensor(pfx + "b1", [128, KT], F32))
        b2r = stB.enter_context(nc.sbuf_tensor(pfx + "b2r", [1, D], BF16))
        xls = stB.enter_context(nc.sbuf_tensor(pfx + "xlsB", [128, KT * 512], BF16))
        hb = stB.enter_context(nc.sbuf_tensor(pfx + "hb", [128, KT * 512], BF16))
        rep = stB.enter_context(nc.sbuf_tensor(pfx + "rep", [128, 512], BF16))
        crow = stB.enter_context(nc.sbuf_tensor(pfx + "crow", [1, 512], BF16))
        osb = stB.enter_context(nc.sbuf_tensor(pfx + "osb", [128, KT * 512], F32))
        ps1 = [stB.enter_context(nc.psum_tensor(pfx + f"ps1{i}", [128, 512], F32)) for i in range(2)]
        ps2 = [stB.enter_context(nc.psum_tensor(pfx + f"ps2{i}", [128, 512], F32)) for i in range(2)]
        ps_r = stB.enter_context(nc.psum_tensor(pfx + "psr", [128, 512], F32))

        f1_t = fc1_dr.rearrange("e (kt p) m -> e p kt m", p=128)
        f2_t = fc2_dr.rearrange("e (kt p) m -> e p kt m", p=128)
        xln_t = xln_dr.rearrange("(kt p) n -> p kt n", p=128)
        out_t = out_dr.rearrange("(kt p) (s n) -> p kt s n", p=128, n=512)

        tk = {"combdone": bar_tk}
        with nc.Block() as block:
            for e in range(NE):
                @block.gpsimd
                def _(g, e=e):
                    c.wait(g, tk.get(("mm2last", e - 1)))
                    w1 = [c.dinc(pfx + "wf1", g.dma_start(
                        out=f1[:, q * 4 * D:(q + 1) * 4 * D].rearrange("p (kt m) -> p kt m", m=D),
                        in_=f1_t[e, :, q * 4:(q + 1) * 4, :])) for q in range(4)]
                    w2 = [c.dinc(pfx + "wf2", g.dma_start(
                        out=f2[:, q * 4 * D:(q + 1) * 4 * D].rearrange("p (kt m) -> p kt m", m=D),
                        in_=f2_t[e, :, q * 4:(q + 1) * 4, :])) for q in range(4)]
                    wb = [c.dinc(pfx + "wb1", g.dma_start(out=b1[:, :], in_=fc1bT_dr[e, :, :])),
                          c.dinc(pfx + "wb2", g.dma_start(out=b2r[:, :], in_=fc2b_dr[e, :][None, :]))]
                    tk[("w", e)] = [Tok("D_" + pfx + "wf1", c.cnt["D_" + pfx + "wf1"]),
                                    Tok("D_" + pfx + "wf2", c.cnt["D_" + pfx + "wf2"]),
                                    Tok("D_" + pfx + "wb1", c.cnt["D_" + pfx + "wb1"]),
                                    Tok("D_" + pfx + "wb2", c.cnt["D_" + pfx + "wb2"])]

                for s in range(NSTRIP):
                    @block.gpsimd
                    def _(g, e=e, s=s):
                        c.wait(g, tk.get(("mm1last", e, s - 1)) or tk.get(("mm1last", e - 1, NSTRIP - 1)))
                        tk[("xld", e, s)] = c.dinc(pfx + "m_x", g.dma_start(
                            out=xls.ap().rearrange("p (kt n) -> p kt n", n=512),
                            in_=xln_t[:, :, s * 512:(s + 1) * 512]))

                    for q in range(4):
                        @block.tensor
                        def _(t, e=e, s=s, q=q):
                            c.wait(t, tk["combdone"], ctok)
                            c.wait(t, tk.get(("crowev", e, s, q - 1))
                                   or tk.get(("reptr", e, s - 1))
                                   or tk.get(("reptr", e - 1, NSTRIP - 1)))
                            tk[("crowtr", e, s, q)] = c.inc("tensor", t.transpose(
                                ps_t[:, :], combTok[:, (s * 4 + q) * 8 + e:(s * 4 + q) * 8 + e + 1],
                                ident[:, :]))

                        @block.vector
                        def _(v, e=e, s=s, q=q):
                            c.wait(v, tk[("crowtr", e, s, q)])
                            if q == 0:
                                c.wait(v, tk.get(("reptr", e, s - 1))
                                       or tk.get(("reptr", e - 1, NSTRIP - 1)))
                            tk[("crowev", e, s, q)] = c.inc("vector", v.tensor_copy(
                                crow[0:1, q * 128:(q + 1) * 128], ps_t[0:1, :]))

                    @block.tensor
                    def _(t, e=e, s=s):
                        c.wait(t, [tk[("crowev", e, s, q)] for q in range(4)])
                        c.wait(t, tk.get(("repev", e, s - 1)) or tk.get(("repev", e - 1, NSTRIP - 1)))
                        tk[("reptr", e, s)] = c.inc("tensor", t.matmul(
                            ps_r[:, :], onesrow[:, :], crow[0:1, :], start=True, stop=True))

                    @block.vector
                    def _(v, e=e, s=s):
                        c.wait(v, tk[("reptr", e, s)])
                        tk[("repev", e, s)] = c.inc("vector", v.tensor_copy(rep[:, :], ps_r[:, :]))

                    for it in range(KT):
                        @block.tensor
                        def _(t, e=e, s=s, it=it):
                            c.wait(t, tk[("xld", e, s)], tk[("w", e)])
                            c.wait(t, tk.get(("h", e, s, it - 2)))
                            for k in range(KT):
                                last = t.matmul(ps1[it % 2][:, :], f1[:, k * D + it * 128: k * D + (it + 1) * 128],
                                                xls[:, k * 512:(k + 1) * 512],
                                                start=(k == 0), stop=(k == KT - 1))
                            tk[("mm1", e, s, it)] = c.inc("tensor", last)

                        @block.scalar
                        def _(sc, e=e, s=s, it=it):
                            c.wait(sc, tk[("mm1", e, s, it)])
                            c.wait(sc, tk.get(("mm2last", e, s - 1)) or tk.get(("mm2last", e - 1)))
                            tk[("gel", e, s, it)] = c.inc("scalar", sc.activation(
                                hb[:, it * 512:(it + 1) * 512], ps1[it % 2][:, :], AF.Gelu,
                                bias=b1[:, it:it + 1]))

                        @block.vector
                        def _(v, e=e, s=s, it=it):
                            c.wait(v, tk[("gel", e, s, it)], tk[("repev", e, s)])
                            tk[("h", e, s, it)] = c.inc("vector", v.tensor_mul(
                                hb[:, it * 512:(it + 1) * 512], hb[:, it * 512:(it + 1) * 512], rep[:, :]))

                    for m in range(KT):
                        @block.tensor
                        def _(t, e=e, s=s, m=m):
                            c.wait(t, [tk[("h", e, s, it)] for it in range(KT)])
                            c.wait(t, tk.get(("oev", e, s, m - 2)))
                            t.matmul(ps2[m % 2][:, :], b2r[0:1, m * 128:(m + 1) * 128],
                                     crow[0:1, :], start=True, stop=False)
                            for k in range(KT):
                                last = t.matmul(ps2[m % 2][:, :], f2[:, k * D + m * 128: k * D + (m + 1) * 128],
                                                hb[:, k * 512:(k + 1) * 512],
                                                start=False, stop=(k == KT - 1))
                            tk[("mm2", e, s, m)] = c.inc("tensor", last)

                        @block.vector
                        def _(v, e=e, s=s, m=m):
                            c.wait(v, tk[("mm2", e, s, m)])
                            if m == 0:
                                c.wait(v, tk.get(("acc", e, s - 1)) or tk.get(("acc", e - 1, NSTRIP - 1)))
                            tk[("oev", e, s, m)] = c.inc("vector", v.tensor_copy(
                                osb[:, m * 512:(m + 1) * 512], ps2[m % 2][:, :]))

                    tk[("mm1last", e, s)] = tk[("mm1", e, s, KT - 1)]
                    tk[("mm2last", e, s)] = tk[("mm2", e, s, KT - 1)]

                    @block.gpsimd
                    def _(g, e=e, s=s):
                        c.wait(g, [tk[("oev", e, s, m)] for m in range(KT)])
                        if e == 0:
                            c.wait(g, init_tok)
                        else:
                            c.wait(g, tk.get(("acc", e - 1, s)))
                        tk[("acc", e, s)] = c.dinc(pfx + f"acc{s}", g.dma_start(
                            out=out_t[:, :, s, :],
                            in_=osb.ap().rearrange("p (kt n) -> p kt n", n=512),
                            accum_op=ALU.add))

                tk[("mm2last", e)] = tk[("mm2last", e, NSTRIP - 1)]

            @block.sync
            def _(sy):
                c.wait(sy, [tk[("acc", NE - 1, s)] for s in range(NSTRIP)])
    return



def emit_gemm(nc, c, st, pfx, wT_dr, IN, OUT, x_dr, out_dr, residual_dr=None,
              scale_fn=None, in_dtype=BF16):
    KTI = IN // 128
    MT = OUT // 128
    with contextlib.ExitStack() as stA:
        xsb = stA.enter_context(nc.sbuf_tensor(pfx + "x", [128, KTI * NTOK], BF16))
        wt = [stA.enter_context(nc.sbuf_tensor(pfx + f"w{i}", [128, KTI * 128], BF16))
              for i in range(2)]
        ob = stA.enter_context(nc.sbuf_tensor(pfx + "ob", [128, NTOK],
                                              F32 if residual_dr is not None else BF16))
        res = [stA.enter_context(nc.sbuf_tensor(pfx + f"r{i}", [128, NTOK], F32))
               for i in range(2)] if residual_dr is not None else None
        ps = [stA.enter_context(nc.psum_tensor(pfx + f"ps{i}", [128, 512], F32))
              for i in range(4)]

        x_t = x_dr.rearrange("(kt p) n -> p kt n", p=128)
        w_t = wT_dr.rearrange("(kt p) (mt m) -> p kt mt m", p=128, m=128)
        o_t = out_dr.rearrange("(mt p) n -> p mt n", p=128)
        r_t = residual_dr.rearrange("(mt p) n -> p mt n", p=128) if residual_dr is not None else None

        tk = {}
        with nc.Block() as block:
            @block.gpsimd
            def _(g):
                nq = max(1, KTI // 4)
                tk["x"] = [c.dinc(pfx + "x", g.dma_start(
                    out=xsb[:, q * (KTI // nq) * NTOK:(q + 1) * (KTI // nq) * NTOK].rearrange(
                        "p (kt n) -> p kt n", n=NTOK),
                    in_=x_t[:, q * (KTI // nq):(q + 1) * (KTI // nq), :])) for q in range(nq)]
                tk["x"] = Tok("D_" + pfx + "x", c.cnt["D_" + pfx + "x"])

            for mt in range(MT):
                buf = mt % 2

                @block.gpsimd
                def _(g, mt=mt, buf=buf):
                    c.wait(g, tk.get(("mml", mt - 2)))
                    tk[("w", mt)] = c.dinc(pfx + f"w{buf}", g.dma_start(
                        out=wt[buf].ap().rearrange("p (kt m) -> p kt m", m=128),
                        in_=w_t[:, :, mt, :]))
                    if res is not None:
                        c.wait(g, tk.get(("ev", mt - 2, NSTRIP - 1)))
                        tk[("r", mt)] = c.dinc(pfx + f"r{buf}", g.dma_start(
                            out=res[buf].ap(), in_=r_t[:, mt, :]))

                for s in range(NSTRIP):
                    @block.tensor
                    def _(t, mt=mt, buf=buf, s=s):
                        if s == 0:
                            c.wait(t, tk[("w", mt)], tk["x"])
                        bank = (mt * NSTRIP + s) % 4
                        c.wait(t, tk.get(("ev", mt, s - 4)) or
                               (tk.get(("ev", mt - 1, NSTRIP + s - 4)) if s < 4 else None))
                        for k in range(KTI):
                            last = t.matmul(ps[bank][:, :], wt[buf][:, k * 128:(k + 1) * 128],
                                            xsb[:, k * NTOK + s * 512: k * NTOK + (s + 1) * 512],
                                            start=(k == 0), stop=(k == KTI - 1))
                        tk[("mm", mt, s)] = c.inc("tensor", last)
                        if s == NSTRIP - 1:
                            tk[("mml", mt)] = tk[("mm", mt, s)]

                    if res is not None:
                        @block.vector
                        def _(v, mt=mt, buf=buf, s=s):
                            bank = (mt * NSTRIP + s) % 4
                            c.wait(v, tk[("mm", mt, s)], tk.get(("r", mt)))
                            if s == 0:
                                c.wait(v, tk.get(("ost", mt - 1)))
                            sc = 1.0 if scale_fn is None else scale_fn(mt)
                            tk[("ev", mt, s)] = c.inc("vector", v.scalar_tensor_tensor(
                                ob[:, s * 512:(s + 1) * 512], ps[bank][:, :], sc,
                                res[buf][:, s * 512:(s + 1) * 512], ALU.mult, ALU.add))
                    else:
                        @block.scalar
                        def _(sc_, mt=mt, s=s):
                            bank = (mt * NSTRIP + s) % 4
                            c.wait(sc_, tk[("mm", mt, s)])
                            if s == 0:
                                c.wait(sc_, tk.get(("ost", mt - 1)))
                            sc = 1.0 if scale_fn is None else scale_fn(mt)
                            tk[("ev", mt, s)] = c.inc("scalar", sc_.mul(
                                ob[:, s * 512:(s + 1) * 512], ps[bank][:, :], sc))

                @block.sync
                def _(sy, mt=mt):
                    c.wait(sy, [tk[("ev", mt, s)] for s in range(NSTRIP)])
                    tk[("ost", mt)] = c.dinc(pfx + "o", sy.dma_start(
                        out=o_t[:, mt, :], in_=ob.ap()))

            @block.sync
            def _(sy):
                c.wait(sy, tk[("ost", MT - 1)])
    return



def emit_attn(nc, c, st, pfx, qkv_dr, out_dr, ident, ctok):
    NH = 16
    QT = T // 128
    with contextlib.ExitStack() as stA:
        qt_ = stA.enter_context(nc.sbuf_tensor(pfx + "q", [128, T], BF16))
        kt_ = stA.enter_context(nc.sbuf_tensor(pfx + "k", [128, T], BF16))
        vr = stA.enter_context(nc.sbuf_tensor(pfx + "v", [128, T], BF16))
        pb = stA.enter_context(nc.sbuf_tensor(pfx + "pb", [128, T], BF16))
        pbT = stA.enter_context(nc.sbuf_tensor(pfx + "pbT", [128, T], BF16))
        at = stA.enter_context(nc.sbuf_tensor(pfx + "at", [128, T], BF16))
        rows = stA.enter_context(nc.sbuf_tensor(pfx + "rows", [128, 2], F32))
        mctT = stA.enter_context(nc.sbuf_tensor(pfx + "mctT", [128, 128], BF16))
        ps_s = stA.enter_context(nc.psum_tensor(pfx + "pss", [128, T], F32))
        ps_tr = [stA.enter_context(nc.psum_tensor(pfx + f"pstr{i}", [128, 128], BF16))
                 for i in range(2)]
        ps_a = stA.enter_context(nc.psum_tensor(pfx + "psa", [128, 128], F32))

        tk = {}
        with nc.Block() as block:
            @block.gpsimd
            def _(g):
                t0 = c.inc("gpsimd", g.memset(mctT[:, :], 0.0))
                c.wait(g, t0)
                tk["mask"] = c.inc("gpsimd", g.affine_select(
                    out=mctT[:, :], in_=mctT[:, :], compare_op=ALU.is_ge,
                    fill=-1e30, base=0, pattern=[[1, 128]], channel_multiplier=-1))

            for b in range(BPC):
                for h in range(NH):
                    bh = b * NH + h

                    @block.gpsimd
                    def _(g, b=b, h=h, bh=bh):
                        c.wait(g, tk.get(("aev", bh - 1, QT - 1)))
                        tk[("lq", bh)] = c.dinc(pfx + "lq", g.dma_start(
                            out=qt_[:, :], in_=qkv_dr[h * 128:(h + 1) * 128, b * T:(b + 1) * T]))
                        tk[("lk", bh)] = c.dinc(pfx + "lk", g.dma_start(
                            out=kt_[:, :], in_=qkv_dr[2048 + h * 128:2048 + (h + 1) * 128,
                                                      b * T:(b + 1) * T]))

                    @block.scalar
                    def _(sc, b=b, h=h, bh=bh):
                        c.wait(sc, tk.get(("amml", bh - 1)))
                        for j in range(QT):
                            c.dinc(pfx + "lv", sc.dma_start(
                                out=vr[:, j * 128:(j + 1) * 128],
                                in_=qkv_dr[4096 + h * 128:4096 + (h + 1) * 128,
                                           b * T + j * 128: b * T + (j + 1) * 128],
                                transpose=True))
                        tk[("lv", bh)] = Tok("D_" + pfx + "lv", c.cnt["D_" + pfx + "lv"])

                    for q in range(QT):
                        kend = (q + 1) * 128

                        @block.tensor
                        def _(t, bh=bh, q=q):
                            c.wait(t, tk[("lq", bh)], tk[("lk", bh)], tk["mask"], ctok)
                            c.wait(t, tk.get(("exp", bh, q - 1)) or tk.get(("exp", bh - 1, QT - 1)))
                            for ks in range(0, kend, 512):
                                n = min(512, kend - ks)
                                t.matmul(ps_s[:, ks:ks + n], qt_[:, q * 128:(q + 1) * 128],
                                         kt_[:, ks:ks + n], start=True, stop=(ks + n < kend))
                            last = t.matmul(ps_s[:, q * 128:kend], mctT[:, :], ident[:, :],
                                            start=False, stop=True)
                            tk[("smm", bh, q)] = c.inc("tensor", last)

                        @block.vector
                        def _(v, bh=bh, q=q):
                            c.wait(v, tk[("smm", bh, q)])
                            tk[("nm", bh, q)] = c.inc("vector", v.reduce_max(
                                rows[:, 0:1], ps_s[:, 0:kend], AX.X, negate=True))

                        @block.scalar
                        def _(sc, bh=bh, q=q):
                            c.wait(sc, tk[("nm", bh, q)])
                            c.wait(sc, tk.get(("trl", bh, q - 1)) or tk.get(("trl", bh - 1, QT - 1)))
                            tk[("exp", bh, q)] = c.inc("scalar", sc.activation(
                                pb[:, 0:kend], ps_s[:, 0:kend], AF.Exp, bias=rows[:, 0:1],
                                accum_out=rows[:, 1:2]))

                        @block.vector
                        def _(v, bh=bh, q=q):
                            c.wait(v, tk[("exp", bh, q)])
                            t1 = c.inc("vector", v.reciprocal(rows[:, 1:2], rows[:, 1:2]))
                            c.wait(v, t1)
                            tk[("pscale", bh, q)] = c.inc("vector", v.tensor_scalar(
                                pb[:, 0:kend], pb[:, 0:kend], rows[:, 1:2], None, ALU.mult))

                        for j in range(q + 1):
                            @block.tensor
                            def _(t, bh=bh, q=q, j=j):
                                c.wait(t, tk[("pscale", bh, q)])
                                c.wait(t, tk.get(("trev", bh, q, j - 2)) or
                                       tk.get(("trev", bh, q - 1, q - 1 - (1 - j))) or
                                       tk.get(("trev", bh - 1, QT - 1, QT - 2 + j)))
                                tk[("tr", bh, q, j)] = c.inc("tensor", t.transpose(
                                    ps_tr[j % 2][:, :], pb[:, j * 128:(j + 1) * 128], ident[:, :]))
                                if j == q:
                                    tk[("trl", bh, q)] = tk[("tr", bh, q, j)]

                            @block.vector
                            def _(v, bh=bh, q=q, j=j):
                                c.wait(v, tk[("tr", bh, q, j)])
                                if j <= 1:
                                    c.wait(v, tk.get(("amml", bh, q - 1)) or
                                           tk.get(("amml", bh - 1, QT - 1)))
                                tk[("trev", bh, q, j)] = c.inc("vector", v.tensor_copy(
                                    pbT[:, j * 128:(j + 1) * 128], ps_tr[j % 2][:, :]))

                            @block.tensor
                            def _(t, bh=bh, q=q, j=j):
                                c.wait(t, tk[("trev", bh, q, j)], tk[("lv", bh)])
                                if j == 0:
                                    c.wait(t, tk.get(("aev", bh, q - 1)) or
                                           tk.get(("aev", bh - 1, QT - 1)))
                                tk[("amm", bh, q, j)] = c.inc("tensor", t.matmul(
                                    ps_a[:, :], vr[:, j * 128:(j + 1) * 128],
                                    pbT[:, j * 128:(j + 1) * 128],
                                    start=(j == 0), stop=(j == q)))
                                if j == q:
                                    tk[("amml", bh, q)] = tk[("amm", bh, q, j)]
                                    if q == QT - 1:
                                        tk[("amml", bh)] = tk[("amm", bh, q, j)]

                        @block.scalar
                        def _(sc, bh=bh, q=q):
                            c.wait(sc, tk[("amml", bh, q)])
                            if q == 0:
                                c.wait(sc, tk.get(("ast", bh - 1)))
                            tk[("aev", bh, q)] = c.inc("scalar", sc.copy(
                                at[:, q * 128:(q + 1) * 128], ps_a[:, :]))

                    @block.sync
                    def _(sy, b=b, h=h, bh=bh):
                        c.wait(sy, [tk[("aev", bh, q)] for q in range(QT)])
                        tk[("ast", bh)] = c.dinc(pfx + "ao", sy.dma_start(
                            out=out_dr[h * 128:(h + 1) * 128, b * T:(b + 1) * T],
                            in_=at[:, :]))

            @block.sync
            def _(sy):
                c.wait(sy, tk[("ast", BPC * NH - 1)])
    return



def emit_gemm(nc, c, st, pfx, wT_dr, IN, OUT, x_dr, out_dr, residual_dr=None,
              scale_fn=None, in_dtype=BF16):
    KTI = IN // 128
    MT = OUT // 128
    with contextlib.ExitStack() as stA:
        xsb = stA.enter_context(nc.sbuf_tensor(pfx + "x", [128, KTI * NTOK], BF16))
        wt = [stA.enter_context(nc.sbuf_tensor(pfx + f"w{i}", [128, KTI * 128], BF16))
              for i in range(2)]
        ob = stA.enter_context(nc.sbuf_tensor(pfx + "ob", [128, NTOK],
                                              F32 if residual_dr is not None else BF16))
        res = [stA.enter_context(nc.sbuf_tensor(pfx + f"r{i}", [128, NTOK], F32))
               for i in range(2)] if residual_dr is not None else None
        ps = [stA.enter_context(nc.psum_tensor(pfx + f"ps{i}", [128, 512], F32))
              for i in range(4)]

        x_t = x_dr.rearrange("(kt p) n -> p kt n", p=128)
        w_t = wT_dr.rearrange("(kt p) (mt m) -> p kt mt m", p=128, m=128)
        o_t = out_dr.rearrange("(mt p) n -> p mt n", p=128)
        r_t = residual_dr.rearrange("(mt p) n -> p mt n", p=128) if residual_dr is not None else None

        tk = {}
        with nc.Block() as block:
            @block.gpsimd
            def _(g):
                nq = max(1, KTI // 4)
                tk["x"] = [c.dinc(pfx + "x", g.dma_start(
                    out=xsb[:, q * (KTI // nq) * NTOK:(q + 1) * (KTI // nq) * NTOK].rearrange(
                        "p (kt n) -> p kt n", n=NTOK),
                    in_=x_t[:, q * (KTI // nq):(q + 1) * (KTI // nq), :])) for q in range(nq)]
                tk["x"] = Tok("D_" + pfx + "x", c.cnt["D_" + pfx + "x"])

            for mt in range(MT):
                buf = mt % 2

                @block.gpsimd
                def _(g, mt=mt, buf=buf):
                    c.wait(g, tk.get(("mml", mt - 2)))
                    tk[("w", mt)] = c.dinc(pfx + f"w{buf}", g.dma_start(
                        out=wt[buf].ap().rearrange("p (kt m) -> p kt m", m=128),
                        in_=w_t[:, :, mt, :]))
                    if res is not None:
                        c.wait(g, tk.get(("ev", mt - 2, NSTRIP - 1)))
                        tk[("r", mt)] = c.dinc(pfx + f"r{buf}", g.dma_start(
                            out=res[buf].ap(), in_=r_t[:, mt, :]))

                for s in range(NSTRIP):
                    @block.tensor
                    def _(t, mt=mt, buf=buf, s=s):
                        if s == 0:
                            c.wait(t, tk[("w", mt)], tk["x"])
                        bank = (mt * NSTRIP + s) % 4
                        c.wait(t, tk.get(("ev", mt, s - 4)) or
                               (tk.get(("ev", mt - 1, NSTRIP + s - 4)) if s < 4 else None))
                        for k in range(KTI):
                            last = t.matmul(ps[bank][:, :], wt[buf][:, k * 128:(k + 1) * 128],
                                            xsb[:, k * NTOK + s * 512: k * NTOK + (s + 1) * 512],
                                            start=(k == 0), stop=(k == KTI - 1))
                        tk[("mm", mt, s)] = c.inc("tensor", last)
                        if s == NSTRIP - 1:
                            tk[("mml", mt)] = tk[("mm", mt, s)]

                    if res is not None:
                        @block.vector
                        def _(v, mt=mt, buf=buf, s=s):
                            bank = (mt * NSTRIP + s) % 4
                            c.wait(v, tk[("mm", mt, s)], tk.get(("r", mt)))
                            if s == 0:
                                c.wait(v, tk.get(("ost", mt - 1)))
                            sc = 1.0 if scale_fn is None else scale_fn(mt)
                            tk[("ev", mt, s)] = c.inc("vector", v.scalar_tensor_tensor(
                                ob[:, s * 512:(s + 1) * 512], ps[bank][:, :], sc,
                                res[buf][:, s * 512:(s + 1) * 512], ALU.mult, ALU.add))
                    else:
                        @block.scalar
                        def _(sc_, mt=mt, s=s):
                            bank = (mt * NSTRIP + s) % 4
                            c.wait(sc_, tk[("mm", mt, s)])
                            if s == 0:
                                c.wait(sc_, tk.get(("ost", mt - 1)))
                            sc = 1.0 if scale_fn is None else scale_fn(mt)
                            tk[("ev", mt, s)] = c.inc("scalar", sc_.mul(
                                ob[:, s * 512:(s + 1) * 512], ps[bank][:, :], sc))

                @block.sync
                def _(sy, mt=mt):
                    c.wait(sy, [tk[("ev", mt, s)] for s in range(NSTRIP)])
                    tk[("ost", mt)] = c.dinc(pfx + "o", sy.dma_start(
                        out=o_t[:, mt, :], in_=ob.ap()))

            @block.sync
            def _(sy):
                c.wait(sy, tk[("ost", MT - 1)])
    return



def emit_attn(nc, c, st, pfx, qkv_dr, out_dr, ident, ctok):
    NH = 16
    QT = T // 128
    with contextlib.ExitStack() as stA:
        qt_ = stA.enter_context(nc.sbuf_tensor(pfx + "q", [128, T], BF16))
        kt_ = stA.enter_context(nc.sbuf_tensor(pfx + "k", [128, T], BF16))
        vr = stA.enter_context(nc.sbuf_tensor(pfx + "v", [128, T], BF16))
        pb = stA.enter_context(nc.sbuf_tensor(pfx + "pb", [128, T], BF16))
        pbT = stA.enter_context(nc.sbuf_tensor(pfx + "pbT", [128, T], BF16))
        at = stA.enter_context(nc.sbuf_tensor(pfx + "at", [128, T], BF16))
        rows = stA.enter_context(nc.sbuf_tensor(pfx + "rows", [128, 2], F32))
        maskc = stA.enter_context(nc.sbuf_tensor(pfx + "maskc", [128, 128], F32))
        ps_s = stA.enter_context(nc.psum_tensor(pfx + "pss", [128, T], F32))
        ps_tr = stA.enter_context(nc.psum_tensor(pfx + "pstr", [128, 128], BF16))
        ps_a = stA.enter_context(nc.psum_tensor(pfx + "psa", [128, 128], F32))

        tk = {}
        with nc.Block() as block:
            @block.gpsimd
            def _(g):
                t0 = c.inc("gpsimd", g.memset(maskc[:, :], 0.0))
                c.wait(g, t0)
                tk["mask"] = c.inc("gpsimd", g.affine_select(
                    out=maskc[:, :], in_=maskc[:, :], compare_op=ALU.is_ge,
                    fill=-1e30, base=0, pattern=[[-1, 128]], channel_multiplier=1))

            for b in range(BPC):
                for h in range(NH):
                    bh = b * NH + h

                    @block.gpsimd
                    def _(g, b=b, h=h, bh=bh):
                        c.wait(g, tk.get(("aev", bh - 1, QT - 1)))
                        tk[("lq", bh)] = c.dinc(pfx + "lq", g.dma_start(
                            out=qt_[:, :], in_=qkv_dr[h * 128:(h + 1) * 128, b * T:(b + 1) * T]))
                        tk[("lk", bh)] = c.dinc(pfx + "lk", g.dma_start(
                            out=kt_[:, :], in_=qkv_dr[2048 + h * 128:2048 + (h + 1) * 128,
                                                      b * T:(b + 1) * T]))

                    @block.scalar
                    def _(sc, b=b, h=h, bh=bh):
                        c.wait(sc, tk.get(("amml", bh - 1)))
                        vt = []
                        for j in range(QT):
                            vt.append(c.dinc(pfx + "lv", sc.dma_start(
                                out=vr[:, j * 128:(j + 1) * 128],
                                in_=qkv_dr[4096 + h * 128:4096 + (h + 1) * 128,
                                           b * T + j * 128: b * T + (j + 1) * 128],
                                transpose=True)))
                        tk[("lv", bh)] = Tok("D_" + pfx + "lv", c.cnt["D_" + pfx + "lv"])

                    for q in range(QT):
                        kend = (q + 1) * 128

                        @block.tensor
                        def _(t, bh=bh, q=q):
                            c.wait(t, tk[("lq", bh)], tk[("lk", bh)])
                            c.wait(t, tk.get(("exp", bh, q - 1)) or tk.get(("exp", bh - 1, QT - 1)))
                            last = None
                            for ks in range(0, kend, 512):
                                n = min(512, kend - ks)
                                last = t.matmul(ps_s[:, ks:ks + n], qt_[:, q * 128:(q + 1) * 128],
                                                kt_[:, ks:ks + n], start=True, stop=True)
                            tk[("smm", bh, q)] = c.inc("tensor", last)

                        @block.vector
                        def _(v, bh=bh, q=q):
                            c.wait(v, tk[("smm", bh, q)], tk["mask"])
                            t1 = c.inc("vector", v.tensor_add(
                                ps_s[:, q * 128:(q + 1) * 128], ps_s[:, q * 128:(q + 1) * 128],
                                maskc[:, :]))
                            c.wait(v, t1)
                            tk[("nm", bh, q)] = c.inc("vector", v.reduce_max(
                                rows[:, 0:1], ps_s[:, 0:kend], AX.X, negate=True))

                        @block.scalar
                        def _(sc, bh=bh, q=q):
                            c.wait(sc, tk[("nm", bh, q)])
                            c.wait(sc, tk.get(("trl", bh, q - 1)) or tk.get(("trl", bh - 1, QT - 1)))
                            tk[("exp", bh, q)] = c.inc("scalar", sc.activation(
                                pb[:, 0:kend], ps_s[:, 0:kend], AF.Exp, bias=rows[:, 0:1],
                                accum_out=rows[:, 1:2]))

                        @block.vector
                        def _(v, bh=bh, q=q):
                            c.wait(v, tk[("exp", bh, q)])
                            t1 = c.inc("vector", v.reciprocal(rows[:, 1:2], rows[:, 1:2]))
                            c.wait(v, t1)
                            tk[("pscale", bh, q)] = c.inc("vector", v.tensor_scalar(
                                pb[:, 0:kend], pb[:, 0:kend], rows[:, 1:2], None, ALU.mult))

                        for j in range(q + 1):
                            @block.tensor
                            def _(t, bh=bh, q=q, j=j):
                                c.wait(t, tk[("pscale", bh, q)])
                                c.wait(t, tk.get(("trev", bh, q, j - 1)) or
                                       tk.get(("trev", bh, q - 1, q - 1)) or
                                       tk.get(("trev", bh - 1, QT - 1, QT - 1)))
                                tk[("tr", bh, q, j)] = c.inc("tensor", t.transpose(
                                    ps_tr[:, :], pb[:, j * 128:(j + 1) * 128], ident[:, :]))
                                if j == q:
                                    tk[("trl", bh, q)] = tk[("tr", bh, q, j)]

                            @block.vector
                            def _(v, bh=bh, q=q, j=j):
                                c.wait(v, tk[("tr", bh, q, j)])
                                if j == 0:
                                    c.wait(v, tk.get(("amml", bh, q - 1)) or
                                           tk.get(("amml", bh - 1, QT - 1)))
                                tk[("trev", bh, q, j)] = c.inc("vector", v.tensor_copy(
                                    pbT[:, j * 128:(j + 1) * 128], ps_tr[:, :]))

                        @block.tensor
                        def _(t, bh=bh, q=q):
                            c.wait(t, [tk[("trev", bh, q, j)] for j in range(q + 1)],
                                   tk[("lv", bh)])
                            c.wait(t, tk.get(("aev", bh, q - 1)) or tk.get(("aev", bh - 1, QT - 1)))
                            for j in range(q + 1):
                                last = t.matmul(ps_a[:, :], vr[:, j * 128:(j + 1) * 128],
                                                pbT[:, j * 128:(j + 1) * 128],
                                                start=(j == 0), stop=(j == q))
                            tk[("amm", bh, q)] = c.inc("tensor", last)
                            if q == QT - 1:
                                tk[("amml", bh)] = tk[("amm", bh, q)]
                            tk[("amml", bh, q)] = tk[("amm", bh, q)]

                        @block.scalar
                        def _(sc, bh=bh, q=q):
                            c.wait(sc, tk[("amm", bh, q)])
                            if q == 0:
                                c.wait(sc, tk.get(("ast", bh - 1)))
                            tk[("aev", bh, q)] = c.inc("scalar", sc.copy(
                                at[:, q * 128:(q + 1) * 128], ps_a[:, :]))

                    @block.sync
                    def _(sy, b=b, h=h, bh=bh):
                        c.wait(sy, [tk[("aev", bh, q)] for q in range(QT)])
                        tk[("ast", bh)] = c.dinc(pfx + "ao", sy.dma_start(
                            out=out_dr[h * 128:(h + 1) * 128, b * T:(b + 1) * T],
                            in_=at[:, :]))

            @block.sync
            def _(sy):
                c.wait(sy, tk[("ast", BPC * NH - 1)])
    return



def emit_l0proj(nc, c, st, pfx, xln_dr, wqT, wkT, wvT, convq, convk, convv,
                wf1T, wf2T, dtbT_dr, alog_dr, wbT, wg1T, bg1_dr, wg2T, bg2T_dr,
                qn_dr, kni_dr, vni_dr, la_dr, beta_dr, gate_dr,
                onescol, onesrow, cmap, ctok, identb=None):
    SCALE_NONE = None
    xln = st.enter_context(nc.sbuf_tensor(pfx + "xln", [128, KT * NTOK], BF16))
    xln_t = xln_dr.rearrange("(kt p) n -> p kt n", p=128)
    tk = {}
    with nc.Block() as block:
        @block.gpsimd
        def _(g):
            for q in range(4):
                c.dinc(pfx + "xin", g.dma_start(
                    out=xln[:, q * 4 * NTOK:(q + 1) * 4 * NTOK].rearrange("p (kt n) -> p kt n", n=NTOK),
                    in_=xln_t[:, q * 4:(q + 1) * 4, :]))
            tk["x"] = Tok("D_" + pfx + "xin", c.cnt["D_" + pfx + "xin"])

    with contextlib.ExitStack() as s1:
        wf1s = s1.enter_context(nc.sbuf_tensor(pfx + "wf1", [128, KT * 128], BF16))
        wf2s = s1.enter_context(nc.sbuf_tensor(pfx + "wf2", [128, D], BF16))
        wbs = s1.enter_context(nc.sbuf_tensor(pfx + "wbs", [128, KT * 64], BF16))
        wg1s = s1.enter_context(nc.sbuf_tensor(pfx + "wg1", [128, KT * 128], BF16))
        wg2s = s1.enter_context(nc.sbuf_tensor(pfx + "wg2", [128, D], BF16))
        dtb = s1.enter_context(nc.sbuf_tensor(pfx + "dtb", [128, 16], F32))
        bg2 = s1.enter_context(nc.sbuf_tensor(pfx + "bg2", [128, 16], F32))
        bg1 = s1.enter_context(nc.sbuf_tensor(pfx + "bg1", [128, 1], F32))
        alr = s1.enter_context(nc.sbuf_tensor(pfx + "alr", [1, 32], F32))
        alrb = s1.enter_context(nc.sbuf_tensor(pfx + "alrb", [1, 16], BF16))
        calog = s1.enter_context(nc.sbuf_tensor(pfx + "calog", [128, 16], F32))
        t1 = s1.enter_context(nc.sbuf_tensor(pfx + "t1", [128, NTOK], BF16))
        t2 = s1.enter_context(nc.sbuf_tensor(pfx + "t2", [128, NTOK], BF16))
        spb = s1.enter_context(nc.sbuf_tensor(pfx + "spb", [128, 512 * 4], F32))
        lab = s1.enter_context(nc.sbuf_tensor(pfx + "lab", [128, NTOK], F32))
        gb = s1.enter_context(nc.sbuf_tensor(pfx + "gb", [128, NTOK], BF16))
        btb = s1.enter_context(nc.sbuf_tensor(pfx + "btb", [64, NTOK], BF16))
        psA = [s1.enter_context(nc.psum_tensor(pfx + f"psA{i}", [128, 512], F32)) for i in range(2)]
        psB = [s1.enter_context(nc.psum_tensor(pfx + f"psB{i}", [128, 512], F32)) for i in range(2)]
        psC = [s1.enter_context(nc.psum_tensor(pfx + f"psC{i}", [64, 512], F32)) for i in range(2)]
        psr = s1.enter_context(nc.psum_tensor(pfx + "psr", [128, 16], F32))

        with nc.Block() as block:
            @block.gpsimd
            def _(g):
                c.dinc(pfx + "s1w", g.dma_start(
                    out=wf1s.ap().rearrange("p (kt m) -> p kt m", m=128),
                    in_=wf1T.rearrange("(kt p) m -> p kt m", p=128)))
                c.dinc(pfx + "s1w", g.dma_start(out=wf2s[:, :], in_=wf2T[:, :]))
                c.dinc(pfx + "s1w", g.dma_start(
                    out=wbs.ap().rearrange("p (kt m) -> p kt m", m=64),
                    in_=wbT.rearrange("(kt p) m -> p kt m", p=128)))
                c.dinc(pfx + "s1w", g.dma_start(
                    out=wg1s.ap().rearrange("p (kt m) -> p kt m", m=128),
                    in_=wg1T.rearrange("(kt p) m -> p kt m", p=128)))
                c.dinc(pfx + "s1w", g.dma_start(out=wg2s[:, :], in_=wg2T[:, :]))
                c.dinc(pfx + "s1w", g.dma_start(out=dtb[:, :], in_=dtbT_dr[:, :]))
                c.dinc(pfx + "s1w", g.dma_start(out=bg2[:, :], in_=bg2T_dr[:, :]))
                c.dinc(pfx + "s1w", g.dma_start(out=bg1[:, :], in_=bg1_dr[:, None]))
                c.dinc(pfx + "s1w", g.dma_start(out=alr[0:1, 0:16], in_=alog_dr[None, :]))
                tk["s1w"] = Tok("D_" + pfx + "s1w", c.cnt["D_" + pfx + "s1w"])

            @block.scalar
            def _(sc):
                c.wait(sc, tk["s1w"])
                tk["al1"] = c.inc("scalar", sc.activation(alr[0:1, 16:32], alr[0:1, 0:16], AF.Exp))

            @block.vector
            def _(v):
                c.wait(v, tk["al1"])
                t_ = c.inc("vector", v.tensor_scalar(alr[0:1, 16:32], alr[0:1, 16:32], -1.0, None, ALU.mult))
                c.wait(v, t_)
                tk["al2"] = c.inc("vector", v.tensor_copy(alrb[0:1, :], alr[0:1, 16:32]))

            @block.tensor
            def _(t):
                c.wait(t, tk["al2"], ctok)
                tk["al3"] = c.inc("tensor", t.matmul(psr[:, :], onesrow[:, :], alrb[0:1, :],
                                                     start=True, stop=True))

            @block.vector
            def _(v):
                c.wait(v, tk["al3"])
                tk["calog"] = c.inc("vector", v.tensor_copy(calog[:, :], psr[:, :]))

            for s in range(NSTRIP):
                @block.tensor
                def _(t, s=s):
                    c.wait(t, tk["s1w"], tk["x"])
                    c.wait(t, tk.get(("ev1", s - 2)), tk.get(("ev2", s - 2)), tk.get(("evb", s - 2)))
                    for k in range(KT):
                        l1 = t.matmul(psA[s % 2][:, :], wf1s[:, k * 128:(k + 1) * 128],
                                      xln[:, k * NTOK + s * 512: k * NTOK + (s + 1) * 512],
                                      start=(k == 0), stop=(k == KT - 1))
                    tk[("mm1", s)] = c.inc("tensor", l1)
                    for k in range(KT):
                        l2 = t.matmul(psB[s % 2][:, :], wg1s[:, k * 128:(k + 1) * 128],
                                      xln[:, k * NTOK + s * 512: k * NTOK + (s + 1) * 512],
                                      start=(k == 0), stop=(k == KT - 1))
                    tk[("mm2", s)] = c.inc("tensor", l2)
                    for k in range(KT):
                        l3 = t.matmul(psC[s % 2][:, :], wbs[:, k * 64:(k + 1) * 64],
                                      xln[:, k * NTOK + s * 512: k * NTOK + (s + 1) * 512],
                                      start=(k == 0), stop=(k == KT - 1))
                    tk[("mm3", s)] = c.inc("tensor", l3)

                @block.scalar
                def _(sc, s=s):
                    c.wait(sc, tk[("mm1", s)])
                    tk[("ev1", s)] = c.inc("scalar", sc.copy(t1[:, s * 512:(s + 1) * 512], psA[s % 2][:, :]))
                    c.wait(sc, tk[("mm2", s)])
                    tk[("ev2", s)] = c.inc("scalar", sc.activation(
                        t2[:, s * 512:(s + 1) * 512], psB[s % 2][:, :], AF.Identity, bias=bg1[:, 0:1]))
                    c.wait(sc, tk[("mm3", s)])
                    tk[("evb", s)] = c.inc("scalar", sc.activation(
                        btb[0:64, s * 512:(s + 1) * 512], psC[s % 2][0:64, :], AF.Sigmoid))

            @block.sync
            def _(sy):
                c.wait(sy, [tk[("evb", s)] for s in range(NSTRIP)])
                tk["bst"] = c.dinc(pfx + "bst", sy.dma_start(out=beta_dr[:, :], in_=btb[:, :]))

            for h in range(16):
                for s in range(NSTRIP):
                    @block.tensor
                    def _(t, h=h, s=s):
                        c.wait(t, [tk[("ev1", ss)] for ss in range(NSTRIP)])
                        c.wait(t, tk.get(("spl", h, s - 2)) or tk.get(("spl", h - 1, NSTRIP - 2 + s)))
                        tk[("mla", h, s)] = c.inc("tensor", t.matmul(
                            psA[s % 2][:, :], wf2s[:, h * 128:(h + 1) * 128],
                            t1[:, s * 512:(s + 1) * 512], start=True, stop=True))

                    @block.scalar
                    def _(sc, h=h, s=s):
                        c.wait(sc, tk[("mla", h, s)])
                        c.wait(sc, tk.get(("laf", h, s - 1)) or tk.get(("laf", h - 1, NSTRIP - 1)))
                        e1 = c.inc("scalar", sc.activation(
                            spb[:, 0:512], psA[s % 2][:, :], AF.Relu, bias=dtb[:, h:h + 1]))
                        e2 = c.inc("scalar", sc.activation(
                            spb[:, 512:1024], psA[s % 2][:, :], AF.Abs, bias=dtb[:, h:h + 1]))
                        c.wait(sc, e2)
                        e3 = c.inc("scalar", sc.activation(
                            spb[:, 1024:1536], spb[:, 512:1024], AF.Exp, scale=-1.0))
                        c.wait(sc, e3)
                        e4 = c.inc("scalar", sc.activation(
                            spb[:, 1536:2048], spb[:, 1024:1536], AF.Ln, bias=cmap[1.0]))
                        tk[("spl", h, s)] = [e1, e4]

                    @block.vector
                    def _(v, h=h, s=s):
                        c.wait(v, tk[("spl", h, s)], tk["calog"])
                        t_ = c.inc("vector", v.tensor_add(spb[:, 0:512], spb[:, 0:512],
                                                          spb[:, 1536:2048]))
                        c.wait(v, t_)
                        c.wait(v, tk.get(("last", h - 1)))
                        tk[("laf", h, s)] = c.inc("vector", v.tensor_scalar(
                            lab[:, s * 512:(s + 1) * 512], spb[:, 0:512],
                            calog[:, h:h + 1], None, ALU.mult))

                @block.sync
                def _(sy, h=h):
                    c.wait(sy, [tk[("laf", h, s)] for s in range(NSTRIP)])
                    tk[("last", h)] = c.dinc(pfx + "last", sy.dma_start(
                        out=la_dr[h, :, :], in_=lab[:, :]))

            for h in range(16):
                for s in range(NSTRIP):
                    @block.tensor
                    def _(t, h=h, s=s):
                        c.wait(t, [tk[("ev2", ss)] for ss in range(NSTRIP)])
                        c.wait(t, tk.get(("gev", h, s - 2)) or tk.get(("gev", h - 1, NSTRIP - 2 + s)))
                        tk[("mg", h, s)] = c.inc("tensor", t.matmul(
                            psB[s % 2][:, :], wg2s[:, h * 128:(h + 1) * 128],
                            t2[:, s * 512:(s + 1) * 512], start=True, stop=True))

                    @block.scalar
                    def _(sc, h=h, s=s):
                        c.wait(sc, tk[("mg", h, s)])
                        if s == 0:
                            c.wait(sc, tk.get(("gst", h - 1)))
                        tk[("gev", h, s)] = c.inc("scalar", sc.activation(
                            gb[:, s * 512:(s + 1) * 512], psB[s % 2][:, :], AF.Identity,
                            bias=bg2[:, h:h + 1]))

                @block.sync
                def _(sy, h=h):
                    c.wait(sy, [tk[("gev", h, s)] for s in range(NSTRIP)])
                    tk[("gst", h)] = c.dinc(pfx + "gst", sy.dma_start(
                        out=gate_dr[h * 128:(h + 1) * 128, :], in_=gb[:, :]))

    emit_barrier(nc, c)

    with contextlib.ExitStack() as s2:
        wpl = [s2.enter_context(nc.sbuf_tensor(pfx + f"wpl{i}", [128, KT * 128], BF16)) for i in range(2)]
        cvw = s2.enter_context(nc.sbuf_tensor(pfx + "cvw", [128, (16 + 64 + 64) * 4], F32))
        dgj = [s2.enter_context(nc.sbuf_tensor(pfx + f"dgj{i}", [128, 4 * 128], BF16))
               for i in range(2)]
        praw2 = [s2.enter_context(nc.sbuf_tensor(pfx + f"praw{i}", [128, NTOK + 8], BF16))
                 for i in range(2)]
        pcv = s2.enter_context(nc.sbuf_tensor(pfx + "pcv", [128, NTOK], BF16))
        kist = s2.enter_context(nc.sbuf_tensor(pfx + "kist", [128, 2 * 8192], BF16))
        rowb = s2.enter_context(nc.sbuf_tensor(pfx + "rowb", [1, NTOK], BF16))
        pspl = [s2.enter_context(nc.psum_tensor(pfx + f"pspl{i}", [128, 512], F32)) for i in range(4)]
        psn = s2.enter_context(nc.psum_tensor(pfx + "psn", [1, NTOK // 4], F32))
        psrep = [s2.enter_context(nc.psum_tensor(pfx + f"psrep{i}", [128, 512], F32)) for i in range(2)]

        with nc.Block() as block:
            @block.gpsimd
            def _(g):
                c.dinc(pfx + "cv", g.dma_start(
                    out=cvw[:, 0:64].rearrange("p (pl j) -> p pl j", j=4),
                    in_=convq.rearrange("(pl p) j -> p pl j", p=128)))
                c.dinc(pfx + "cv", g.dma_start(
                    out=cvw[:, 64:320].rearrange("p (pl j) -> p pl j", j=4),
                    in_=convk.rearrange("(pl p) j -> p pl j", p=128)))
                c.dinc(pfx + "cv", g.dma_start(
                    out=cvw[:, 320:576].rearrange("p (pl j) -> p pl j", j=4),
                    in_=convv.rearrange("(pl p) j -> p pl j", p=128)))
                tk["cv"] = Tok("D_" + pfx + "cv", c.cnt["D_" + pfx + "cv"])
                z1 = c.inc("gpsimd", g.memset(praw2[0][:, 0:4], 0.0))
                z2 = c.inc("gpsimd", g.memset(praw2[0][:, 2052:2056], 0.0))
                z3 = c.inc("gpsimd", g.memset(praw2[1][:, 0:4], 0.0))
                z4 = c.inc("gpsimd", g.memset(praw2[1][:, 2052:2056], 0.0))
                tk["pz"] = [z1, z2, z3, z4]

            planes = []
            for h in range(16):
                planes.append(("q", h, 0, wqT, h, h))
            for h in range(16):
                for i in range(4):
                    planes.append(("k", h, i, wkT, h * 4 + i, 16 + h * 4 + i))
                for i in range(4):
                    planes.append(("v", h, i, wvT, h * 4 + i, 80 + h * 4 + i))

            pending = []
            for ip, (kind, h, i, wT, mcol, ccol) in enumerate(planes):
                buf = ip % 2
                praw = praw2[ip % 2]
                w_t = wT.rearrange("(kt p) (mt m) -> p kt mt m", p=128, m=128)

                @block.gpsimd
                def _(g, ip=ip, buf=buf, w_t=w_t, mcol=mcol):
                    c.wait(g, tk.get(("pmml", ip - 2)))
                    tk[("wld", ip)] = c.dinc(pfx + f"wpl{buf}", g.dma_start(
                        out=wpl[buf].ap().rearrange("p (kt m) -> p kt m", m=128),
                        in_=w_t[:, :, mcol, :]))

                for s in range(NSTRIP):
                    @block.tensor
                    def _(t, ip=ip, buf=buf, s=s):
                        c.wait(t, tk[("wld", ip)], tk["x"])
                        c.wait(t, tk.get(("prev", ip, s - 2)) or tk.get(("prev", ip - 1, NSTRIP - 2 + s)))
                        for k in range(KT):
                            last = t.matmul(pspl[s % 2][:, :], wpl[buf][:, k * 128:(k + 1) * 128],
                                            xln[:, k * NTOK + s * 512: k * NTOK + (s + 1) * 512],
                                            start=(k == 0), stop=(k == KT - 1))
                        tk[("pmm", ip, s)] = c.inc("tensor", last)
                        if s == NSTRIP - 1:
                            tk[("pmml", ip)] = tk[("pmm", ip, s)]

                    @block.scalar
                    def _(sc, ip=ip, s=s):
                        c.wait(sc, tk[("pmm", ip, s)])
                        if s == 0:
                            c.wait(sc, tk.get(("praw_free", ip - 2)))
                        b_ = s // 4
                        off = 4 + b_ * 2052 + (s % 4) * 512
                        tk[("prev", ip, s)] = c.inc("scalar", sc.copy(
                            praw[:, off:off + 512], pspl[s % 2][:, :]))

                while pending:
                    pending.pop(0)()

                @block.vector
                def _(v, ip=ip, ccol=ccol):
                    c.wait(v, tk["cv"], ctok)
                    c.wait(v, tk.get(("convmm", ip - 2)))
                    last = None
                    for j in range(4):
                        last = c.inc("vector", v.tensor_scalar(
                            dgj[ip % 2][:, j * 128:(j + 1) * 128], identb[:, :],
                            cvw[:, ccol * 4 + j:ccol * 4 + j + 1], None, ALU.mult))
                    tk[("dg", ip)] = last

                for s8 in range(NSTRIP):
                    @block.tensor
                    def _(t, ip=ip, s8=s8):
                        b = s8 // 4
                        off = 4 + b * 2052 + (s8 % 4) * 512
                        c.wait(t, tk[("dg", ip)], tk[("prev", ip, s8)], tk["pz"])
                        c.wait(t, tk.get(("cvev", ip, s8 - 2)) or
                               tk.get(("cvev", ip - 1, NSTRIP - 2 + s8)))
                        last = None
                        for j in range(4):
                            sh = 3 - j
                            last = t.matmul(pspl[2 + s8 % 2][:, :],
                                            dgj[ip % 2][:, j * 128:(j + 1) * 128],
                                            praw[:, off - sh:off - sh + 512],
                                            start=(j == 0), stop=(j == 3))
                        tk[("cvmm", ip, s8)] = c.inc("tensor", last)

                    @block.scalar
                    def _(sc, ip=ip, s8=s8):
                        c.wait(sc, tk[("cvmm", ip, s8)])
                        if s8 == 0:
                            c.wait(sc, tk.get(("sq", ip - 1)))
                            c.wait(sc, tk.get(("pncp", ip - 1)), tk.get(("pst", ip - 1)))
                        tk[("cvev", ip, s8)] = c.inc("scalar", sc.activation(
                            pcv[:, s8 * 512:(s8 + 1) * 512], pspl[2 + s8 % 2][:, :], AF.Silu))
                tk[("conv", ip)] = [tk[("cvev", ip, s8)] for s8 in range(NSTRIP)]
                tk[("convmm", ip)] = tk[("cvmm", ip, NSTRIP - 1)]
                tk[("silu", ip)] = tk[("conv", ip)]

                if kind == "v":
                    tk[("praw_free", ip)] = tk[("convmm", ip)]

                def deferred(ip=ip, kind=kind, h=h, i=i, ccol=ccol, praw=praw, pcv=pcv):
                    if kind in ("q", "k"):
                        @block.vector
                        def _(v, ip=ip):
                            c.wait(v, tk[("silu", ip)])
                            c.inc("vector", v.tensor_mul(
                                praw[:, 4:4 + T], pcv[:, 0:T], pcv[:, 0:T]))
                            tk[("sq", ip)] = c.inc("vector", v.tensor_mul(
                                praw[:, 2056:2056 + T], pcv[:, T:2 * T], pcv[:, T:2 * T]))

                        for hf in range(4):
                            @block.tensor
                            def _(t, ip=ip, hf=hf):
                                if hf == 0:
                                    c.wait(t, tk[("sq", ip)], ctok)
                                c.wait(t, tk.get(("nrm", ip, hf - 1)) or tk.get(("nrm", ip - 1, 3)))
                                last = None
                                for s_ in range(2 * hf, 2 * hf + 2):
                                    offq = 4 + (s_ // 4) * 2052 + (s_ % 4) * 512
                                    last = t.matmul(psn[0:1, (s_ - 2 * hf) * 512:(s_ - 2 * hf + 1) * 512],
                                                    onescol[:, :], praw[:, offq:offq + 512],
                                                    start=True, stop=True)
                                tk[("ssm", ip, hf)] = c.inc("tensor", last)
                                if hf == 3:
                                    tk[("praw_free", ip)] = tk[("ssm", ip, hf)]

                            @block.scalar
                            def _(sc, ip=ip, hf=hf):
                                c.wait(sc, tk[("ssm", ip, hf)])
                                t_ = c.inc("scalar", sc.activation(
                                    psn[0:1, :], psn[0:1, :], AF.Ln, bias=cmap[1e-24][0:1, 0:1]))
                                c.wait(sc, t_)
                                tk[("srt", ip, hf)] = c.inc("scalar", sc.activation(
                                    psn[0:1, :], psn[0:1, :], AF.Exp, scale=-0.5))

                            @block.vector
                            def _(v, ip=ip, hf=hf):
                                c.wait(v, tk[("srt", ip, hf)])
                                tk[("nrm", ip, hf)] = c.inc("vector", v.tensor_copy(
                                    rowb[0:1, hf * 1024:(hf + 1) * 1024], psn[0:1, :]))

                        for s in range(NSTRIP):
                            @block.tensor
                            def _(t, ip=ip, s=s):
                                c.wait(t, tk[("nrm", ip, s // 2)])
                                c.wait(t, tk.get(("pnm", ip, s - 2)) or
                                       tk.get(("pnm", ip - 1, NSTRIP - 2 + s)))
                                tk[("rep", ip, s)] = c.inc("tensor", t.matmul(
                                    psrep[s % 2][:, :], onesrow[:, :], rowb[0:1, s * 512:(s + 1) * 512],
                                    start=True, stop=True))

                            @block.vector
                            def _(v, ip=ip, s=s):
                                c.wait(v, tk[("rep", ip, s)], tk[("conv", ip)], tk[("silu", ip)])
                                if s == 0:
                                    c.wait(v, tk.get(("pncp", ip - 1)) or tk.get(("pst", ip - 1)))
                                tk[("pnm", ip, s)] = c.inc("vector", v.tensor_mul(
                                    pcv[:, s * 512:(s + 1) * 512], pcv[:, s * 512:(s + 1) * 512],
                                    psrep[s % 2][:, :]))
                        src = pcv
                        src_tok = lambda ip=ip: [tk[("pnm", ip, s)] for s in range(NSTRIP)]
                    else:
                        src = pcv
                        src_tok = lambda ip=ip: [tk[("silu", ip)]]

                    if kind == "q":
                        @block.sync
                        def _(sy, ip=ip, h=h, src=src):
                            c.wait(sy, src_tok())
                            tk[("pst", ip)] = c.dinc(pfx + "qst", sy.dma_start(
                                out=qn_dr[h, :, :], in_=src.ap()))
                    else:
                        @block.vector
                        def _(v, ip=ip, i=i, src=src):
                            c.wait(v, src_tok())
                            if i == 0:
                                c.wait(v, tk.get(("ist", ip - 1)) or tk.get(("ist", ip - 4)))
                            ts = []
                            for b in range(BPC):
                                ts.append(c.inc("vector", v.tensor_copy(
                                    kist.ap().rearrange("p (b t i) -> p b t i", b=2, i=4)[:, b, :, i],
                                    src[:, b * T:(b + 1) * T])))
                            tk[("icp", ip)] = ts
                            tk[("pncp", ip)] = ts

                        if i == 3:
                            @block.sync
                            def _(sy, ip=ip, h=h, kind=kind):
                                c.wait(sy, [tk[("icp", ip - d)] for d in range(4)])
                                dst = kni_dr if kind == "k" else vni_dr
                                t0 = c.dinc(pfx + "ist", sy.dma_start(
                                    out=dst[0, h, :, :], in_=kist[:, 0:8192]))
                                t1 = c.dinc(pfx + "ist", sy.dma_start(
                                    out=dst[1, h, :, :], in_=kist[:, 8192:16384]))
                                tk[("ist", ip)] = [t0, t1]


                pending.append(deferred)
            for fn_ in pending:
                fn_()

            @block.sync
            def _(sy):
                c.wait(sy, tk[("ist", len(planes) - 1)])
    return



MKSCALE = float(128.0 ** -0.5)


def emit_mkda(nc, c, st, pfx, qn_dr, kni_dr, vni_dr, la_dr, beta_dr, o_dr,
              mskK_dr, mskQ_dr, onesrow, identb, ctok, bh_list=None):
    NCH = T // 64
    if bh_list is None:
        bh_list = [(b, h) for b in range(BPC) for h in range(16)]
    kI = st.enter_context(nc.sbuf_tensor(pfx + "kI", [128, 4 * T], BF16))
    qnh = st.enter_context(nc.sbuf_tensor(pfx + "qnh", [128, T], BF16))
    laG = st.enter_context(nc.sbuf_tensor(pfx + "laG", [128, T], F32))
    G = st.enter_context(nc.sbuf_tensor(pfx + "G", [128, T], F32))
    tmpB = st.enter_context(nc.sbuf_tensor(pfx + "tmpB", [128, 32 * 48], F32))
    brep = st.enter_context(nc.sbuf_tensor(pfx + "brep", [128, 4 * T], BF16))
    w_ = st.enter_context(nc.sbuf_tensor(pfx + "w", [128, 4 * T], BF16))
    khA = st.enter_context(nc.sbuf_tensor(pfx + "khA", [128, 4 * T], BF16))
    kexp = st.enter_context(nc.sbuf_tensor(pfx + "kexp", [128, 4 * T], BF16))
    kdec = st.enter_context(nc.sbuf_tensor(pfx + "kdec", [128, 4 * T], BF16))
    whA = st.enter_context(nc.sbuf_tensor(pfx + "whA", [128, 4 * T], BF16))
    whB = [None] + [st.enter_context(nc.sbuf_tensor(pfx + f"whB{i}", [128, NCH * 64 * i], BF16))
                    for i in (1, 2, 3)]
    ecB = [None] + [st.enter_context(nc.sbuf_tensor(pfx + f"ecB{i}", [128, NCH * 16 * i], BF16))
                    for i in (1, 2, 3)]
    qhA = st.enter_context(nc.sbuf_tensor(pfx + "qhA", [128, T], BF16))
    qgn = st.enter_context(nc.sbuf_tensor(pfx + "qgn", [128, T], BF16))
    e_r = st.enter_context(nc.sbuf_tensor(pfx + "e_r", [128, T], BF16))
    e_c = st.enter_context(nc.sbuf_tensor(pfx + "e_c", [128, T], BF16))
    e_g = st.enter_context(nc.sbuf_tensor(pfx + "e_g", [128, T], BF16))
    e_d = st.enter_context(nc.sbuf_tensor(pfx + "e_d", [128, T], BF16))
    glr = st.enter_context(nc.sbuf_tensor(pfx + "glr", [128, 2 * NCH], F32))
    negS = st.enter_context(nc.sbuf_tensor(pfx + "negS", [128, 128], F32))
    negSb = st.enter_context(nc.sbuf_tensor(pfx + "negSb", [128, 128], BF16))
    msm = st.enter_context(nc.sbuf_tensor(pfx + "msm", [128, 2 * 320], BF16))
    mskK = st.enter_context(nc.sbuf_tensor(pfx + "mskK", [128, 64], BF16))
    mskQ = st.enter_context(nc.sbuf_tensor(pfx + "mskQ", [128, 16], BF16))
    a0s = [st.enter_context(nc.sbuf_tensor(pfx + f"a0s{i}", [128, 128], BF16)) for i in range(2)]
    aB = [st.enter_context(nc.sbuf_tensor(pfx + f"aB{i}", [128, 128], BF16)) for i in range(2)]
    bL = [None] + [st.enter_context(nc.sbuf_tensor(pfx + f"bL{i}", [128, 128], BF16)) for i in range(1, 7)]
    uB = [st.enter_context(nc.sbuf_tensor(pfx + f"uB{i}", [128, 128], BF16)) for i in range(2)]
    u1f = st.enter_context(nc.sbuf_tensor(pfx + "u1f", [128, 128], BF16))
    vrow = st.enter_context(nc.sbuf_tensor(pfx + "vrow", [128, 256], BF16))
    kdT = st.enter_context(nc.sbuf_tensor(pfx + "kdT", [128, 256], BF16))
    obh = st.enter_context(nc.sbuf_tensor(pfx + "obh", [128, 16 * 128], BF16))

    pm = [st.enter_context(nc.psum_tensor(pfx + f"pm{i}", [128, 320], F32)) for i in range(2)]
    ps_tr = st.enter_context(nc.psum_tensor(pfx + "pstr", [128, 128], BF16))
    psq = st.enter_context(nc.psum_tensor(pfx + "psq", [128, 128], F32))
    pu = [st.enter_context(nc.psum_tensor(pfx + f"pu{i}", [128, 128], F32)) for i in range(2)]
    po = st.enter_context(nc.psum_tensor(pfx + "po", [64, 128], F32))
    psS = st.enter_context(nc.psum_tensor(pfx + "psS", [128, 128], F32))

    def Gv(tsz):
        return G.ap().rearrange("p (c t) -> p c t", t=tsz)

    tk = {}
    with nc.Block() as block:
        @block.gpsimd
        def _(g):
            c.dinc(pfx + "msk", g.dma_start(out=mskK[:, :], in_=mskK_dr[:, :]))
            c.dinc(pfx + "msk", g.dma_start(out=mskQ[:, :], in_=mskQ_dr[:, :]))
            tk["msk"] = Tok("D_" + pfx + "msk", c.cnt["D_" + pfx + "msk"])

        for ibh, (b, h) in enumerate(bh_list):
            @block.gpsimd
            def _(g, b=b, h=h, ibh=ibh):
                c.wait(g, tk.get(("ost", ibh - 1)))
                c.dinc(pfx + "in", g.dma_start(out=kI[:, :], in_=kni_dr[b, h, :, :]))
                c.dinc(pfx + "in", g.dma_start(out=qnh[:, :], in_=qn_dr[h, :, b * T:(b + 1) * T]))
                c.dinc(pfx + "in", g.dma_start(out=laG[:, :], in_=la_dr[h, :, b * T:(b + 1) * T]))
                for i in range(4):
                    c.dinc(pfx + "in", g.dma_start(
                        out=brep[:, i * T:(i + 1) * T],
                        in_=beta_dr[h * 4 + i:h * 4 + i + 1,
                                    b * T:(b + 1) * T].partition_broadcast(128)))
                tk[("in", ibh)] = Tok("D_" + pfx + "in", c.cnt["D_" + pfx + "in"])

            @block.vector
            def _(v, ibh=ibh):
                c.wait(v, tk[("in", ibh)])
                for cc in range(NCH):
                    c.inc("vector", v.tensor_tensor_scan(
                        G[:, cc * 64:(cc + 1) * 64], laG[:, cc * 64:(cc + 1) * 64],
                        laG[:, cc * 64:(cc + 1) * 64], 0.0, ALU.add, ALU.bypass))
                tk[("G", ibh)] = c.mark("vector")
                c.wait(v, tk[("G", ibh)])
                t1 = c.inc("vector", v.tensor_copy(
                    glr[:, 0:NCH].rearrange("p (c o) -> p c o", o=1), Gv(64)[:, :, 63:64]))
                t2 = c.inc("vector", v.tensor_sub(
                    laG.ap().rearrange("p (blk t) -> p blk t", t=16),
                    G.ap().rearrange("p (blk t) -> p blk t", t=16),
                    G.ap().rearrange("p (blk t) -> p blk t", t=16)[:, :, 0:1]
                     .broadcast_to([128, 128, 16])))
                tk[("dpre", ibh)] = [t1, t2]

            @block.scalar
            def _(sc, ibh=ibh):
                c.wait(sc, tk[("dpre", ibh)])
                c.inc("scalar", sc.activation(e_r[:, :], laG[:, :], AF.Exp))
                c.inc("scalar", sc.activation(e_c[:, :], laG[:, :], AF.Exp, scale=-1.0))
                c.inc("scalar", sc.activation(e_g[:, :], G[:, :], AF.Exp))
                c.inc("scalar", sc.activation(glr[:, NCH:2 * NCH], glr[:, 0:NCH], AF.Exp))
                tk[("exp1", ibh)] = c.mark("scalar")

            @block.vector
            def _(v, ibh=ibh):
                c.wait(v, tk[("exp1", ibh)])
                tk[("dd", ibh)] = c.inc("vector", v.tensor_sub(
                    laG.ap().rearrange("p (c t) -> p c t", t=64),
                    Gv(64)[:, :, 63:64].broadcast_to([128, NCH, 64]),
                    Gv(64)))

            @block.scalar
            def _(sc, ibh=ibh):
                c.wait(sc, tk[("dd", ibh)])
                tk[("e_dx", ibh)] = c.inc("scalar", sc.activation(e_d[:, :], laG[:, :], AF.Exp))

            for I in (1, 2, 3):
                @block.vector
                def _(v, ibh=ibh, I=I):
                    c.wait(v, tk[("G", ibh)])
                    if I > 1:
                        c.wait(v, tk[("ecBx", ibh, I - 1)])
                    tk[("dB", ibh, I)] = c.inc("vector", v.tensor_sub(
                        tmpB[:, 0:NCH * 16 * I].rearrange("p (cc s) -> p cc s", s=16 * I),
                        Gv(64)[:, :, 16 * I:16 * I + 1].broadcast_to([128, NCH, 16 * I]),
                        Gv(64)[:, :, 0:16 * I]))

                @block.scalar
                def _(sc, ibh=ibh, I=I):
                    c.wait(sc, tk[("dB", ibh, I)])
                    tk[("ecBx", ibh, I)] = c.inc("scalar", sc.activation(
                        ecB[I][:, :], tmpB[:, 0:NCH * 16 * I], AF.Exp))

            @block.vector
            def _(v, ibh=ibh):
                c.wait(v, tk[("exp1", ibh)], tk[("in", ibh)])
                t1 = c.inc("vector", v.tensor_mul(
                    w_.ap().rearrange("p (t i) -> p t i", i=4),
                    kI.ap().rearrange("p (t i) -> p t i", i=4),
                    brep.ap().rearrange("p (i t) -> p t i", t=T)))
                t2 = c.inc("vector", v.tensor_mul(
                    khA.ap().rearrange("p (t i) -> p t i", i=4),
                    kI.ap().rearrange("p (t i) -> p t i", i=4),
                    e_r.ap().rearrange("p (t o) -> p t o", o=1).broadcast_to([128, T, 4])))
                c.wait(v, t1)
                t3 = c.inc("vector", v.tensor_mul(
                    whA.ap().rearrange("p (t i) -> p t i", i=4),
                    w_.ap().rearrange("p (t i) -> p t i", i=4),
                    e_c.ap().rearrange("p (t o) -> p t o", o=1).broadcast_to([128, T, 4])))
                c.wait(v, tk[("e_dx", ibh)])
                t4 = c.inc("vector", v.tensor_mul(
                    kdec.ap().rearrange("p (t i) -> p t i", i=4),
                    w_.ap().rearrange("p (t i) -> p t i", i=4),
                    e_d.ap().rearrange("p (t o) -> p t o", o=1).broadcast_to([128, T, 4])))
                t5 = c.inc("vector", v.tensor_mul(qhA[:, :], qnh[:, :], e_r[:, :]))
                t6 = c.inc("vector", v.tensor_mul(qgn[:, :], qnh[:, :], e_g[:, :]))
                c.wait(v, t6)
                t7 = c.inc("vector", v.tensor_scalar(qgn[:, :], qgn[:, :], -1.0, None, ALU.mult))
                tk[("modv", ibh)] = [t2, t3, t4, t5, t7]

            @block.gpsimd
            def _(g, ibh=ibh):
                c.wait(g, tk[("exp1", ibh)])
                t1 = c.inc("gpsimd", g.tensor_mul(
                    kexp.ap().rearrange("p (t i) -> p t i", i=4),
                    kI.ap().rearrange("p (t i) -> p t i", i=4),
                    e_g.ap().rearrange("p (t o) -> p t o", o=1).broadcast_to([128, T, 4])))
                tk[("modg", ibh)] = [t1]

            for I in (1, 2, 3):
                @block.gpsimd
                def _(g, ibh=ibh, I=I):
                    c.wait(g, tk[("ecBx", ibh, I)], tk[("modv", ibh)])
                    tk[("modB", ibh, I)] = c.inc("gpsimd", g.tensor_mul(
                        whB[I].ap().rearrange("p (cc s i) -> p cc s i", s=16 * I, i=4),
                        w_.ap().rearrange("p (cc m) -> p cc m", m=256)[:, :, 0:64 * I]
                           .rearrange("p cc (s i) -> p cc s i", i=4),
                        ecB[I].ap().rearrange("p (cc s o) -> p cc s o", s=16 * I, o=1)
                              .broadcast_to([128, NCH, 16 * I, 4])))

            @block.vector
            def _(v, ibh=ibh):
                c.wait(v, tk[("in", ibh)], tk.get(("sup_last", ibh - 1)))
                t1 = c.inc("vector", v.memset(negS[:, :], 0.0))
                c.wait(v, t1)
                tk[("S0", ibh)] = c.inc("vector", v.tensor_copy(negSb[:, :], negS[:, :]))

            prep_tok = ibh

            for cc in range(NCH):
                mc = cc * 256
                tc = cc * 64

                @block.tensor
                def _(t, ibh=ibh, cc=cc, mc=mc, tc=tc):
                    c.wait(t, tk[("modv", ibh)], tk[("modg", ibh)],
                           [tk[("modB", ibh, I)] for I in (1, 2, 3)], tk["msk"])
                    c.wait(t, tk.get(("fevac", ibh, cc - 1)))
                    for j in range(4):
                        rt, base = j // 2, 64 * (j % 2)
                        t.matmul(pm[rt][base:base + 64, 64 * j:64 * j + 64],
                                 whA[:, mc + 64 * j: mc + 64 * j + 64],
                                 khA[:, mc + 64 * j: mc + 64 * j + 64], start=True, stop=True)
                        t.matmul(pm[rt][base:base + 64, 256 + 16 * j:256 + 16 * j + 16],
                                 whA[:, mc + 64 * j: mc + 64 * j + 64],
                                 qhA[:, tc + 16 * j: tc + 16 * j + 16], start=True, stop=True)
                    for I in (1, 2, 3):
                        wB = whB[I]
                        mb = cc * 64 * I
                        if I < 3:
                            t.matmul(pm[0][0:64 * I, 64 * I:64 * I + 64],
                                     wB[:, mb: mb + 64 * I],
                                     khA[:, mc + 64 * I: mc + 64 * I + 64], start=True, stop=True)
                            t.matmul(pm[0][0:64 * I, 256 + 16 * I:256 + 16 * I + 16],
                                     wB[:, mb: mb + 64 * I],
                                     qhA[:, tc + 16 * I: tc + 16 * I + 16], start=True, stop=True)
                        else:
                            t.matmul(pm[0][0:128, 192:256], wB[:, mb: mb + 128],
                                     khA[:, mc + 192: mc + 256], start=True, stop=True)
                            t.matmul(pm[1][0:64, 192:256], wB[:, mb + 128: mb + 192],
                                     khA[:, mc + 192: mc + 256], start=True, stop=True)
                            t.matmul(pm[0][0:128, 304:320], wB[:, mb: mb + 128],
                                     qhA[:, tc + 48: tc + 64], start=True, stop=True)
                            last = t.matmul(pm[1][0:64, 304:320], wB[:, mb + 128: mb + 192],
                                            qhA[:, tc + 48: tc + 64], start=True, stop=True)
                    tk[("fmm", ibh, cc)] = c.inc("tensor", last)

                @block.gpsimd
                def _(g, ibh=ibh, cc=cc):
                    c.wait(g, tk.get(("chain_done", ibh, cc - 1)))
                    tk[("msz", ibh, cc)] = c.inc("gpsimd", g.memset(msm[:, :], 0.0))

                @block.vector
                def _(v, ibh=ibh, cc=cc):
                    c.wait(v, tk[("fmm", ibh, cc)], tk[("msz", ibh, cc)])
                    ts = []
                    for j in range(4):
                        rt, base = j // 2, 64 * (j % 2)
                        ts.append(c.inc("vector", v.tensor_mul(
                            msm[base:base + 64, rt * 320 + 64 * j: rt * 320 + 64 * j + 64],
                            pm[rt][base:base + 64, 64 * j:64 * j + 64], mskK[base:base + 64, :])))
                        ts.append(c.inc("vector", v.tensor_mul(
                            msm[base:base + 64, rt * 320 + 256 + 16 * j: rt * 320 + 256 + 16 * j + 16],
                            pm[rt][base:base + 64, 256 + 16 * j:256 + 16 * j + 16],
                            mskQ[base:base + 64, :])))
                    tk[("evA", ibh, cc)] = ts

                @block.scalar
                def _(sc, ibh=ibh, cc=cc):
                    c.wait(sc, tk[("fmm", ibh, cc)], tk[("msz", ibh, cc)])
                    ts = []
                    ts.append(c.inc("scalar", sc.mul(msm[0:64, 64:256], pm[0][0:64, 64:256], -1.0)))
                    ts.append(c.inc("scalar", sc.mul(msm[64:128, 128:256], pm[0][64:128, 128:256], -1.0)))
                    ts.append(c.inc("scalar", sc.mul(msm[0:64, 320 + 192:320 + 256],
                                                     pm[1][0:64, 192:256], -1.0)))
                    ts.append(c.inc("scalar", sc.copy(msm[0:64, 272:320], pm[0][0:64, 272:320])))
                    ts.append(c.inc("scalar", sc.copy(msm[64:128, 288:320], pm[0][64:128, 288:320])))
                    ts.append(c.inc("scalar", sc.copy(msm[0:64, 320 + 304:320 + 320],
                                                      pm[1][0:64, 304:320])))
                    tk[("evB", ibh, cc)] = ts
                    tk[("fevac", ibh, cc)] = [tk[("evA", ibh, cc)], ts]

                @block.tensor
                def _(t, ibh=ibh, cc=cc, mc=mc):
                    c.wait(t, tk[("fevac", ibh, cc)], ctok)
                    c.wait(t, tk.get(("trev", ibh, cc - 1)))
                    t1 = c.inc("tensor", t.transpose(ps_tr[:, :], msm[:, 0:128], identb[:, :]))
                    tk[("tr0", ibh, cc)] = t1

                @block.vector
                def _(v, ibh=ibh, cc=cc):
                    c.wait(v, tk[("tr0", ibh, cc)])
                    tk[("trev0", ibh, cc)] = c.inc("vector", v.tensor_copy(a0s[0][:, :], ps_tr[:, :]))

                @block.tensor
                def _(t, ibh=ibh, cc=cc):
                    c.wait(t, tk[("trev0", ibh, cc)])
                    tk[("tr1", ibh, cc)] = c.inc("tensor", t.transpose(
                        ps_tr[:, :], msm[:, 320 + 128:320 + 256], identb[:, :]))

                @block.vector
                def _(v, ibh=ibh, cc=cc):
                    c.wait(v, tk[("tr1", ibh, cc)])
                    tk[("trev1", ibh, cc)] = c.inc("vector", v.tensor_copy(a0s[1][:, :], ps_tr[:, :]))

                @block.tensor
                def _(t, ibh=ibh, cc=cc, mc=mc):
                    c.wait(t, tk[("trev1", ibh, cc)], tk[("modv", ibh)])
                    tk[("tr2", ibh, cc)] = c.inc("tensor", t.transpose(
                        ps_tr[:, :], kdec[:, mc:mc + 128], identb[:, :]))

                @block.scalar
                def _(sc, ibh=ibh, cc=cc):
                    c.wait(sc, tk[("tr2", ibh, cc)], tk.get(("sup", ibh, cc - 1)))
                    tk[("trev2", ibh, cc)] = c.inc("scalar", sc.mul(kdT[:, 0:128], ps_tr[:, :], -1.0))

                @block.tensor
                def _(t, ibh=ibh, cc=cc, mc=mc):
                    c.wait(t, tk[("trev2", ibh, cc)])
                    tk[("tr3", ibh, cc)] = c.inc("tensor", t.transpose(
                        ps_tr[:, :], kdec[:, mc + 128:mc + 256], identb[:, :]))

                @block.scalar
                def _(sc, ibh=ibh, cc=cc):
                    c.wait(sc, tk[("tr3", ibh, cc)])
                    tk[("trev3", ibh, cc)] = c.inc("scalar", sc.mul(kdT[:, 128:256], ps_tr[:, :], -1.0))
                    tk[("trev", ibh, cc)] = [tk[("trev0", ibh, cc)], tk[("trev1", ibh, cc)],
                                             tk[("trev2", ibh, cc)], tk[("trev3", ibh, cc)]]

                @block.scalar
                def _(sc, ibh=ibh, cc=cc, b=b, h=h, mc=mc):
                    c.wait(sc, tk.get(("vcons", ibh, cc - 1)))
                    t0 = c.dinc(pfx + "vld", sc.dma_start(
                        out=vrow[:, 0:128], in_=vni_dr[b, h, :, mc:mc + 128], transpose=True))
                    t1 = c.dinc(pfx + "vld", sc.dma_start(
                        out=vrow[:, 128:256], in_=vni_dr[b, h, :, mc + 128:mc + 256], transpose=True))
                    tk[("vld", ibh, cc)] = Tok("D_" + pfx + "vld", c.cnt["D_" + pfx + "vld"])

                def chain_block(blk, ibh=ibh, cc=cc, mc=mc):
                    b0ap = msm[:, 0:128] if blk == 0 else msm[:, 320 + 128:320 + 256]
                    @block.tensor
                    def _(t, ibh=ibh, cc=cc, blk=blk, mc=mc, b0ap=b0ap):
                        c.wait(t, tk[("vld", ibh, cc)], tk[("S0", ibh)] if cc == 0 else None,
                               tk.get(("sbf", ibh, cc - 1)))
                        c.wait(t, tk.get(("uadd", ibh, cc - 1, blk, 6)))
                        if blk == 1:
                            c.wait(t, tk[("uadd", ibh, cc, 0, 6)])
                        t.matmul(pu[blk][:, :], identb[:, :], vrow[:, blk * 128:(blk + 1) * 128],
                                 start=True, stop=False)
                        last = t.matmul(pu[blk][:, :], kexp[:, mc + blk * 128: mc + (blk + 1) * 128],
                                        negSb[:, :], start=False, stop=(blk == 0))
                        if blk == 1:
                            last = t.matmul(pu[1][:, :], msm[:, 128:256], u1f[:, :],
                                            start=False, stop=True)
                        tk[("rhsmm", ibh, cc, blk)] = c.inc("tensor", last)

                    @block.vector
                    def _(v, ibh=ibh, cc=cc, blk=blk):
                        c.wait(v, tk[("rhsmm", ibh, cc, blk)])
                        tk[("rhsev", ibh, cc, blk)] = c.inc("vector", v.tensor_copy(
                            uB[0][:, :], pu[blk][:, :]))

                    for lvl in range(1, 7):
                        aprev = (aB[(lvl - 1) % 2][:, :] if lvl > 1 else a0s[blk][:, :])
                        bprev = (bL[lvl - 1][:, :] if lvl > 1 else b0ap)

                        @block.tensor
                        def _(t, ibh=ibh, cc=cc, blk=blk, lvl=lvl, aprev=aprev, bprev=bprev):
                            if lvl == 1:
                                c.wait(t, tk[("trev0" if blk == 0 else "trev1", ibh, cc)])
                                c.wait(t, tk.get(("chend", ibh, cc, blk - 1)) or
                                       tk.get(("chend", ibh, cc - 1, 1)) or
                                       tk.get(("chend_last", ibh - 1)))
                            else:
                                c.wait(t, tk[("chev_a", ibh, cc, blk, lvl - 1)],
                                       tk[("chev_b", ibh, cc, blk, lvl - 1)])
                            if lvl < 6:
                                t1 = t.matmul(psq[:, :], bprev, aprev, start=True, stop=True)
                            else:
                                t1 = t.matmul(psq[:, :], aprev, bprev, start=True, stop=True)
                            tk[("chmm_a", ibh, cc, blk, lvl)] = c.inc("tensor", t1)

                        @block.vector
                        def _(v, ibh=ibh, cc=cc, blk=blk, lvl=lvl):
                            c.wait(v, tk[("chmm_a", ibh, cc, blk, lvl)])
                            dst = aB[lvl % 2] if lvl < 6 else bL[6]
                            tk[("chev_a", ibh, cc, blk, lvl)] = c.inc("vector", v.tensor_copy(
                                dst[:, :], psq[:, :]))

                        if lvl < 6:
                            @block.tensor
                            def _(t, ibh=ibh, cc=cc, blk=blk, lvl=lvl, aprev=aprev, bprev=bprev):
                                c.wait(t, tk[("chev_a", ibh, cc, blk, lvl)])
                                t1 = t.matmul(psq[:, :], aprev, bprev, start=True, stop=True)
                                tk[("chmm_b", ibh, cc, blk, lvl)] = c.inc("tensor", t1)

                            @block.vector
                            def _(v, ibh=ibh, cc=cc, blk=blk, lvl=lvl):
                                c.wait(v, tk[("chmm_b", ibh, cc, blk, lvl)])
                                tk[("chev_b", ibh, cc, blk, lvl)] = c.inc("vector", v.tensor_copy(
                                    bL[lvl][:, :], psq[:, :]))
                        else:
                            tk[("chev_b", ibh, cc, blk, lvl)] = tk[("chev_a", ibh, cc, blk, lvl)]
                            tk[("chend", ibh, cc, blk)] = tk[("chev_a", ibh, cc, blk, lvl)]
                            if cc == NCH - 1 and blk == 1:
                                tk[("chend_last", ibh)] = tk[("chev_a", ibh, cc, blk, lvl)]

                    for ai, lvl in enumerate(range(6, -1, -1)):
                        @block.tensor
                        def _(t, ibh=ibh, cc=cc, blk=blk, lvl=lvl, ai=ai, b0ap=b0ap):
                            bl = b0ap if lvl == 0 else bL[lvl][:, :]
                            ucur = uB[ai % 2][:, :]
                            if ai == 0:
                                c.wait(t, tk[("rhsev", ibh, cc, blk)],
                                       tk[("chev_a", ibh, cc, blk, 6)])
                            else:
                                c.wait(t, tk[("uadd", ibh, cc, blk, ai - 1)])
                                if lvl >= 1:
                                    c.wait(t, tk[("chev_b", ibh, cc, blk, lvl)])
                            tk[("apmm", ibh, cc, blk, ai)] = c.inc("tensor", t.matmul(
                                pu[blk][:, :], bl, ucur, start=True, stop=True))

                        @block.vector
                        def _(v, ibh=ibh, cc=cc, blk=blk, ai=ai):
                            c.wait(v, tk[("apmm", ibh, cc, blk, ai)])
                            unxt = uB[(ai + 1) % 2] if ai < 6 else (u1f if blk == 0 else uB[1])
                            ucur = uB[ai % 2]
                            tk[("uadd", ibh, cc, blk, ai)] = c.inc("vector", v.tensor_add(
                                unxt[:, :], pu[blk][:, :], ucur[:, :]))

                chain_block(0)
                chain_block(1)

                @block.tensor
                def _(t, ibh=ibh, cc=cc, tc=tc):
                    c.wait(t, tk[("uadd", ibh, cc, 0, 6)], tk[("uadd", ibh, cc, 1, 6)])
                    c.wait(t, tk.get(("oev", ibh, cc - 1)))
                    t.matmul(po[:, :], msm[0:128, 256:320], u1f[:, :], start=True, stop=False)
                    t.matmul(po[:, :], msm[0:128, 320 + 256:320 + 320], uB[1][:, :],
                             start=False, stop=False)
                    last = t.matmul(po[:, :], qgn[:, tc:tc + 64], negSb[:, :],
                                    start=False, stop=True)
                    tk[("omm", ibh, cc)] = c.inc("tensor", last)

                @block.scalar
                def _(sc, ibh=ibh, cc=cc):
                    c.wait(sc, tk[("omm", ibh, cc)])
                    if cc <= 1:
                        c.wait(sc, tk.get(("ost", ibh - 1)))
                    tk[("oev", ibh, cc)] = c.inc("scalar", sc.mul(
                        obh[64 * (cc % 2):64 * (cc % 2) + 64, (cc // 2) * 128:(cc // 2 + 1) * 128],
                        po[:, :], MKSCALE))

                @block.tensor
                def _(t, ibh=ibh, cc=cc):
                    c.wait(t, tk[("trev2", ibh, cc)], tk[("trev3", ibh, cc)],
                           tk[("uadd", ibh, cc, 0, 6)], tk[("uadd", ibh, cc, 1, 6)],
                           tk[("omm", ibh, cc)])
                    c.wait(t, tk.get(("sup", ibh, cc - 1)))
                    t.matmul(psS[:, :], kdT[:, 0:128], u1f[:, :], start=True, stop=False)
                    last = t.matmul(psS[:, :], kdT[:, 128:256], uB[1][:, :],
                                    start=False, stop=True)
                    tk[("smm", ibh, cc)] = c.inc("tensor", last)

                @block.vector
                def _(v, ibh=ibh, cc=cc):
                    c.wait(v, tk[("smm", ibh, cc)])
                    tk[("sup", ibh, cc)] = c.inc("vector", v.scalar_tensor_tensor(
                        negS[:, :], negS[:, :], glr[:, NCH + cc:NCH + cc + 1], psS[:, :],
                        ALU.mult, ALU.add))

                @block.scalar
                def _(sc, ibh=ibh, cc=cc):
                    c.wait(sc, tk[("sup", ibh, cc)])
                    tk[("sbf", ibh, cc)] = c.inc("scalar", sc.copy(negSb[:, :], negS[:, :]))
                    if cc == NCH - 1:
                        tk[("sup_last", ibh)] = tk[("sbf", ibh, cc)]

                tk[("vcons", ibh, cc)] = [tk[("rhsmm", ibh, cc, 0)], tk[("rhsmm", ibh, cc, 1)]]
                tk[("chain_done", ibh, cc)] = [tk[("uadd", ibh, cc, 1, 6)], tk[("omm", ibh, cc)],
                                               tk[("smm", ibh, cc)]]
                tk[("trev", ibh, cc)] = tk[("trev", ibh, cc)]

            @block.sync
            def _(sy, ibh=ibh, b=b, h=h):
                c.wait(sy, [tk[("oev", ibh, cc)] for cc in range(NCH)])
                tk[("ost", ibh)] = c.dinc(pfx + "ost", sy.dma_start(
                    out=o_dr[b, h].rearrange("(cc p) dv -> p cc dv", p=128), in_=obh.ap()
                    .rearrange("p (cc dv) -> p cc dv", dv=128)))

        @block.sync
        def _(sy):
            c.wait(sy, tk[("ost", len(bh_list) - 1)])
    return



def emit_l0out(nc, c, st, pfx, o_dr, gate_dr, onR_dr, obarT_dr, identb, cmap, ctok):
    NH = 16
    with contextlib.ExitStack() as stA:
        obh = stA.enter_context(nc.sbuf_tensor(pfx + "obh", [128, 16 * 128], BF16))
        grow = stA.enter_context(nc.sbuf_tensor(pfx + "grow", [128, 16 * 128], BF16))
        sq = stA.enter_context(nc.sbuf_tensor(pfx + "sq", [128, 2048], F32))
        ssb = stA.enter_context(nc.sbuf_tensor(pfx + "ssb", [128, 3 * 16], F32))
        onR = stA.enter_context(nc.sbuf_tensor(pfx + "onR", [128, 128], BF16))
        obT = stA.enter_context(nc.sbuf_tensor(pfx + "obT", [128, 16 * 128], BF16))
        ps_tr = stA.enter_context(nc.psum_tensor(pfx + "pstr", [128, 128], BF16))

        tk = {}
        with nc.Block() as block:
            @block.gpsimd
            def _(g):
                tk["onR"] = c.dinc(pfx + "onR", g.dma_start(out=onR[:, :], in_=onR_dr[:, :]))

            for ibh in range(2 * NH):
                b, h = ibh // NH, ibh % NH

                @block.gpsimd
                def _(g, b=b, h=h, ibh=ibh):
                    c.wait(g, tk.get(("trl", ibh - 1)))
                    tk[("lo", ibh)] = c.dinc(pfx + "lo", g.dma_start(
                        out=obh.ap().rearrange("p (cc dv) -> p cc dv", dv=128),
                        in_=o_dr[b, h].rearrange("(cc p) dv -> p cc dv", p=128)))

                @block.scalar
                def _(sc, b=b, h=h, ibh=ibh):
                    c.wait(sc, tk.get(("gated", ibh - 1)))
                    ts = []
                    for j in range(16):
                        ts.append(c.dinc(pfx + "lg", sc.dma_start(
                            out=grow[:, j * 128:(j + 1) * 128],
                            in_=gate_dr[h * 128:(h + 1) * 128,
                                        b * T + j * 128: b * T + (j + 1) * 128],
                            transpose=True)))
                    tk[("lg", ibh)] = Tok("D_" + pfx + "lg", c.cnt["D_" + pfx + "lg"])

                @block.scalar
                def _(sc, ibh=ibh):
                    c.wait(sc, tk[("lo", ibh)])
                    c.wait(sc, tk.get(("rstd", ibh - 1)))
                    tk[("sq", ibh)] = c.inc("scalar", sc.activation(sq[:, :], obh[:, :], AF.Square))

                @block.vector
                def _(v, ibh=ibh):
                    c.wait(v, tk[("sq", ibh)])
                    for j in range(16):
                        c.inc("vector", v.reduce_sum(ssb[:, j:j + 1], sq[:, j * 128:(j + 1) * 128], AX.X))
                    tk[("ss", ibh)] = c.mark("vector")

                @block.scalar
                def _(sc, ibh=ibh):
                    c.wait(sc, tk[("ss", ibh)])
                    tk[("srt", ibh)] = c.inc("scalar", sc.activation(
                        ssb[:, 16:32], ssb[:, 0:16], AF.Sqrt, scale=1.0 / 128.0, bias=cmap[1e-5]))

                @block.vector
                def _(v, ibh=ibh):
                    c.wait(v, tk[("srt", ibh)])
                    tk[("rstd", ibh)] = c.inc("vector", v.reciprocal(ssb[:, 32:48], ssb[:, 16:32]))

                @block.scalar
                def _(sc, ibh=ibh):
                    c.wait(sc, tk[("lg", ibh)])
                    c.wait(sc, tk.get(("gated", ibh - 1)))
                    tk[("sig", ibh)] = c.inc("scalar", sc.activation(grow[:, :], grow[:, :], AF.Sigmoid))

                @block.vector
                def _(v, ibh=ibh):
                    c.wait(v, tk[("rstd", ibh)], tk[("sig", ibh)], tk["onR"])
                    c.wait(v, tk.get(("trl", ibh - 1)))
                    for j in range(16):
                        t1 = c.inc("vector", v.tensor_scalar(
                            obh[:, j * 128:(j + 1) * 128], obh[:, j * 128:(j + 1) * 128],
                            ssb[:, 32 + j:33 + j], None, ALU.mult))
                        c.wait(v, t1)
                        t2 = c.inc("vector", v.tensor_mul(
                            obh[:, j * 128:(j + 1) * 128], obh[:, j * 128:(j + 1) * 128], onR[:, :]))
                        c.wait(v, t2)
                        c.inc("vector", v.tensor_mul(
                            obh[:, j * 128:(j + 1) * 128], obh[:, j * 128:(j + 1) * 128],
                            grow[:, j * 128:(j + 1) * 128]))
                    tk[("gated", ibh)] = c.mark("vector")

                for j in range(16):
                    @block.tensor
                    def _(t, ibh=ibh, j=j):
                        c.wait(t, tk[("gated", ibh)], ctok)
                        c.wait(t, tk.get(("trev", ibh, j - 1)) or tk.get(("trev", ibh - 1, 15)))
                        tk[("tr", ibh, j)] = c.inc("tensor", t.transpose(
                            ps_tr[:, :], obh[:, j * 128:(j + 1) * 128], identb[:, :]))
                        if j == 15:
                            tk[("trl", ibh)] = tk[("tr", ibh, j)]

                    @block.vector
                    def _(v, ibh=ibh, j=j):
                        c.wait(v, tk[("tr", ibh, j)])
                        if j == 0:
                            c.wait(v, tk.get(("st", ibh - 1)))
                        tk[("trev", ibh, j)] = c.inc("vector", v.tensor_copy(
                            obT[:, j * 128:(j + 1) * 128], ps_tr[:, :]))

                @block.sync
                def _(sy, b=b, h=h, ibh=ibh):
                    c.wait(sy, [tk[("trev", ibh, j)] for j in range(16)])
                    tk[("st", ibh)] = c.dinc(pfx + "st", sy.dma_start(
                        out=obarT_dr[h * 128:(h + 1) * 128, b * T:(b + 1) * T], in_=obT[:, :]))

            @block.sync
            def _(sy):
                c.wait(sy, tk[("st", 2 * NH - 1)])
    return



def _newnc():
    return bass.Bass("TRN2", target_bir_lowering=False, debug=False)


def _din(nc, name, shape, dt):
    return nc.dram_tensor(name, shape, dt, kind="ExternalInput").ap()


def _dout(nc, name, shape, dt):
    return nc.dram_tensor(name, shape, dt, kind="ExternalOutput").ap()


def _run(nc, in_maps):
    return run_bass_kernel_spmd(nc, in_maps, core_ids=list(range(8)))


def _ln_phase(in_maps, out32=False):
    nc = _newnc()
    x = _din(nc, "x", [D, NTOK], F32)
    o = _dout(nc, "o", [D, NTOK], BF16)
    o32 = _dout(nc, "o32", [D, NTOK], F32) if out32 else None
    c = Ctx(nc)
    with contextlib.ExitStack() as st:
        c.make_sems(st)
        onescol, onesrow, ident, cmap, ctok = emit_consts(nc, c, st, values=[1e-5, 1.0, 1e-24])
        emit_ln(nc, c, st, "ln", x, o, onescol, onesrow, cmap, ctok, out32_dr=o32)
    return _run(nc, in_maps)


_DBG = {}


def kernel_unfused(**inputs):
    ins = {k: np.asarray(v) for k, v in inputs.items()}
    x0 = ins["L0_attn_hidden_states"]

    TT = lambda a: np.ascontiguousarray(np.asarray(a).T)
    wqT = TT(ins["L0_attn_Wq"]); wkT = TT(ins["L0_attn_Wk"]); wvT = TT(ins["L0_attn_Wv"])
    wf1T = TT(ins["L0_attn_Wf1"]); wf2T = TT(ins["L0_attn_Wf2"]); wbT = TT(ins["L0_attn_Wb"])
    wg1T = TT(ins["L0_attn_Wg1"]); wg2T = TT(ins["L0_attn_Wg2"]); woT = TT(ins["L0_attn_Wo"])
    dtbT = np.ascontiguousarray(ins["L0_attn_dt_bias"].reshape(16, 128).T)
    bg2T = np.ascontiguousarray(ins["L0_attn_bg2"].reshape(16, 128).T)
    bg1 = np.ascontiguousarray(ins["L0_attn_bg1"])
    onR = np.ascontiguousarray(np.tile(ins["L0_attn_onorm_w"][None, :], (128, 1)))
    sgT = np.ascontiguousarray(ins["L0_ffn_s_gate"].reshape(64, 128).T)
    suT = np.ascontiguousarray(ins["L0_ffn_s_up"].reshape(64, 128).T)
    sdT = np.ascontiguousarray(ins["L0_ffn_s_down"].reshape(16, 128).T)
    wqkvT = TT(ins["L1_attn_w_qkv"]); wo1T = TT(ins["L1_attn_w_o"])
    fc1bT = np.ascontiguousarray(ins["L1_ffn_fc1_b"].reshape(8, 16, 128).transpose(0, 2, 1))
    alphaR = np.ascontiguousarray(np.tile(ins["L1_ffn_alpha"][None, :], (128, 1)))
    p_ = np.arange(64)
    mskK = np.tile(np.where(np.arange(64)[None, :] > p_[:, None], -1.0, 0.0), (2, 1)) \
        .astype(ml_dtypes.bfloat16)
    mskQ = np.tile(np.where(np.arange(16)[None, :] >= (p_ // 4)[:, None], 1.0, 0.0), (2, 1)) \
        .astype(ml_dtypes.bfloat16)

    shared = dict(wqT=wqT, wkT=wkT, wvT=wvT, wf1T=wf1T, wf2T=wf2T, wbT=wbT,
                  wg1T=wg1T, wg2T=wg2T, woT=woT, dtbT=dtbT, bg2T=bg2T, bg1=bg1,
                  onR=onR, sgT=sgT, suT=suT, sdT=sdT, wqkvT=wqkvT, wo1T=wo1T,
                  fc1bT=fc1bT, alphaR=alphaR, mskK=mskK, mskQ=mskQ,
                  convq=ins["L0_attn_conv_q"], convk=ins["L0_attn_conv_k"],
                  convv=ins["L0_attn_conv_v"], alog=ins["L0_attn_A_log"],
                  wgate=ins["L0_ffn_w_gate"], wup=ins["L0_ffn_w_up"],
                  wdown=ins["L0_ffn_w_down"], gatew=ins["L1_ffn_gate_w"],
                  fc1=ins["L1_ffn_fc1_w"], fc2=ins["L1_ffn_fc2_w"],
                  fc2b=ins["L1_ffn_fc2_b"])

    xT = [np.ascontiguousarray(
        x0[2 * cid:2 * cid + 2].transpose(2, 0, 1).reshape(D, NTOK)) for cid in range(8)]

    st_ = {}

    res = _ln_phase([{"x": xT[i]} for i in range(8)])
    ln0 = [r["o"] for r in res.results]

    nc = _newnc()
    a = {"xln": _din(nc, "xln", [D, NTOK], BF16),
         "wqT": _din(nc, "wqT", [2048, 2048], F32), "wkT": _din(nc, "wkT", [2048, 8192], F32),
         "wvT": _din(nc, "wvT", [2048, 8192], F32),
         "convq": _din(nc, "convq", [2048, 4], F32), "convk": _din(nc, "convk", [8192, 4], F32),
         "convv": _din(nc, "convv", [8192, 4], F32),
         "wf1T": _din(nc, "wf1T", [2048, 128], F32), "wf2T": _din(nc, "wf2T", [128, 2048], F32),
         "dtbT": _din(nc, "dtbT", [128, 16], F32), "alog": _din(nc, "alog", [16], F32),
         "wbT": _din(nc, "wbT", [2048, 64], F32), "wg1T": _din(nc, "wg1T", [2048, 128], F32),
         "bg1": _din(nc, "bg1", [128], F32), "wg2T": _din(nc, "wg2T", [128, 2048], F32),
         "bg2T": _din(nc, "bg2T", [128, 16], F32)}
    o = {"qn": _dout(nc, "qn", [16, 128, NTOK], BF16),
         "kni": _dout(nc, "kni", [2, 16, 128, 8192], BF16),
         "vni": _dout(nc, "vni", [2, 16, 128, 8192], BF16),
         "la": _dout(nc, "la", [16, 128, NTOK], F32),
         "beta": _dout(nc, "beta", [64, NTOK], BF16),
         "gate": _dout(nc, "gate", [D, NTOK], BF16)}
    c = Ctx(nc)
    with contextlib.ExitStack() as stk:
        c.make_sems(stk)
        onescol, onesrow, ident, cmap, ctok = emit_consts(nc, c, stk, values=[1e-5, 1.0, 1e-24])
        emit_l0proj(nc, c, stk, "p2", a["xln"], a["wqT"], a["wkT"], a["wvT"],
                    a["convq"], a["convk"], a["convv"], a["wf1T"], a["wf2T"],
                    a["dtbT"], a["alog"], a["wbT"], a["wg1T"], a["bg1"], a["wg2T"], a["bg2T"],
                    o["qn"], o["kni"], o["vni"], o["la"], o["beta"], o["gate"],
                    onescol, onesrow, cmap, ctok)
    res = _run(nc, [{"xln": ln0[i], "wqT": wqT, "wkT": wkT, "wvT": wvT,
                     "convq": shared["convq"], "convk": shared["convk"], "convv": shared["convv"],
                     "wf1T": wf1T, "wf2T": wf2T, "dtbT": dtbT, "alog": shared["alog"],
                     "wbT": wbT, "wg1T": wg1T, "bg1": bg1, "wg2T": wg2T, "bg2T": bg2T}
                    for i in range(8)])
    qn = [r["qn"] for r in res.results]; kni = [r["kni"] for r in res.results]
    vni = [r["vni"] for r in res.results]; la = [r["la"] for r in res.results]
    beta = [r["beta"] for r in res.results]; gate = [r["gate"] for r in res.results]

    nc = _newnc()
    a = {"qn": _din(nc, "qn", [16, 128, NTOK], BF16),
         "kni": _din(nc, "kni", [2, 16, 128, 8192], BF16),
         "vni": _din(nc, "vni", [2, 16, 128, 8192], BF16),
         "la": _din(nc, "la", [16, 128, NTOK], F32),
         "beta": _din(nc, "beta", [64, NTOK], BF16),
         "mskK": _din(nc, "mskK", [128, 64], BF16),
         "mskQ": _din(nc, "mskQ", [128, 16], BF16)}
    od = _dout(nc, "o", [2, 16, 2048, 128], BF16)
    c = Ctx(nc)
    with contextlib.ExitStack() as stk:
        c.make_sems(stk)
        onescol, onesrow, ident, cmap, ctok = emit_consts(nc, c, stk, values=[1e-5, 1.0, 1e-24])
        emit_mkda(nc, c, stk, "mk", a["qn"], a["kni"], a["vni"], a["la"], a["beta"], od,
                  a["mskK"], a["mskQ"], onesrow, ident, ctok)
    res = _run(nc, [{"qn": qn[i], "kni": kni[i], "vni": vni[i], "la": la[i],
                     "beta": beta[i], "mskK": mskK, "mskQ": mskQ} for i in range(8)])
    o_scr = [r["o"] for r in res.results]

    nc = _newnc()
    a = {"o": _din(nc, "o", [2, 16, 2048, 128], BF16),
         "gate": _din(nc, "gate", [D, NTOK], BF16),
         "onR": _din(nc, "onR", [128, 128], F32),
         "woT": _din(nc, "woT", [2048, 2048], F32),
         "x0": _din(nc, "x0", [D, NTOK], F32)}
    obarT = nc.dram_tensor("obarT", [2048, NTOK], BF16, kind="Internal").ap()
    x1d = _dout(nc, "x1", [D, NTOK], F32)
    c = Ctx(nc)
    with contextlib.ExitStack() as stk:
        c.make_sems(stk)
        onescol, onesrow, ident, cmap, ctok = emit_consts(nc, c, stk, values=[1e-5, 1.0, 1e-24])
        emit_l0out(nc, c, stk, "p4", a["o"], a["gate"], a["onR"], obarT, ident, cmap, ctok)
        emit_barrier(nc, c)
        emit_gemm(nc, c, stk, "p4g", a["woT"], 2048, 2048, obarT, x1d, residual_dr=a["x0"])
    res = _run(nc, [{"o": o_scr[i], "gate": gate[i], "onR": onR, "woT": woT,
                     "x0": xT[i]} for i in range(8)])
    x1 = [r["x1"] for r in res.results]
    _DBG["x1"] = x1; _DBG["ln0"] = ln0; _DBG["o_scr"] = o_scr; _DBG["gate"] = gate; _DBG["qn"] = qn; _DBG["kni"] = kni; _DBG["vni"] = vni; _DBG["la"] = la; _DBG["beta"] = beta

    res = _ln_phase([{"x": x1[i]} for i in range(8)])
    ln1 = [r["o"] for r in res.results]
    nc = _newnc()
    a = {"xln": _din(nc, "xln", [D, NTOK], BF16), "xres": _din(nc, "xres", [D, NTOK], F32),
         "wg": _din(nc, "wg", [2048, 8192], I8), "wu": _din(nc, "wu", [2048, 8192], I8),
         "wd": _din(nc, "wd", [8192, 2048], I8),
         "sgT": _din(nc, "sgT", [128, 64], F32), "suT": _din(nc, "suT", [128, 64], F32),
         "sdT": _din(nc, "sdT", [128, 16], F32)}
    out2 = _dout(nc, "out", [D, NTOK], F32)
    fuse = nc.dram_tensor("fuse", [8192, NTOK], BF16, kind="Internal").ap()
    c = Ctx(nc)
    with contextlib.ExitStack() as stk:
        c.make_sems(stk)
        emit_ffn0(nc, c, stk, "f0", a["xln"], a["wg"], a["wu"], a["wd"],
                  a["sgT"], a["suT"], a["sdT"], a["xres"], out2, fuse)
    res = _run(nc, [{"xln": ln1[i], "xres": x1[i], "wg": shared["wgate"],
                     "wu": shared["wup"], "wd": shared["wdown"],
                     "sgT": sgT, "suT": suT, "sdT": sdT} for i in range(8)])
    x2 = [r["out"] for r in res.results]
    _DBG["x2"] = x2

    res = _ln_phase([{"x": x2[i]} for i in range(8)])
    ln2 = [r["o"] for r in res.results]
    nc = _newnc()
    a = {"xln": _din(nc, "xln", [D, NTOK], BF16),
         "wqkvT": _din(nc, "wqkvT", [2048, 6144], F32)}
    qkvd = _dout(nc, "qkv", [6144, NTOK], BF16)
    c = Ctx(nc)
    with contextlib.ExitStack() as stk:
        c.make_sems(stk)
        emit_gemm(nc, c, stk, "p6", a["wqkvT"], 2048, 6144, a["xln"], qkvd,
                  scale_fn=lambda mt: (float(128.0 ** -0.5) if mt < 16 else 1.0))
    res = _run(nc, [{"xln": ln2[i], "wqkvT": wqkvT} for i in range(8)])
    qkv = [r["qkv"] for r in res.results]

    nc = _newnc()
    a = {"qkv": _din(nc, "qkv", [6144, NTOK], BF16)}
    attnd = _dout(nc, "attn", [2048, NTOK], BF16)
    c = Ctx(nc)
    with contextlib.ExitStack() as stk:
        c.make_sems(stk)
        onescol, onesrow, ident, cmap, ctok = emit_consts(nc, c, stk, values=[1e-5, 1.0, 1e-24])
        emit_attn(nc, c, stk, "at", a["qkv"], attnd, ident, ctok)
    res = _run(nc, [{"qkv": qkv[i]} for i in range(8)])
    attn = [r["attn"] for r in res.results]

    nc = _newnc()
    a = {"attn": _din(nc, "attn", [2048, NTOK], BF16),
         "wo1T": _din(nc, "wo1T", [2048, 2048], F32),
         "x2": _din(nc, "x2", [D, NTOK], F32)}
    x3d = _dout(nc, "x3", [D, NTOK], F32)
    c = Ctx(nc)
    with contextlib.ExitStack() as stk:
        c.make_sems(stk)
        emit_gemm(nc, c, stk, "p8", a["wo1T"], 2048, 2048, a["attn"], x3d, residual_dr=a["x2"])
    res = _run(nc, [{"attn": attn[i], "wo1T": wo1T, "x2": x2[i]} for i in range(8)])
    x3 = [r["x3"] for r in res.results]
    _DBG["x3"] = x3; _DBG["qkv"] = qkv; _DBG["attn"] = attn

    res = _ln_phase([{"x": x3[i]} for i in range(8)], out32=True)
    ln3 = [r["o"] for r in res.results]
    ln3_32 = [r["o32"] for r in res.results]
    nc = _newnc()
    a = {"xln": _din(nc, "xln", [D, NTOK], BF16), "xln32": _din(nc, "xln32", [D, NTOK], F32),
         "xres": _din(nc, "xres", [D, NTOK], F32),
         "gw": _din(nc, "gw", [2048, 8], F32), "fc1": _din(nc, "fc1", [8, 2048, 2048], F32),
         "fc1bT": _din(nc, "fc1bT", [8, 128, 16], F32),
         "fc2": _din(nc, "fc2", [8, 2048, 2048], F32), "fc2b": _din(nc, "fc2b", [8, 2048], F32),
         "alphaR": _din(nc, "alphaR", [128, 8], F32)}
    outd = _dout(nc, "out", [D, NTOK], F32)
    c = Ctx(nc)
    with contextlib.ExitStack() as stk:
        c.make_sems(stk)
        onescol, onesrow, ident, cmap, ctok = emit_consts(nc, c, stk, values=[1e-5, 1.0, 1e-24])
        emit_moe(nc, c, stk, "mo", a["xln"], a["xln32"], a["gw"], a["fc1"], a["fc1bT"],
                 a["fc2"], a["fc2b"], a["alphaR"], a["xres"], outd, onesrow, ident, ctok)
    res = _run(nc, [{"xln": ln3[i], "xln32": ln3_32[i], "xres": x3[i],
                     "gw": shared["gatew"], "fc1": shared["fc1"], "fc1bT": fc1bT,
                     "fc2": shared["fc2"], "fc2b": shared["fc2b"], "alphaR": alphaR}
                    for i in range(8)])
    outT = [r["out"] for r in res.results]

    full = np.empty((16, 2048, 2048), np.float32)
    for i in range(8):
        full[2 * i:2 * i + 2] = outT[i].reshape(D, 2, T).transpose(1, 2, 0)
    return full



def _build_fused():
    nc = _newnc()
    a = {}
    a["x0"] = _din(nc, "x0", [D, NTOK], F32)
    a["wqT"] = _din(nc, "wqT", [2048, 2048], F32)
    a["wkT"] = _din(nc, "wkT", [2048, 8192], F32)
    a["wvT"] = _din(nc, "wvT", [2048, 8192], F32)
    a["convq"] = _din(nc, "convq", [2048, 4], F32)
    a["convk"] = _din(nc, "convk", [8192, 4], F32)
    a["convv"] = _din(nc, "convv", [8192, 4], F32)
    a["wf1T"] = _din(nc, "wf1T", [2048, 128], F32)
    a["wf2T"] = _din(nc, "wf2T", [128, 2048], F32)
    a["dtbT"] = _din(nc, "dtbT", [128, 16], F32)
    a["alog"] = _din(nc, "alog", [16], F32)
    a["wbT"] = _din(nc, "wbT", [2048, 64], F32)
    a["wg1T"] = _din(nc, "wg1T", [2048, 128], F32)
    a["bg1"] = _din(nc, "bg1", [128], F32)
    a["wg2T"] = _din(nc, "wg2T", [128, 2048], F32)
    a["bg2T"] = _din(nc, "bg2T", [128, 16], F32)
    a["onR"] = _din(nc, "onR", [128, 128], F32)
    a["woT"] = _din(nc, "woT", [2048, 2048], F32)
    a["mskK"] = _din(nc, "mskK", [128, 64], BF16)
    a["mskQ"] = _din(nc, "mskQ", [128, 16], BF16)
    a["wg"] = _din(nc, "wg", [2048, 8192], I8)
    a["wu"] = _din(nc, "wu", [2048, 8192], I8)
    a["wd"] = _din(nc, "wd", [8192, 2048], I8)
    a["sgT"] = _din(nc, "sgT", [128, 64], F32)
    a["suT"] = _din(nc, "suT", [128, 64], F32)
    a["sdT"] = _din(nc, "sdT", [128, 16], F32)
    a["wqkvT"] = _din(nc, "wqkvT", [2048, 6144], F32)
    a["wo1T"] = _din(nc, "wo1T", [2048, 2048], F32)
    a["gw"] = _din(nc, "gw", [2048, 8], F32)
    a["fc1"] = _din(nc, "fc1", [8, 2048, 2048], F32)
    a["fc1bT"] = _din(nc, "fc1bT", [8, 128, 16], F32)
    a["fc2"] = _din(nc, "fc2", [8, 2048, 2048], F32)
    a["fc2b"] = _din(nc, "fc2b", [8, 2048], F32)
    a["alphaR"] = _din(nc, "alphaR", [128, 8], F32)
    out = _dout(nc, "out", [D, NTOK], F32)

    def scr(name, shape, dt):
        return nc.dram_tensor(name, shape, dt, kind="Internal").ap()
    ln0 = scr("s_ln0", [D, NTOK], BF16)
    qn = scr("s_qn", [16, 128, NTOK], BF16)
    kni = scr("s_kni", [2, 16, 128, 8192], BF16)
    vni = scr("s_vni", [2, 16, 128, 8192], BF16)
    la = scr("s_la", [16, 128, NTOK], F32)
    beta = scr("s_beta", [64, NTOK], BF16)
    gate = scr("s_gate", [D, NTOK], BF16)
    o_scr = scr("s_o", [2, 16, 2048, 128], BF16)
    obarT = scr("s_obarT", [2048, NTOK], BF16)
    x1 = scr("s_x1", [D, NTOK], F32)
    ln1 = scr("s_ln1", [D, NTOK], BF16)
    fuse = scr("s_fuse", [8192, NTOK], BF16)
    x2 = scr("s_x2", [D, NTOK], F32)
    ln2 = scr("s_ln2", [D, NTOK], BF16)
    qkv = scr("s_qkv", [6144, NTOK], BF16)
    attn = scr("s_attn", [2048, NTOK], BF16)
    x3 = scr("s_x3", [D, NTOK], F32)
    ln3 = scr("s_ln3", [D, NTOK], BF16)
    ln3_32 = scr("s_ln3_32", [D, NTOK], F32)

    c = Ctx(nc)
    with contextlib.ExitStack() as st:
        c.make_sems(st)
        onescol, onesrow, ident, cmap, ctok = emit_consts(nc, c, st, values=[1e-5, 1.0, 1e-24])

        with contextlib.ExitStack() as ph:
            emit_ln(nc, c, ph, "lnA", a["x0"], ln0, onescol, onesrow, cmap, ctok)
        emit_barrier(nc, c)
        with contextlib.ExitStack() as ph:
            emit_l0proj(nc, c, ph, "p2", ln0, a["wqT"], a["wkT"], a["wvT"],
                        a["convq"], a["convk"], a["convv"], a["wf1T"], a["wf2T"],
                        a["dtbT"], a["alog"], a["wbT"], a["wg1T"], a["bg1"], a["wg2T"],
                        a["bg2T"], qn, kni, vni, la, beta, gate,
                        onescol, onesrow, cmap, ctok, identb=ident)
        emit_barrier(nc, c)
        with contextlib.ExitStack() as ph:
            emit_mkda3(nc, c, ph, "mk", qn, kni, vni, la, beta, o_scr,
                       a["mskK"], a["mskQ"], onesrow, ident, ctok)
        emit_barrier(nc, c)
        with contextlib.ExitStack() as ph:
            emit_l0out(nc, c, ph, "p4", o_scr, gate, a["onR"], obarT, ident, cmap, ctok)
        emit_barrier(nc, c)
        with contextlib.ExitStack() as ph:
            emit_gemm(nc, c, ph, "p4g", a["woT"], 2048, 2048, obarT, x1, residual_dr=a["x0"])
        emit_barrier(nc, c)
        with contextlib.ExitStack() as ph:
            emit_ln(nc, c, ph, "lnB", x1, ln1, onescol, onesrow, cmap, ctok)
        emit_barrier(nc, c)
        with contextlib.ExitStack() as ph:
            emit_ffn0(nc, c, ph, "f0", ln1, a["wg"], a["wu"], a["wd"],
                      a["sgT"], a["suT"], a["sdT"], x1, x2, fuse)
        emit_barrier(nc, c)
        with contextlib.ExitStack() as ph:
            emit_ln(nc, c, ph, "lnC", x2, ln2, onescol, onesrow, cmap, ctok)
        emit_barrier(nc, c)
        with contextlib.ExitStack() as ph:
            emit_gemm(nc, c, ph, "p6", a["wqkvT"], 2048, 6144, ln2, qkv,
                      scale_fn=lambda mt: (float(128.0 ** -0.5) if mt < 16 else 1.0))
        emit_barrier(nc, c)
        with contextlib.ExitStack() as ph:
            emit_attn(nc, c, ph, "at", qkv, attn, ident, ctok)
        emit_barrier(nc, c)
        with contextlib.ExitStack() as ph:
            emit_gemm(nc, c, ph, "p8", a["wo1T"], 2048, 2048, attn, x3, residual_dr=x2)
        emit_barrier(nc, c)
        with contextlib.ExitStack() as ph:
            emit_ln(nc, c, ph, "lnD", x3, ln3, onescol, onesrow, cmap, ctok, out32_dr=ln3_32)
        emit_barrier(nc, c)
        with contextlib.ExitStack() as ph:
            emit_moe(nc, c, ph, "mo", ln3, ln3_32, a["gw"], a["fc1"], a["fc1bT"],
                     a["fc2"], a["fc2b"], a["alphaR"], x3, out, onesrow, ident, ctok)
    return nc


def kernel_fused(**inputs):
    ins = {k: np.asarray(v) for k, v in inputs.items()}
    x0 = ins["L0_attn_hidden_states"]
    TT = lambda arr: np.ascontiguousarray(np.asarray(arr).T)
    p_ = np.arange(64)
    host = dict(
        wqT=TT(ins["L0_attn_Wq"]), wkT=TT(ins["L0_attn_Wk"]), wvT=TT(ins["L0_attn_Wv"]),
        convq=ins["L0_attn_conv_q"], convk=ins["L0_attn_conv_k"], convv=ins["L0_attn_conv_v"],
        wf1T=TT(ins["L0_attn_Wf1"]), wf2T=TT(ins["L0_attn_Wf2"]),
        dtbT=np.ascontiguousarray(ins["L0_attn_dt_bias"].reshape(16, 128).T),
        alog=ins["L0_attn_A_log"],
        wbT=TT(ins["L0_attn_Wb"]), wg1T=TT(ins["L0_attn_Wg1"]),
        bg1=np.ascontiguousarray(ins["L0_attn_bg1"]), wg2T=TT(ins["L0_attn_Wg2"]),
        bg2T=np.ascontiguousarray(ins["L0_attn_bg2"].reshape(16, 128).T),
        onR=np.ascontiguousarray(np.tile(ins["L0_attn_onorm_w"][None, :], (128, 1))),
        woT=TT(ins["L0_attn_Wo"]),
        mskK=np.tile(np.where(np.arange(64)[None, :] > p_[:, None], -1.0, 0.0), (2, 1))
            .astype(ml_dtypes.bfloat16),
        mskQ=np.tile(np.where(np.arange(16)[None, :] >= (p_ // 4)[:, None], 1.0, 0.0), (2, 1))
            .astype(ml_dtypes.bfloat16),
        wg=ins["L0_ffn_w_gate"], wu=ins["L0_ffn_w_up"], wd=ins["L0_ffn_w_down"],
        sgT=np.ascontiguousarray(ins["L0_ffn_s_gate"].reshape(64, 128).T),
        suT=np.ascontiguousarray(ins["L0_ffn_s_up"].reshape(64, 128).T),
        sdT=np.ascontiguousarray(ins["L0_ffn_s_down"].reshape(16, 128).T),
        wqkvT=TT(ins["L1_attn_w_qkv"]), wo1T=TT(ins["L1_attn_w_o"]),
        gw=ins["L1_ffn_gate_w"], fc1=ins["L1_ffn_fc1_w"],
        fc1bT=np.ascontiguousarray(ins["L1_ffn_fc1_b"].reshape(8, 16, 128).transpose(0, 2, 1)),
        fc2=ins["L1_ffn_fc2_w"], fc2b=ins["L1_ffn_fc2_b"],
        alphaR=np.ascontiguousarray(np.tile(ins["L1_ffn_alpha"][None, :], (128, 1))))
    nc = _build_fused()
    in_maps = []
    for cid in range(8):
        m = dict(host)
        m["x0"] = np.ascontiguousarray(
            x0[2 * cid:2 * cid + 2].transpose(2, 0, 1).reshape(D, NTOK))
        in_maps.append(m)
    res = _run(nc, in_maps)
    full = np.empty((16, 2048, 2048), np.float32)
    for i in range(8):
        full[2 * i:2 * i + 2] = res.results[i]["out"].reshape(D, 2, T).transpose(1, 2, 0)
    return full


def kernel(**inputs):
    return kernel_fused(**inputs)



class _BlockProxy:

    def __init__(self):
        self.steps = []
        self._pend = 0

    def begin_atomic(self, n):
        self._pend = n

    def _mk(self, eng):
        def deco(fn):
            if self._pend > 0:
                if self._pend == self._first:
                    self.steps.append([(eng, fn)])
                else:
                    self.steps[-1].append((eng, fn))
                self._pend -= 1
            else:
                self.steps.append([(eng, fn)])
            return fn
        return deco

    def begin_atomic(self, n):
        self._pend = n
        self._first = n

    @property
    def tensor(self):
        return self._mk("tensor")

    @property
    def vector(self):
        return self._mk("vector")

    @property
    def scalar(self):
        return self._mk("scalar")

    @property
    def gpsimd(self):
        return self._mk("gpsimd")

    @property
    def sync(self):
        return self._mk("sync")


def _replay(block, steps):
    for grp in steps:
        for eng, fn in grp:
            getattr(block, eng)(fn)


def emit_mkda3(nc, c, st, pfx, qn_dr, kni_dr, vni_dr, la_dr, beta_dr, o_dr,
               mskK_dr, mskQ_dr, onesrow, identb, ctok, bh_list=None):
    NCH = T // 64
    if bh_list is None:
        bh_list = [(b, h) for b in range(BPC) for h in range(16)]
    kI = st.enter_context(nc.sbuf_tensor(pfx + "kI", [128, 4 * T], BF16))
    kexp = kI
    qnh = st.enter_context(nc.sbuf_tensor(pfx + "qnh", [128, T], BF16))
    qgn = qnh
    laG = st.enter_context(nc.sbuf_tensor(pfx + "laG", [128, T], F32))
    G = st.enter_context(nc.sbuf_tensor(pfx + "G", [128, T], F32))
    tmpB = st.enter_context(nc.sbuf_tensor(pfx + "tmpB", [128, 32 * 48], F32))
    brep = st.enter_context(nc.sbuf_tensor(pfx + "brep", [128, 4 * T], BF16))
    w_ = st.enter_context(nc.sbuf_tensor(pfx + "w", [128, 4 * T], BF16))
    khA = st.enter_context(nc.sbuf_tensor(pfx + "khA", [128, 4 * T], BF16))
    kdec = st.enter_context(nc.sbuf_tensor(pfx + "kdec", [128, 4 * T], BF16))
    whA = st.enter_context(nc.sbuf_tensor(pfx + "whA", [128, 4 * T], BF16))
    whB = [None] + [st.enter_context(nc.sbuf_tensor(pfx + f"whB{i}", [128, NCH * 64 * i], BF16))
                    for i in (1, 2, 3)]
    ecB = [None] + [st.enter_context(nc.sbuf_tensor(pfx + f"ecB{i}", [128, NCH * 16 * i], BF16))
                    for i in (1, 2, 3)]
    qhA = st.enter_context(nc.sbuf_tensor(pfx + "qhA", [128, T], BF16))
    e_r = st.enter_context(nc.sbuf_tensor(pfx + "e_r", [128, T], BF16))
    e_c = st.enter_context(nc.sbuf_tensor(pfx + "e_c", [128, T], BF16))
    e_g = st.enter_context(nc.sbuf_tensor(pfx + "e_g", [128, T], BF16))
    e_d = st.enter_context(nc.sbuf_tensor(pfx + "e_d", [128, T], BF16))
    glr = st.enter_context(nc.sbuf_tensor(pfx + "glr", [128, 2 * NCH], F32))
    negS = st.enter_context(nc.sbuf_tensor(pfx + "negS", [128, 128], F32))
    negSb = st.enter_context(nc.sbuf_tensor(pfx + "negSb", [128, 128], BF16))
    mskK = st.enter_context(nc.sbuf_tensor(pfx + "mskK", [128, 64], BF16))
    mskQ = st.enter_context(nc.sbuf_tensor(pfx + "mskQ", [128, 16], BF16))
    msm2 = [st.enter_context(nc.sbuf_tensor(pfx + f"msm{i}", [128, 640], BF16)) for i in range(2)]
    kdT2 = [st.enter_context(nc.sbuf_tensor(pfx + f"kdT{i}", [128, 256], BF16)) for i in range(2)]
    Wt2 = [st.enter_context(nc.sbuf_tensor(pfx + f"Wt{i}", [128, 256], BF16)) for i in range(2)]
    uv2 = [st.enter_context(nc.sbuf_tensor(pfx + f"uv{i}", [128, 256], BF16)) for i in range(2)]
    usb2 = [st.enter_context(nc.sbuf_tensor(pfx + f"usb{i}", [128, 256], BF16)) for i in range(2)]
    vrow2 = [st.enter_context(nc.sbuf_tensor(pfx + f"vr{i}", [128, 256], BF16)) for i in range(2)]
    aB4 = [[[st.enter_context(nc.sbuf_tensor(pfx + f"aB{pp}{b_}{i}", [128, 128], BF16))
             for i in range(2)] for b_ in range(2)] for pp in range(2)]
    bB4 = [[[st.enter_context(nc.sbuf_tensor(pfx + f"bB{pp}{b_}{i}", [128, 128], BF16))
             for i in range(2)] for b_ in range(2)] for pp in range(2)]
    Dnv = [[st.enter_context(nc.sbuf_tensor(pfx + f"Dnv{pp}{i}", [128, 128], BF16))
            for i in range(2)] for pp in range(2)]
    tmpw = [st.enter_context(nc.sbuf_tensor(pfx + f"tmpw{pp}", [128, 256], BF16)) for pp in range(2)]
    tmpu = [[st.enter_context(nc.sbuf_tensor(pfx + f"tmpu{pp}{i}", [128, 128], BF16))
             for i in range(2)] for pp in range(2)]
    Wrow = [[st.enter_context(nc.sbuf_tensor(pfx + f"Wrow{pp}{i}", [128, 128], BF16))
             for i in range(2)] for pp in range(2)]
    a0s2 = [[st.enter_context(nc.sbuf_tensor(pfx + f"a0s{pp}{i}", [128, 128], BF16))
             for i in range(2)] for pp in range(2)]
    obh = st.enter_context(nc.sbuf_tensor(pfx + "obh", [128, 16 * 128], BF16))

    pmT = st.enter_context(nc.psum_tensor(pfx + "pmT", [128, 640], F32))
    pm2 = [pmT.ap(), pmT.ap()]
    pstrT = st.enter_context(nc.psum_tensor(pfx + "pstrT", [128, 128], BF16))
    ps_tr2 = [pstrT.ap(), pstrT.ap()]
    psq2 = [st.enter_context(nc.psum_tensor(pfx + f"psqP{pp}", [128, 256], F32))
            for pp in range(2)]
    pVW2 = [st.enter_context(nc.psum_tensor(pfx + f"pVW{pp}", [128, 512], F32))
            for pp in range(2)]
    pV2 = [pVW2[0][:, 0:256], pVW2[1][:, 0:256]]
    pW2 = [pVW2[0][:, 256:512], pVW2[1][:, 256:512]]
    puS = st.enter_context(nc.psum_tensor(pfx + "puS", [128, 512], F32))
    pu = puS[:, 0:256]
    poS = puS[:, 256:512]

    def Gv(tsz):
        return G.ap().rearrange("p (c t) -> p c t", t=tsz)

    tk = {}
    with nc.Block() as block:
        @block.gpsimd
        def _(g):
            c.dinc(pfx + "msk", g.dma_start(out=mskK[:, :], in_=mskK_dr[:, :]))
            c.dinc(pfx + "msk", g.dma_start(out=mskQ[:, :], in_=mskQ_dr[:, :]))
            tk["msk"] = Tok("D_" + pfx + "msk", c.cnt["D_" + pfx + "msk"])

        for ibh, (b, h) in enumerate(bh_list):
            @block.gpsimd
            def _(g, b=b, h=h, ibh=ibh):
                c.wait(g, tk.get(("ost", ibh - 1)))
                c.dinc(pfx + "in", g.dma_start(out=kI[:, :], in_=kni_dr[b, h, :, :]))
                c.dinc(pfx + "in", g.dma_start(out=qnh[:, :], in_=qn_dr[h, :, b * T:(b + 1) * T]))
                c.dinc(pfx + "in", g.dma_start(out=laG[:, :], in_=la_dr[h, :, b * T:(b + 1) * T]))
                for i in range(4):
                    c.dinc(pfx + "in", g.dma_start(
                        out=brep[:, i * T:(i + 1) * T],
                        in_=beta_dr[h * 4 + i:h * 4 + i + 1,
                                    b * T:(b + 1) * T].partition_broadcast(128)))
                tk[("in", ibh)] = Tok("D_" + pfx + "in", c.cnt["D_" + pfx + "in"])

            @block.vector
            def _(v, ibh=ibh):
                c.wait(v, tk[("in", ibh)])
                for cc in range(NCH):
                    c.inc("vector", v.tensor_tensor_scan(
                        G[:, cc * 64:(cc + 1) * 64], laG[:, cc * 64:(cc + 1) * 64],
                        laG[:, cc * 64:(cc + 1) * 64], 0.0, ALU.add, ALU.bypass))
                tk[("G", ibh)] = c.mark("vector")
                c.wait(v, tk[("G", ibh)])
                t1 = c.inc("vector", v.tensor_copy(
                    glr[:, 0:NCH].rearrange("p (c o) -> p c o", o=1), Gv(64)[:, :, 63:64]))
                t2 = c.inc("vector", v.tensor_sub(
                    laG.ap().rearrange("p (blk t) -> p blk t", t=16),
                    G.ap().rearrange("p (blk t) -> p blk t", t=16),
                    G.ap().rearrange("p (blk t) -> p blk t", t=16)[:, :, 0:1]
                     .broadcast_to([128, 128, 16])))
                tk[("dpre", ibh)] = [t1, t2]

            @block.scalar
            def _(sc, ibh=ibh):
                c.wait(sc, tk[("dpre", ibh)])
                c.inc("scalar", sc.activation(e_r[:, :], laG[:, :], AF.Exp))
                c.inc("scalar", sc.activation(e_c[:, :], laG[:, :], AF.Exp, scale=-1.0))
                c.inc("scalar", sc.activation(e_g[:, :], G[:, :], AF.Exp))
                c.inc("scalar", sc.activation(glr[:, NCH:2 * NCH], glr[:, 0:NCH], AF.Exp))
                tk[("exp1", ibh)] = c.mark("scalar")

            @block.vector
            def _(v, ibh=ibh):
                c.wait(v, tk[("exp1", ibh)])
                tk[("dd", ibh)] = c.inc("vector", v.tensor_sub(
                    laG.ap().rearrange("p (c t) -> p c t", t=64),
                    Gv(64)[:, :, 63:64].broadcast_to([128, NCH, 64]),
                    Gv(64)))

            @block.scalar
            def _(sc, ibh=ibh):
                c.wait(sc, tk[("dd", ibh)])
                tk[("e_dx", ibh)] = c.inc("scalar", sc.activation(e_d[:, :], laG[:, :], AF.Exp))

            for I in (1, 2, 3):
                @block.vector
                def _(v, ibh=ibh, I=I):
                    c.wait(v, tk[("G", ibh)])
                    if I > 1:
                        c.wait(v, tk[("ecBx", ibh, I - 1)])
                    tk[("dB", ibh, I)] = c.inc("vector", v.tensor_sub(
                        tmpB[:, 0:NCH * 16 * I].rearrange("p (cc s) -> p cc s", s=16 * I),
                        Gv(64)[:, :, 16 * I:16 * I + 1].broadcast_to([128, NCH, 16 * I]),
                        Gv(64)[:, :, 0:16 * I]))

                @block.scalar
                def _(sc, ibh=ibh, I=I):
                    c.wait(sc, tk[("dB", ibh, I)])
                    tk[("ecBx", ibh, I)] = c.inc("scalar", sc.activation(
                        ecB[I][:, :], tmpB[:, 0:NCH * 16 * I], AF.Exp))

            @block.vector
            def _(v, ibh=ibh):
                c.wait(v, tk[("exp1", ibh)], tk[("in", ibh)])
                t1 = c.inc("vector", v.tensor_mul(
                    w_.ap().rearrange("p (t i) -> p t i", i=4),
                    kI.ap().rearrange("p (t i) -> p t i", i=4),
                    brep.ap().rearrange("p (i t) -> p t i", t=T)))
                t2 = c.inc("vector", v.tensor_mul(
                    khA.ap().rearrange("p (t i) -> p t i", i=4),
                    kI.ap().rearrange("p (t i) -> p t i", i=4),
                    e_r.ap().rearrange("p (t o) -> p t o", o=1).broadcast_to([128, T, 4])))
                c.wait(v, t1)
                t3 = c.inc("vector", v.tensor_mul(
                    whA.ap().rearrange("p (t i) -> p t i", i=4),
                    w_.ap().rearrange("p (t i) -> p t i", i=4),
                    e_c.ap().rearrange("p (t o) -> p t o", o=1).broadcast_to([128, T, 4])))
                c.wait(v, tk[("e_dx", ibh)])
                t4 = c.inc("vector", v.tensor_mul(
                    kdec.ap().rearrange("p (t i) -> p t i", i=4),
                    w_.ap().rearrange("p (t i) -> p t i", i=4),
                    e_d.ap().rearrange("p (t o) -> p t o", o=1).broadcast_to([128, T, 4])))
                t5 = c.inc("vector", v.tensor_mul(qhA[:, :], qnh[:, :], e_r[:, :]))
                c.wait(v, t5)
                t6 = c.inc("vector", v.tensor_mul(qgn[:, :], qnh[:, :], e_g[:, :]))
                c.wait(v, t6)
                t7 = c.inc("vector", v.tensor_scalar(qgn[:, :], qgn[:, :], -1.0, None, ALU.mult))
                tk[("modv", ibh)] = [t2, t3, t4, t5, t7]

            @block.gpsimd
            def _(g, ibh=ibh):
                c.wait(g, tk[("exp1", ibh)], tk[("modv", ibh)])
                tk[("modg", ibh)] = c.inc("gpsimd", g.tensor_mul(
                    kexp.ap().rearrange("p (t i) -> p t i", i=4),
                    kI.ap().rearrange("p (t i) -> p t i", i=4),
                    e_g.ap().rearrange("p (t o) -> p t o", o=1).broadcast_to([128, T, 4])))

            for I in (1, 2, 3):
                @block.gpsimd
                def _(g, ibh=ibh, I=I):
                    c.wait(g, tk[("ecBx", ibh, I)], tk[("modv", ibh)])
                    tk[("modB", ibh, I)] = c.inc("gpsimd", g.tensor_mul(
                        whB[I].ap().rearrange("p (cc s i) -> p cc s i", s=16 * I, i=4),
                        w_.ap().rearrange("p (cc m) -> p cc m", m=256)[:, :, 0:64 * I]
                           .rearrange("p cc (s i) -> p cc s i", i=4),
                        ecB[I].ap().rearrange("p (cc s o) -> p cc s o", s=16 * I, o=1)
                              .broadcast_to([128, NCH, 16 * I, 4])))

            @block.vector
            def _(v, ibh=ibh):
                c.wait(v, tk[("in", ibh)], tk.get(("sbf", ibh - 1, NCH - 1)))
                t1 = c.inc("vector", v.memset(negS[:, :], 0.0))
                c.wait(v, t1)
                tk[("S0", ibh)] = c.inc("vector", v.tensor_copy(negSb[:, :], negS[:, :]))
            tk[("sbf", ibh, -1)] = tk[("S0", ibh)]

            def stageP(cc, blk_o, ibh=ibh, b=b, h=h):
                mc, tc, pc = cc * 256, cc * 64, cc % 2
                msm, kdT, Wt, uv, vrow = msm2[pc], kdT2[pc], Wt2[pc], uv2[pc], vrow2[pc]

                @blk_o.scalar
                def _(sc, cc=cc, mc=mc, pc=pc):
                    c.wait(sc, tk.get(("uvdone", ibh, cc - 2)))
                    c.dinc(pfx + f"vld{pc}", sc.dma_start(
                        out=vrow2[pc][:, 0:128], in_=vni_dr[b, h, :, mc:mc + 128], transpose=True))
                    c.dinc(pfx + f"vld{pc}", sc.dma_start(
                        out=vrow2[pc][:, 128:256], in_=vni_dr[b, h, :, mc + 128:mc + 256],
                        transpose=True))
                    tk[("vld", ibh, cc)] = Tok("D_" + pfx + f"vld{pc}",
                                               c.cnt["D_" + pfx + f"vld{pc}"])

                @blk_o.tensor
                def _(t, cc=cc, mc=mc, tc=tc):
                    c.wait(t, tk[("modv", ibh)], tk[("modg", ibh)],
                           [tk[("modB", ibh, I)] for I in (1, 2, 3)], tk["msk"])
                    c.wait(t, tk.get(("fevac", ibh, cc - 1)))
                    for j in range(4):
                        rt, base = j // 2, 64 * (j % 2)
                        t.matmul(pm2[pc][base:base + 64, rt * 320 + 64 * j:rt * 320 + 64 * j + 64],
                                 whA[:, mc + 64 * j: mc + 64 * j + 64],
                                 khA[:, mc + 64 * j: mc + 64 * j + 64], start=True, stop=True)
                        t.matmul(pm2[pc][base:base + 64, rt * 320 + 256 + 16 * j:rt * 320 + 256 + 16 * j + 16],
                                 whA[:, mc + 64 * j: mc + 64 * j + 64],
                                 qhA[:, tc + 16 * j: tc + 16 * j + 16], start=True, stop=True)
                    for I in (1, 2, 3):
                        wB = whB[I]
                        mb = cc * 64 * I
                        if I < 3:
                            t.matmul(pm2[pc][0:64 * I, 64 * I:64 * I + 64], wB[:, mb: mb + 64 * I],
                                     khA[:, mc + 64 * I: mc + 64 * I + 64], start=True, stop=True)
                            last = t.matmul(pm2[pc][0:64 * I, 256 + 16 * I:256 + 16 * I + 16],
                                            wB[:, mb: mb + 64 * I],
                                            qhA[:, tc + 16 * I: tc + 16 * I + 16],
                                            start=True, stop=True)
                        else:
                            t.matmul(pm2[pc][0:128, 192:256], wB[:, mb: mb + 128],
                                     khA[:, mc + 192: mc + 256], start=True, stop=True)
                            t.matmul(pm2[pc][0:64, 320 + 192:320 + 256], wB[:, mb + 128: mb + 192],
                                     khA[:, mc + 192: mc + 256], start=True, stop=True)
                            t.matmul(pm2[pc][0:128, 304:320], wB[:, mb: mb + 128],
                                     qhA[:, tc + 48: tc + 64], start=True, stop=True)
                            last = t.matmul(pm2[pc][0:64, 320 + 304:320 + 320], wB[:, mb + 128: mb + 192],
                                            qhA[:, tc + 48: tc + 64], start=True, stop=True)
                    tk[("fmm", ibh, cc)] = c.inc("tensor", last)

                @blk_o.gpsimd
                def _(g, cc=cc, pc=pc):
                    c.wait(g, tk.get(("omm", ibh, cc - 2)))
                    tk[("msz", ibh, cc)] = c.inc("gpsimd", g.memset(msm2[pc][:, :], 0.0))

                @blk_o.vector
                def _(v, cc=cc, pc=pc):
                    c.wait(v, tk[("fmm", ibh, cc)], tk[("msz", ibh, cc)])
                    ts = []
                    for j in range(4):
                        rt, base = j // 2, 64 * (j % 2)
                        ts.append(c.inc("vector", v.tensor_mul(
                            msm2[pc][base:base + 64, rt * 320 + 64 * j: rt * 320 + 64 * j + 64],
                            pm2[pc][base:base + 64, rt * 320 + 64 * j:rt * 320 + 64 * j + 64],
                            mskK[base:base + 64, :])))
                        ts.append(c.inc("vector", v.tensor_mul(
                            msm2[pc][base:base + 64,
                                     rt * 320 + 256 + 16 * j: rt * 320 + 256 + 16 * j + 16],
                            pm2[pc][base:base + 64, rt * 320 + 256 + 16 * j:rt * 320 + 256 + 16 * j + 16],
                            mskQ[base:base + 64, :])))
                    tk[("evA", ibh, cc)] = ts

                @blk_o.scalar
                def _(sc, cc=cc, pc=pc):
                    c.wait(sc, tk[("fmm", ibh, cc)], tk[("msz", ibh, cc)])
                    msm = msm2[pc]
                    t1 = c.inc("scalar", sc.mul(msm[0:64, 64:256], pm2[pc][0:64, 64:256], -1.0))
                    t2 = c.inc("scalar", sc.mul(msm[64:128, 128:256], pm2[pc][64:128, 128:256], -1.0))
                    t3 = c.inc("scalar", sc.mul(msm[0:64, 320 + 192:320 + 256],
                                                pm2[pc][0:64, 320 + 192:320 + 256], -1.0))
                    t4 = c.inc("scalar", sc.copy(msm[0:64, 272:320], pm2[pc][0:64, 272:320]))
                    t5 = c.inc("scalar", sc.copy(msm[64:128, 288:320], pm2[pc][64:128, 288:320]))
                    t6 = c.inc("scalar", sc.copy(msm[0:64, 320 + 304:320 + 320],
                                                 pm2[pc][0:64, 320 + 304:320 + 320]))
                    tk[("evB", ibh, cc)] = [t1, t2, t3, t4, t5, t6]
                    tk[("fevac", ibh, cc)] = [tk[("evA", ibh, cc)], tk[("evB", ibh, cc)]]

                trs = [("s0", lambda msm=msm2[pc]: msm[:, 0:128], a0s2[pc][0], 1.0, "vector"),
                       ("s1", lambda msm=msm2[pc]: msm[:, 320 + 128:320 + 256], a0s2[pc][1], 1.0, "vector"),
                       ("k0", lambda mc=mc: kdec[:, mc:mc + 128], (kdT2[pc], 0), -1.0, "scalar"),
                       ("k1", lambda mc=mc: kdec[:, mc + 128:mc + 256], (kdT2[pc], 1), -1.0, "scalar")]
                for ti, (nm, src, dst, sgn, eng) in enumerate(trs):
                    blk_o.begin_atomic(2)
                    @blk_o.tensor
                    def _(t, cc=cc, ti=ti, nm=nm, src=src):
                        if nm in ("s0", "s1"):
                            c.wait(t, tk[("fevac", ibh, cc)], ctok)
                        else:
                            c.wait(t, tk[("modv", ibh)], ctok)
                        c.wait(t, tk.get(("trchain", ibh)))
                        if nm.startswith("k"):
                            c.wait(t, tk.get(("smm", ibh, cc - 2)))
                        tk[("tr", ibh, cc, ti)] = c.inc("tensor", t.transpose(
                            ps_tr2[pc][:, :], src(), identb[:, :]))

                    if eng == "vector":
                        @blk_o.vector
                        def _(v, cc=cc, ti=ti, dst=dst):
                            c.wait(v, tk[("tr", ibh, cc, ti)])
                            c.wait(v, tk.get(("ch1", ibh, cc - 2)))
                            tk[("trev", ibh, cc, ti)] = c.inc("vector", v.tensor_copy(
                                dst[:, :], ps_tr2[pc][:, :]))
                            tk[("trchain", ibh)] = tk[("trev", ibh, cc, ti)]
                    else:
                        @blk_o.scalar
                        def _(sc, cc=cc, ti=ti, dst=dst):
                            c.wait(sc, tk[("tr", ibh, cc, ti)])
                            buf, half = dst
                            tk[("trev", ibh, cc, ti)] = c.inc("scalar", sc.mul(
                                buf[:, half * 128:(half + 1) * 128], ps_tr2[pc][:, :], -1.0))
                            tk[("trchain", ibh)] = tk[("trev", ibh, cc, ti)]

                for blk in range(2):
                    @blk_o.vector
                    def _(v, cc=cc, blk=blk, pc=pc):
                        msm = msm2[pc]
                        b0aps = [msm[:, 0:128], msm[:, 320 + 128:320 + 256]]
                        c.wait(v, tk[("fevac", ibh, cc)], ctok)
                        c.wait(v, tk.get(("dapl", ibh, cc - 2)))
                        tk[("dx0", ibh, cc, blk)] = c.inc("vector", v.tensor_add(
                            tmpu[pc][blk][:, :], b0aps[blk], identb[:, :]))

                for lvl in range(1, 7):
                    @blk_o.tensor
                    def _(t, cc=cc, lvl=lvl, pc=pc):
                        msm = msm2[pc]
                        b0aps = [msm[:, 0:128], msm[:, 320 + 128:320 + 256]]
                        if lvl == 1:
                            c.wait(t, tk[("trev", ibh, cc, 0)], tk[("trev", ibh, cc, 1)])
                            c.wait(t, tk.get(("chevA", ibh, cc - 2, 6)))
                        else:
                            c.wait(t, tk[("chevB", ibh, cc, lvl - 1)])
                        last = None
                        for blk in range(2):
                            aprev = (aB4[pc][blk][(lvl - 1) % 2][:, :] if lvl > 1 else a0s2[pc][blk][:, :])
                            bprev = (bB4[pc][blk][(lvl - 1) % 2][:, :] if lvl > 1 else b0aps[blk])
                            last = t.matmul(psq2[pc][:, blk * 128:(blk + 1) * 128],
                                            bprev, aprev, start=True, stop=True)
                        tk[("chmmA", ibh, cc, lvl)] = c.inc("tensor", last)

                    @blk_o.vector
                    def _(v, cc=cc, lvl=lvl):
                        c.wait(v, tk[("chmmA", ibh, cc, lvl)])
                        last = None
                        for blk in range(2):
                            last = c.inc("vector", v.tensor_copy(
                                aB4[pc][blk][lvl % 2][:, :], psq2[pc][:, blk * 128:(blk + 1) * 128]))
                        tk[("chevA", ibh, cc, lvl)] = last

                    @blk_o.tensor
                    def _(t, cc=cc, lvl=lvl, pc=pc):
                        msm = msm2[pc]
                        b0aps = [msm[:, 0:128], msm[:, 320 + 128:320 + 256]]
                        c.wait(t, tk[("chevA", ibh, cc, lvl)])
                        if lvl == 1:
                            c.wait(t, tk[("dx0", ibh, cc, 0)], tk[("dx0", ibh, cc, 1)])
                            c.wait(t, tk.get(("pVfree", ibh, cc - 2)))
                        else:
                            c.wait(t, tk[("dadd", ibh, cc, lvl - 1)])
                        last = None
                        lastpv = None
                        for blk in range(2):
                            aprev = (aB4[pc][blk][(lvl - 1) % 2][:, :] if lvl > 1 else a0s2[pc][blk][:, :])
                            bprev = (bB4[pc][blk][(lvl - 1) % 2][:, :] if lvl > 1 else b0aps[blk])
                            lastpv = t.matmul(pV2[pc][:, blk * 128:(blk + 1) * 128],
                                              aB4[pc][blk][lvl % 2][:, :], tmpu[pc][blk][:, :],
                                              start=True, stop=True)
                            if lvl < 6:
                                last = t.matmul(psq2[pc][:, blk * 128:(blk + 1) * 128],
                                                aprev, bprev, start=True, stop=True)
                        tk[("chmmB", ibh, cc, lvl)] = c.inc("tensor", last if lvl < 6 else lastpv)
                        if lvl == 1:
                            tk[("ch1", ibh, cc)] = tk[("chmmB", ibh, cc, lvl)]

                    @blk_o.vector
                    def _(v, cc=cc, lvl=lvl):
                        c.wait(v, tk[("chmmB", ibh, cc, lvl)])
                        last = None
                        for blk in range(2):
                            dst = tmpu[pc][blk][:, :] if lvl < 6 else Dnv[pc][blk][:, :]
                            last = c.inc("vector", v.tensor_add(
                                dst, pV2[pc][:, blk * 128:(blk + 1) * 128], tmpu[pc][blk][:, :]))
                        tk[("dadd", ibh, cc, lvl)] = last
                        if lvl == 6:
                            pass
                        if lvl < 6:
                            last2 = None
                            for blk in range(2):
                                last2 = c.inc("vector", v.tensor_copy(
                                    bB4[pc][blk][lvl % 2][:, :], psq2[pc][:, blk * 128:(blk + 1) * 128]))
                            tk[("chevB", ibh, cc, lvl)] = last2
                        else:
                            tk[("Dfin", ibh, cc)] = last
                            tk[("dapl", ibh, cc)] = last

                @blk_o.tensor
                def _(t, cc=cc, pc=pc, mc=mc):
                    msm = msm2[pc]
                    c.wait(t, tk[("Dfin", ibh, cc)], tk[("vld", ibh, cc)])
                    c.wait(t, tk.get(("slv_ev", ibh, cc - 2)))
                    t.matmul(pV2[pc][:, 0:128], Dnv[pc][0][:, :], vrow2[pc][:, 0:128], start=True, stop=True)
                    last = t.matmul(pW2[pc][:, 0:128], kexp[:, mc:mc + 128], identb[:, :],
                                    start=True, stop=True)
                    tk[("slv1", ibh, cc)] = c.inc("tensor", last)

                @blk_o.vector
                def _(v, cc=cc, pc=pc):
                    c.wait(v, tk[("slv1", ibh, cc)])
                    c.inc("vector", v.tensor_copy(uv2[pc][:, 0:128], pV2[pc][:, 0:128]))
                    tk[("slv1ev", ibh, cc)] = c.inc("vector", v.tensor_copy(
                        tmpw[pc][:, 0:128], pW2[pc][:, 0:128]))

                @blk_o.tensor
                def _(t, cc=cc, pc=pc, mc=mc):
                    msm = msm2[pc]
                    c.wait(t, tk[("slv1ev", ibh, cc)])
                    t.matmul(pV2[pc][:, 128:256], identb[:, :], vrow2[pc][:, 128:256],
                             start=True, stop=False)
                    t.matmul(pV2[pc][:, 128:256], msm[:, 128:256], uv2[pc][:, 0:128],
                             start=False, stop=True)
                    last = t.matmul(pW2[pc][:, 128:256], Dnv[pc][0][:, :], tmpw[pc][:, 0:128],
                                    start=True, stop=True)
                    tk[("slv2", ibh, cc)] = c.inc("tensor", last)

                @blk_o.vector
                def _(v, cc=cc, pc=pc):
                    c.wait(v, tk[("slv2", ibh, cc)])
                    c.inc("vector", v.tensor_copy(tmpw[pc][:, 128:256], pV2[pc][:, 128:256]))
                    tk[("slv2ev", ibh, cc)] = c.inc("vector", v.tensor_copy(
                        Wrow[pc][0][:, :], pW2[pc][:, 128:256]))

                @blk_o.tensor
                def _(t, cc=cc, pc=pc, mc=mc):
                    msm = msm2[pc]
                    c.wait(t, tk[("slv2ev", ibh, cc)])
                    t.matmul(pV2[pc][:, 0:128], Dnv[pc][1][:, :], tmpw[pc][:, 128:256], start=True, stop=True)
                    t.matmul(pW2[pc][:, 0:128], kexp[:, mc + 128:mc + 256], identb[:, :],
                             start=True, stop=False)
                    last = t.matmul(pW2[pc][:, 0:128], msm[:, 128:256], Wrow[pc][0][:, :],
                                    start=False, stop=True)
                    tk[("slv3", ibh, cc)] = c.inc("tensor", last)

                @blk_o.vector
                def _(v, cc=cc, pc=pc):
                    c.wait(v, tk[("slv3", ibh, cc)])
                    c.inc("vector", v.tensor_copy(uv2[pc][:, 128:256], pV2[pc][:, 0:128]))
                    tk[("slv3ev", ibh, cc)] = c.inc("vector", v.tensor_copy(
                        tmpw[pc][:, 0:128], pW2[pc][:, 0:128]))

                blk_o.begin_atomic(2)
                @blk_o.tensor
                def _(t, cc=cc, pc=pc):
                    c.wait(t, tk[("slv3ev", ibh, cc)])
                    t.matmul(pW2[pc][:, 128:256], Dnv[pc][1][:, :], tmpw[pc][:, 0:128], start=True, stop=True)
                    c.wait(t, tk.get(("trchain", ibh)))
                    last = t.transpose(ps_tr2[pc][:, :], Wrow[pc][0][:, :], identb[:, :])
                    tk[("slv4", ibh, cc)] = c.inc("tensor", last)

                @blk_o.scalar
                def _(sc, cc=cc, pc=pc):
                    c.wait(sc, tk[("slv4", ibh, cc)])
                    c.wait(sc, tk.get(("umm", ibh, cc - 2)))
                    tk[("trevW", ibh, cc, 0)] = c.inc("scalar", sc.copy(
                        Wt2[pc][:, 0:128], ps_tr2[pc][:, :]))
                    tk[("trchain", ibh)] = tk[("trevW", ibh, cc, 0)]

                @blk_o.vector
                def _(v, cc=cc, pc=pc):
                    c.wait(v, tk[("slv4", ibh, cc)], tk.get(("slv5", ibh, cc - 2)))
                    tk[("slv4ev", ibh, cc)] = c.inc("vector", v.tensor_copy(
                        Wrow[pc][1][:, :], pW2[pc][:, 128:256]))

                blk_o.begin_atomic(2)
                @blk_o.tensor
                def _(t, cc=cc, pc=pc):
                    c.wait(t, tk[("slv4ev", ibh, cc)], tk[("trevW", ibh, cc, 0)])
                    c.wait(t, tk.get(("trchain", ibh)))
                    tk[("slv5", ibh, cc)] = c.inc("tensor", t.transpose(
                        ps_tr2[pc][:, :], Wrow[pc][1][:, :], identb[:, :]))

                @blk_o.scalar
                def _(sc, cc=cc, pc=pc):
                    c.wait(sc, tk[("slv5", ibh, cc)])
                    tk[("trevW", ibh, cc, 1)] = c.inc("scalar", sc.copy(
                        Wt2[pc][:, 128:256], ps_tr2[pc][:, :]))
                    tk[("trchain", ibh)] = tk[("trevW", ibh, cc, 1)]
                    tk[("slv_ev", ibh, cc)] = [tk[("slv4ev", ibh, cc)], tk[("slv3ev", ibh, cc)]]
                    tk[("pVfree", ibh, cc)] = [tk[("slv2ev", ibh, cc)], tk[("slv3ev", ibh, cc)], tk[("slv4ev", ibh, cc)]]
                    tk[("Vfinev", ibh, cc, 1)] = tk[("slv3ev", ibh, cc)]
                    tk[("uvdone", ibh, cc)] = tk[("slv3", ibh, cc)]



            def serial(cc, ibh=ibh, b=b, h=h):
                mc, tc, pc = cc * 256, cc * 64, cc % 2
                msm, kdT, Wt, uv = msm2[pc], kdT2[pc], Wt2[pc], uv2[pc]

                @block.tensor
                def _(t, cc=cc, pc=pc, tc=tc):
                    c.wait(t, tk[("Vfinev", ibh, cc, 1)], tk[("trevW", ibh, cc, 1)],
                           tk[("sbf", ibh, cc - 1)])
                    c.wait(t, tk.get(("usbev", ibh, cc - 2)))
                    t.matmul(pu[:, 0:128], Wt2[pc][:, 0:128], negSb[:, :], start=True, stop=True)
                    last = t.matmul(pu[:, 128:256], Wt2[pc][:, 128:256], negSb[:, :],
                                    start=True, stop=True)
                    tk[("umm", ibh, cc)] = c.inc("tensor", last)

                @block.vector
                def _(v, cc=cc, pc=pc):
                    c.wait(v, tk[("umm", ibh, cc)])
                    c.wait(v, tk.get(("smm", ibh, cc - 1)))
                    c.inc("vector", v.tensor_add(usb2[pc][:, 0:128], pu[:, 0:128],
                                                 uv2[pc][:, 0:128]))
                    tk[("usbev", ibh, cc)] = c.inc("vector", v.tensor_add(
                        usb2[pc][:, 128:256], pu[:, 128:256], uv2[pc][:, 128:256]))

                @block.tensor
                def _(t, cc=cc, pc=pc, tc=tc):
                    msm = msm2[pc]
                    c.wait(t, tk[("usbev", ibh, cc)])
                    c.wait(t, tk.get(("oev", ibh, cc - 1)))
                    t.matmul(poS[0:64, 0:128], msm[0:128, 256:320], usb2[pc][:, 0:128],
                             start=True, stop=False)
                    t.matmul(poS[0:64, 0:128], msm[0:128, 320 + 256:320 + 320], usb2[pc][:, 128:256],
                             start=False, stop=False)
                    tk[("omm", ibh, cc)] = c.inc("tensor", t.matmul(
                        poS[0:64, 0:128], qgn[:, tc:tc + 64], negSb[:, :], start=False, stop=True))

                @block.scalar
                def _(sc, cc=cc):
                    c.wait(sc, tk[("omm", ibh, cc)])
                    if cc <= 1:
                        c.wait(sc, tk.get(("ost", ibh - 1)))
                    tk[("oev", ibh, cc)] = c.inc("scalar", sc.mul(
                        obh[64 * (cc % 2):64 * (cc % 2) + 64, (cc // 2) * 128:(cc // 2 + 1) * 128],
                        poS[0:64, 0:128], MKSCALE))

                @block.tensor
                def _(t, cc=cc, pc=pc):
                    c.wait(t, tk.get(("sup", ibh, cc - 1)), tk[("oev", ibh, cc)])
                    t.matmul(poS[:, 128:256], kdT2[pc][:, 0:128], usb2[pc][:, 0:128],
                             start=True, stop=False)
                    tk[("smm", ibh, cc)] = c.inc("tensor", t.matmul(
                        poS[:, 128:256], kdT2[pc][:, 128:256], usb2[pc][:, 128:256],
                        start=False, stop=True))

                @block.vector
                def _(v, cc=cc):
                    c.wait(v, tk[("smm", ibh, cc)])
                    tk[("sup", ibh, cc)] = c.inc("vector", v.scalar_tensor_tensor(
                        negS[:, :], negS[:, :], glr[:, NCH + cc:NCH + cc + 1], poS[:, 128:256],
                        ALU.mult, ALU.add))

                @block.scalar
                def _(sc, cc=cc):
                    c.wait(sc, tk[("sup", ibh, cc)])
                    tk[("sbf", ibh, cc)] = c.inc("scalar", sc.copy(negSb[:, :], negS[:, :]))

            STAG = 5
            for p0 in range(0, NCH, 2):
                pa, pb = _BlockProxy(), _BlockProxy()
                stageP(p0, pa)
                stageP(p0 + 1, pb)
                _replay(block, pa.steps[:STAG])
                sa, sb = pa.steps[STAG:], pb.steps
                n = max(len(sa), len(sb))
                for i in range(n):
                    if i < len(sa):
                        _replay(block, [sa[i]])
                    if i < len(sb):
                        _replay(block, [sb[i]])
                serial(p0)
                serial(p0 + 1)

            @block.sync
            def _(sy, ibh=ibh, b=b, h=h):
                c.wait(sy, [tk[("oev", ibh, cc)] for cc in range(NCH)])
                tk[("ost", ibh)] = c.dinc(pfx + "ost", sy.dma_start(
                    out=o_dr[b, h].rearrange("(cc p) dv -> p cc dv", p=128),
                    in_=obh.ap().rearrange("p (cc dv) -> p cc dv", dv=128)))

        @block.sync
        def _(sy):
            c.wait(sy, tk[("ost", len(bh_list) - 1)])
    return


# version 09
# speedup vs baseline: 1.2682x; 1.2682x over previous
import contextlib
import numpy as np
import ml_dtypes
import concourse.bass as bass
import concourse.mybir as mybir
from concourse.bass_utils import run_bass_kernel_spmd

F32 = mybir.dt.float32
BF16 = mybir.dt.bfloat16
I8 = mybir.dt.int8
AF = mybir.ActivationFunctionType
ALU = mybir.AluOpType
AX = mybir.AxisListType

D = 2048
T = 2048
BPC = 2
NTOK = BPC * T
NSTRIP = NTOK // 512
KT = D // 128

ENGS = ("tensor", "vector", "scalar", "gpsimd", "sync")


class Tok:
    __slots__ = ("eng", "val")

    def __init__(self, eng, val):
        self.eng = eng
        self.val = val


_ENGMAP = {"EngineType.PE": "tensor", "EngineType.DVE": "vector",
           "EngineType.Activation": "scalar", "EngineType.Pool": "gpsimd",
           "EngineType.SP": "sync"}


class Ctx:

    def __init__(self, nc):
        self.nc = nc
        self.sem = {}
        self.cnt = {}
        self._stack = None
        self.eclock = {e: {} for e in ENGS}
        self.snap = {}

    def make_sems(self, stack):
        self._stack = stack
        for e in ENGS:
            self.sem[e] = stack.enter_context(self.nc.semaphore(f"sem_{e}"))
            self.cnt[e] = 0

    def _get(self, name):
        if name not in self.sem:
            self.sem[name] = self._stack.enter_context(self.nc.semaphore(name))
            self.cnt[name] = 0
        return self.sem[name]

    def _engname(self, instr_or_handle):
        e = getattr(instr_or_handle, "engine", None)
        return _ENGMAP.get(str(e), None)

    def inc(self, eng, instr):
        instr.then_inc(self.sem[eng], 1)
        self.cnt[eng] += 1
        snapshot = dict(self.eclock[eng])
        snapshot[eng] = self.cnt[eng]
        self.snap[(eng, self.cnt[eng])] = snapshot
        return Tok(eng, self.cnt[eng])

    def dinc(self, stream, instr):
        key = "D_" + stream
        sem = self._get(key)
        instr.then_inc(sem, 16)
        self.cnt[key] += 16
        eng = self._engname(instr)
        snapshot = dict(self.eclock[eng]) if eng else {}
        snapshot[key] = self.cnt[key]
        self.snap[(key, self.cnt[key])] = snapshot
        return Tok(key, self.cnt[key])

    def wait(self, handle, *toks):
        best = {}

        def add(t):
            if t is None:
                return
            if isinstance(t, (list, tuple)):
                for x in t:
                    add(x)
            else:
                best[t.eng] = max(best.get(t.eng, 0), t.val)
        add(toks)
        eng = self._engname(handle)
        clk = self.eclock[eng] if eng else {}
        for sname, v in best.items():
            if clk.get(sname, 0) >= v:
                continue
            if sname == eng:
                handle.wait_ge(self.sem[sname], v)
                clk[sname] = max(clk.get(sname, 0), v)
                continue
            handle.wait_ge(self.sem[sname], v)
            snapshot = self.snap.get((sname, v))
            if snapshot:
                for k2, v2 in snapshot.items():
                    if k2 == eng:
                        continue
                    if clk.get(k2, 0) < v2:
                        clk[k2] = v2
            else:
                clk[sname] = max(clk.get(sname, 0), v)

    def mark(self, eng):
        return Tok(eng, self.cnt[eng])

    def allmarks(self):
        return [Tok(k, v) for k, v in self.cnt.items()]



def emit_consts(nc, c, st, values=()):
    onescol = st.enter_context(nc.sbuf_tensor("onescol", [128, 1], BF16))
    onesrow = st.enter_context(nc.sbuf_tensor("onesrow", [1, 128], BF16))
    ident = st.enter_context(nc.sbuf_tensor("identc", [128, 128], BF16))
    cvals = list(values)
    ctile = st.enter_context(nc.sbuf_tensor("cvals", [128, max(1, len(cvals))], F32)) \
        if cvals else None
    cmap = {}
    toks = {}
    with nc.Block() as block:
        @block.gpsimd
        def _(g):
            ts = [c.inc("gpsimd", g.memset(onescol[:, :], 1.0)),
                  c.inc("gpsimd", g.memset(onesrow[:, :], 1.0))]
            ti = c.inc("gpsimd", g.memset(ident[:, :], 0.0))
            c.wait(g, ti)
            ts.append(c.inc("gpsimd", g.affine_select(
                out=ident[:, :], in_=ident[:, :], compare_op=ALU.not_equal,
                fill=1.0, base=0, pattern=[[-1, 128]], channel_multiplier=1)))
            for i, v in enumerate(cvals):
                ts.append(c.inc("gpsimd", g.memset(ctile[:, i:i + 1], float(v))))
                cmap[v] = ctile[:, i:i + 1]
            toks["consts"] = ts
    return onescol, onesrow, ident, cmap, toks["consts"]



def emit_ln(nc, c, st, pfx, x_dr, out_dr, onescol, onesrow, cmap, consts_tok, out32_dr=None):
    xf = st.enter_context(nc.sbuf_tensor(pfx + "xf", [128, KT * 512], F32))
    xb = st.enter_context(nc.sbuf_tensor(pfx + "xb", [128, KT * 512], BF16))
    sq = st.enter_context(nc.sbuf_tensor(pfx + "sq", [128, KT * 512], BF16))
    tmp = st.enter_context(nc.sbuf_tensor(pfx + "tmp", [128, KT * 512], F32))
    ob = st.enter_context(nc.sbuf_tensor(pfx + "ob", [128, KT * 512], BF16))
    o32 = st.enter_context(nc.sbuf_tensor(pfx + "o32", [128, KT * 512], F32)) \
        if out32_dr is not None else None
    rows = st.enter_context(nc.sbuf_tensor(pfx + "rows", [1, 512 * 6], F32))
    rowb = st.enter_context(nc.sbuf_tensor(pfx + "rowb", [1, 512 * 2], BF16))
    repr_ = st.enter_context(nc.sbuf_tensor(pfx + "repr", [128, 512], F32))
    repm = st.enter_context(nc.sbuf_tensor(pfx + "repm", [128, 512], F32))
    ps_sum = st.enter_context(nc.psum_tensor(pfx + "psum", [1, 512], F32))
    ps_sq = st.enter_context(nc.psum_tensor(pfx + "psq", [1, 512], F32))
    ps_rr = st.enter_context(nc.psum_tensor(pfx + "prr", [128, 512], F32))
    ps_rm = st.enter_context(nc.psum_tensor(pfx + "prm", [128, 512], F32))

    M_, V_, SR_, RS_, MS_ = (slice(i * 512, (i + 1) * 512) for i in range(5))

    x_t = x_dr.rearrange("(kt p) (s n) -> p kt s n", p=128, n=512)
    o_t = out_dr.rearrange("(kt p) (s n) -> p kt s n", p=128, n=512)
    o32_t = out32_dr.rearrange("(kt p) (s n) -> p kt s n", p=128, n=512) \
        if out32_dr is not None else None

    tk = {}
    prev = {"apply": None, "mm_rep": None, "mm_st": None}
    with nc.Block() as block:
        for s in range(NSTRIP):
            @block.gpsimd
            def _(g, s=s):
                c.wait(g, tk.get(("ap_done", s - 1)))
                tk[("ld", s)] = c.dinc("ln_x", g.dma_start(
                    out=xf.ap().rearrange("p (kt n) -> p kt n", n=512),
                    in_=x_t[:, :, s, :]))

            @block.vector
            def _(v, s=s):
                c.wait(v, tk[("ld", s)], tk.get(("mm_st", s - 1)))
                tk[("xb", s)] = c.inc("vector", v.tensor_copy(xb[:, :], xf[:, :]))

            @block.scalar
            def _(sc, s=s):
                c.wait(sc, tk[("ld", s)], tk.get(("mm_st", s - 1)))
                tk[("sq", s)] = c.inc("scalar", sc.activation(sq[:, :], xf[:, :], AF.Square))

            @block.tensor
            def _(t, s=s):
                c.wait(t, tk[("xb", s)], tk[("sq", s)], consts_tok,
                       tk.get(("rows_done", s - 1)))
                for k in range(KT):
                    last1 = t.matmul(ps_sum[:, :], onescol[:, :], xb[:, k * 512:(k + 1) * 512],
                                     start=(k == 0), stop=(k == KT - 1))
                for k in range(KT):
                    last2 = t.matmul(ps_sq[:, :], onescol[:, :], sq[:, k * 512:(k + 1) * 512],
                                     start=(k == 0), stop=(k == KT - 1))
                tk[("mm_st", s)] = c.inc("tensor", last2)

            @block.vector
            def _(v, s=s):
                c.wait(v, tk[("mm_st", s)])
                t1 = c.inc("vector", v.tensor_scalar(rows[:, M_], ps_sum[:, :], 1.0 / D, None, ALU.mult))
                t2 = c.inc("vector", v.tensor_scalar(rows[:, V_], ps_sq[:, :], 1.0 / D, None, ALU.mult))
                c.wait(v, t1, t2)
                t4 = c.inc("vector", v.tensor_mul(rows[:, SR_], rows[:, M_], rows[:, M_]))
                c.wait(v, t4)
                tk[("var", s)] = c.inc("vector", v.tensor_sub(rows[:, V_], rows[:, V_], rows[:, SR_]))

            @block.scalar
            def _(sc, s=s):
                c.wait(sc, tk[("var", s)])
                tk[("srt", s)] = c.inc("scalar", sc.activation(
                    rows[:, SR_], rows[:, V_], AF.Sqrt, bias=cmap[1e-5][0:1, 0:1]))

            @block.vector
            def _(v, s=s):
                c.wait(v, tk[("srt", s)])
                t1 = c.inc("vector", v.reciprocal(rows[:, RS_], rows[:, SR_]))
                c.wait(v, t1)
                t2 = c.inc("vector", v.tensor_mul(rows[:, MS_], rows[:, M_], rows[:, RS_]))
                t3 = c.inc("vector", v.tensor_copy(rowb[:, 0:512], rows[:, RS_]))
                c.wait(v, t2, t3)
                tk[("rows_done", s)] = c.inc("vector", v.tensor_copy(rowb[:, 512:1024], rows[:, MS_]))

            @block.tensor
            def _(t, s=s):
                c.wait(t, tk[("rows_done", s)], tk.get(("repv", s - 1)))
                t.matmul(ps_rr[:, :], onesrow[:, :], rowb[0:1, 0:512], start=True, stop=True)
                last = t.matmul(ps_rm[:, :], onesrow[:, :], rowb[0:1, 512:1024], start=True, stop=True)
                tk[("mm_rep", s)] = c.inc("tensor", last)

            @block.vector
            def _(v, s=s):
                c.wait(v, tk[("mm_rep", s)])
                t1 = c.inc("vector", v.tensor_copy(repr_[:, :], ps_rr[:, :]))
                t2 = c.inc("vector", v.tensor_copy(repm[:, :], ps_rm[:, :]))
                tk[("repv", s)] = [t1, t2]

            if out32_dr is None:
                @block.vector
                def _(v, s=s):
                    c.wait(v, tk[("repv", s)], tk.get(("ob_store", s - 1)))
                    for k in range(0, KT, 2):
                        t1 = c.inc("vector", v.tensor_mul(
                            tmp[:, k * 512:(k + 1) * 512], xf[:, k * 512:(k + 1) * 512], repr_[:, :]))
                        c.wait(v, t1)
                        c.inc("vector", v.tensor_sub(
                            ob[:, k * 512:(k + 1) * 512], tmp[:, k * 512:(k + 1) * 512], repm[:, :]))
                    tk[("ap_v", s)] = c.mark("vector")

                @block.gpsimd
                def _(g, s=s):
                    c.wait(g, tk[("repv", s)], tk.get(("ob_store", s - 1)))
                    for k in range(1, KT, 2):
                        t1 = c.inc("gpsimd", g.tensor_mul(
                            tmp[:, k * 512:(k + 1) * 512], xf[:, k * 512:(k + 1) * 512], repr_[:, :]))
                        c.wait(g, t1)
                        c.inc("gpsimd", g.tensor_sub(
                            ob[:, k * 512:(k + 1) * 512], tmp[:, k * 512:(k + 1) * 512], repm[:, :]))
                    tk[("ap_g", s)] = c.mark("gpsimd")
                    tk[("ap_done", s)] = [tk[("ap_v", s)], tk[("ap_g", s)]]
            else:
                @block.vector
                def _(v, s=s):
                    c.wait(v, tk[("repv", s)], tk.get(("ob_store", s - 1)), tk.get(("o32_store", s - 1)))
                    for k in range(0, KT, 2):
                        t1 = c.inc("vector", v.tensor_mul(
                            tmp[:, k * 512:(k + 1) * 512], xf[:, k * 512:(k + 1) * 512], repr_[:, :]))
                        c.wait(v, t1)
                        c.inc("vector", v.tensor_sub(
                            o32[:, k * 512:(k + 1) * 512], tmp[:, k * 512:(k + 1) * 512], repm[:, :]))
                    tk[("ap_v", s)] = c.mark("vector")

                @block.gpsimd
                def _(g, s=s):
                    c.wait(g, tk[("repv", s)], tk.get(("ob_store", s - 1)), tk.get(("o32_store", s - 1)))
                    for k in range(1, KT, 2):
                        t1 = c.inc("gpsimd", g.tensor_mul(
                            tmp[:, k * 512:(k + 1) * 512], xf[:, k * 512:(k + 1) * 512], repr_[:, :]))
                        c.wait(g, t1)
                        c.inc("gpsimd", g.tensor_sub(
                            o32[:, k * 512:(k + 1) * 512], tmp[:, k * 512:(k + 1) * 512], repm[:, :]))
                    tk[("ap_g", s)] = c.mark("gpsimd")

                @block.scalar
                def _(sc, s=s):
                    c.wait(sc, tk[("ap_v", s)], tk[("ap_g", s)])
                    t1 = c.inc("scalar", sc.copy(ob[:, 0:8 * 512], o32[:, 0:8 * 512]))
                    t2 = c.inc("scalar", sc.copy(ob[:, 8 * 512:KT * 512], o32[:, 8 * 512:KT * 512]))
                    tk[("ap_done", s)] = [t1, t2]

            @block.sync
            def _(sy, s=s):
                c.wait(sy, tk[("ap_done", s)])
                tk[("ob_store", s)] = c.dinc("ln_o", sy.dma_start(
                    out=o_t[:, :, s, :],
                    in_=ob.ap().rearrange("p (kt n) -> p kt n", n=512)))
                if out32_dr is not None:
                    tk[("o32_store", s)] = c.dinc("ln_o32", sy.dma_start(
                        out=o32_t[:, :, s, :],
                        in_=o32.ap().rearrange("p (kt n) -> p kt n", n=512)))

        @block.sync
        def _(sy):
            c.wait(sy, tk[("ob_store", NSTRIP - 1)])
    return tk[("ob_store", NSTRIP - 1)]


def emit_barrier(nc, c):
    marks = c.allmarks()
    with nc.Block() as block:
        @block.tensor
        def _(t):
            c.wait(t, marks)

        @block.vector
        def _(v):
            c.wait(v, marks)

        @block.scalar
        def _(s):
            c.wait(s, marks)

        @block.gpsimd
        def _(g):
            c.wait(g, marks)

        @block.sync
        def _(sy):
            c.wait(sy, marks)
    return marks



def emit_ffn0(nc, c, st, pfx, xln_dr, wg_dr, wu_dr, wd_dr, sgT_dr, suT_dr, sdT_dr,
              xres_dr, out_dr, fuse_dr):
    IT = 64
    with contextlib.ExitStack() as stA:
        xln = stA.enter_context(nc.sbuf_tensor(pfx + "xln", [128, KT * NTOK], BF16))
        wg = [stA.enter_context(nc.sbuf_tensor(pfx + f"wg{i}", [128, KT * 128], BF16)) for i in range(2)]
        wu = [stA.enter_context(nc.sbuf_tensor(pfx + f"wu{i}", [128, KT * 128], BF16)) for i in range(2)]
        gbuf = stA.enter_context(nc.sbuf_tensor(pfx + "gbuf", [128, NTOK], BF16))
        ubuf = stA.enter_context(nc.sbuf_tensor(pfx + "ubuf", [128, NTOK], BF16))
        fbuf = stA.enter_context(nc.sbuf_tensor(pfx + "fbuf", [128, NTOK], BF16))
        sg = stA.enter_context(nc.sbuf_tensor(pfx + "sg", [128, IT], F32))
        su = stA.enter_context(nc.sbuf_tensor(pfx + "su", [128, IT], F32))
        psg = [stA.enter_context(nc.psum_tensor(pfx + f"psg{i}", [128, 512], F32)) for i in range(2)]
        psu = [stA.enter_context(nc.psum_tensor(pfx + f"psu{i}", [128, 512], F32)) for i in range(2)]

        xln_t = xln_dr.rearrange("(kt p) n -> p kt n", p=128)
        wg_t = wg_dr.rearrange("(kt p) (it m) -> p kt it m", p=128, m=128)
        wu_t = wu_dr.rearrange("(kt p) (it m) -> p kt it m", p=128, m=128)
        fuse_t = fuse_dr.rearrange("(it p) n -> p it n", p=128)

        tk = {}
        with nc.Block() as block:
            @block.gpsimd
            def _(g):
                tk["xln"] = [c.dinc("fa_in", g.dma_start(
                    out=xln[:, q * 4 * NTOK:(q + 1) * 4 * NTOK].rearrange("p (kt n) -> p kt n", n=NTOK),
                    in_=xln_t[:, q * 4:(q + 1) * 4, :])) for q in range(4)]
                tk["sg"] = c.dinc("fa_in", g.dma_start(out=sg[:, :], in_=sgT_dr[:, :]))
                tk["su"] = c.dinc("fa_in", g.dma_start(out=su[:, :], in_=suT_dr[:, :]))
                tk["fa_in"] = Tok("D_fa_in", c.cnt["D_fa_in"])

            for it in range(IT):
                buf = it % 2

                @block.gpsimd
                def _(g, it=it, buf=buf):
                    c.wait(g, tk.get(("mmlast", it - 2)))
                    t1 = c.dinc(f"fa_wg{buf}", g.dma_start(out=wg[buf].ap().rearrange(
                        "p (kt m) -> p kt m", m=128), in_=wg_t[:, :, it, :]))
                    t2 = c.dinc(f"fa_wu{buf}", g.dma_start(out=wu[buf].ap().rearrange(
                        "p (kt m) -> p kt m", m=128), in_=wu_t[:, :, it, :]))
                    tk[("w", it)] = [t1, t2]

                for s in range(NSTRIP):
                    @block.tensor
                    def _(t, it=it, buf=buf, s=s):
                        if s == 0:
                            c.wait(t, tk[("w", it)], tk["fa_in"])
                        c.wait(t, tk.get(("evA", it, s - 2)) or
                               tk.get(("evA", it - 1, NSTRIP - 2 + s)))
                        bk = s % 2
                        for k in range(KT):
                            t.matmul(psg[bk][:, :], wg[buf][:, k * 128:(k + 1) * 128],
                                     xln[:, k * NTOK + s * 512: k * NTOK + (s + 1) * 512],
                                     start=(k == 0), stop=(k == KT - 1))
                        for k in range(KT):
                            last = t.matmul(psu[bk][:, :], wu[buf][:, k * 128:(k + 1) * 128],
                                            xln[:, k * NTOK + s * 512: k * NTOK + (s + 1) * 512],
                                            start=(k == 0), stop=(k == KT - 1))
                        tk[("mm", it, s)] = c.inc("tensor", last)
                        if s == NSTRIP - 1:
                            tk[("mmlast", it)] = tk[("mm", it, s)]

                    @block.scalar
                    def _(sc, it=it, buf=buf, s=s):
                        if s == 0:
                            c.wait(sc, tk["fa_in"], tk.get(("fuse", it - 1)))
                        c.wait(sc, tk[("mm", it, s)])
                        bk = s % 2
                        e1 = c.inc("scalar", sc.activation(
                            gbuf[:, s * 512:(s + 1) * 512], psg[bk][:, :], AF.Silu,
                            scale=sg[:, it:it + 1]))
                        e2 = c.inc("scalar", sc.activation(
                            ubuf[:, s * 512:(s + 1) * 512], psu[bk][:, :], AF.Copy,
                            scale=su[:, it:it + 1]))
                        tk[("evA", it, s)] = [e1, e2]

                @block.vector
                def _(v, it=it):
                    c.wait(v, tk[("evA", it, NSTRIP - 1)], tk.get(("fstore", it - 1)))
                    tk[("fuse", it)] = c.inc("vector", v.tensor_mul(fbuf[:, :], gbuf[:, :], ubuf[:, :]))

                @block.sync
                def _(sy, it=it):
                    c.wait(sy, tk[("fuse", it)])
                    tk[("fstore", it)] = c.dinc("fa_f", sy.dma_start(
                        out=fuse_t[:, it, :], in_=fbuf[:, :]))

            @block.sync
            def _(sy):
                c.wait(sy, tk[("fstore", IT - 1)])

    emit_barrier(nc, c)

    with contextlib.ExitStack() as stB:
        DKT = 64
        fsb = stB.enter_context(nc.sbuf_tensor(pfx + "fsb", [128, DKT * 512], BF16))
        xres = stB.enter_context(nc.sbuf_tensor(pfx + "xres", [128, KT * 512], F32))
        osb = stB.enter_context(nc.sbuf_tensor(pfx + "osb", [128, KT * 512], F32))
        wd = [stB.enter_context(nc.sbuf_tensor(pfx + f"wd{i}", [128, 4 * 8 * 128], BF16)) for i in range(2)]
        sd = stB.enter_context(nc.sbuf_tensor(pfx + "sd", [128, KT], F32))
        psd = [stB.enter_context(nc.psum_tensor(pfx + f"psd{i}", [128, 512], F32)) for i in range(8)]

        fuse_s = fuse_dr.rearrange("(kt p) (s n) -> p kt s n", p=128, n=512)
        x_t = xres_dr.rearrange("(kt p) (s n) -> p kt s n", p=128, n=512)
        o_t = out_dr.rearrange("(kt p) (s n) -> p kt s n", p=128, n=512)
        wd_t = wd_dr.rearrange("(kt p) (mg m) -> p kt mg m", p=128, m=8 * 128)

        tk = {}
        mm_by_j = {}
        with nc.Block() as block:
            @block.gpsimd
            def _(g):
                tk["sd"] = c.dinc("fb_in", g.dma_start(out=sd[:, :], in_=sdT_dr[:, :]))

            for s in range(NSTRIP):
                @block.sync
                def _(sy, s=s):
                    c.wait(sy, tk.get(("ost", s - 1)))
                    c.wait(sy, tk.get(("mmall", s - 1)))
                    tk[("fld", s)] = c.dinc("fb_f", sy.dma_start(
                        out=fsb.ap().rearrange("p (kt n) -> p kt n", n=512),
                        in_=fuse_s[:, :, s, :]))
                    tk[("xld", s)] = c.dinc("fb_x", sy.dma_start(
                        out=xres.ap().rearrange("p (kt n) -> p kt n", n=512),
                        in_=x_t[:, :, s, :]))

                for mg in range(2):
                    for kb in range(0, DKT, 4):
                        j = ((s * 2 + mg) * DKT + kb) // 4

                        @block.gpsimd
                        def _(g, s=s, mg=mg, kb=kb, j=j):
                            c.wait(g, mm_by_j.get(j - 2))
                            tk[("wd", s, mg, kb)] = c.dinc(f"fb_wd{(kb // 4) % 2}", g.dma_start(
                                out=wd[(kb // 4) % 2].ap().rearrange("p (k4 m) -> p k4 m", m=1024),
                                in_=wd_t[:, kb:kb + 4, mg, :]))

                        @block.tensor
                        def _(t, s=s, mg=mg, kb=kb, j=j):
                            c.wait(t, tk[("wd", s, mg, kb)], tk[("fld", s)])
                            if kb == 0:
                                c.wait(t, tk.get(("ev", s, mg - 1)), tk.get(("ev", s - 1, 1)))
                            last = None
                            for k in range(kb, kb + 4):
                                for m in range(8):
                                    last = t.matmul(
                                        psd[m][:, :],
                                        wd[(kb // 4) % 2][:, (k - kb) * 1024 + m * 128:
                                                          (k - kb) * 1024 + (m + 1) * 128],
                                        fsb[:, k * 512:(k + 1) * 512],
                                        start=(k == 0), stop=(k == DKT - 1))
                            mm_by_j[j] = tk[("mmk", s, mg, kb)] = c.inc("tensor", last)

                    @block.vector
                    def _(v, s=s, mg=mg):
                        c.wait(v, tk[("mmk", s, mg, DKT - 4)], tk[("xld", s)], tk["sd"])
                        ev = []
                        for m in range(8):
                            mt = mg * 8 + m
                            ev.append(c.inc("vector", v.scalar_tensor_tensor(
                                osb[:, mt * 512:(mt + 1) * 512], psd[m][:, :],
                                sd[:, mt:mt + 1], xres[:, mt * 512:(mt + 1) * 512],
                                ALU.mult, ALU.add)))
                        tk[("ev", s, mg)] = ev
                tk[("mmall", s)] = tk[("mmk", s, 1, DKT - 4)]

                @block.sync
                def _(sy, s=s):
                    c.wait(sy, tk[("ev", s, 0)], tk[("ev", s, 1)])
                    tk[("ost", s)] = c.dinc("fb_o", sy.dma_start(
                        out=o_t[:, :, s, :],
                        in_=osb.ap().rearrange("p (kt n) -> p kt n", n=512)))

            @block.sync
            def _(sy):
                c.wait(sy, tk[("ost", NSTRIP - 1)])
    return



def emit_moe(nc, c, st, pfx, xln_dr, xln32_dr, gw_dr, fc1_dr, fc1bT_dr, fc2_dr, fc2b_dr,
             alphaR_dr, xres_dr, out_dr, onesrow, ident, ctok):
    NE = 8
    NTT = NTOK // 128
    combTok = st.enter_context(nc.sbuf_tensor(pfx + "combTok", [128, NTOK // 128 * 8], BF16))
    ps_t = st.enter_context(nc.psum_tensor(pfx + "pstr", [1, 128], BF16))
    with contextlib.ExitStack() as stA:
        xls = stA.enter_context(nc.sbuf_tensor(pfx + "xls", [128, KT * 512], F32))
        gw = stA.enter_context(nc.sbuf_tensor(pfx + "gw", [128, KT * NE], F32))
        eb = stA.enter_context(nc.sbuf_tensor(pfx + "eb", [128, NE], F32))
        em = stA.enter_context(nc.sbuf_tensor(pfx + "em", [128, NE], F32))
        c1 = stA.enter_context(nc.sbuf_tensor(pfx + "c1", [128, NE], F32))
        c2 = stA.enter_context(nc.sbuf_tensor(pfx + "c2", [128, NE], F32))
        rowsA = stA.enter_context(nc.sbuf_tensor(pfx + "rowsA", [128, 8], F32))
        alr = stA.enter_context(nc.sbuf_tensor(pfx + "alr", [128, NE], F32))
        combB = stA.enter_context(nc.sbuf_tensor(pfx + "combB", [128, NE], BF16))
        ps_g = stA.enter_context(nc.psum_tensor(pfx + "psgt", [128, NE], F32))

        NM, M2, DEN, S2, M1, TP, S1 = 0, 1, 2, 3, 4, 5, 6

        gw_t = gw_dr.rearrange("(kt p) e -> p kt e", p=128)
        xln_t = xln_dr.rearrange("(kt p) (tt n) -> p kt tt n", p=128, n=128)

        tk = {}
        with nc.Block() as block:
            @block.gpsimd
            def _(g):
                t1 = c.dinc(pfx + "g_in", g.dma_start(out=gw[:, :], in_=gw_t[:, :, :]))
                t2 = c.dinc(pfx + "g_in", g.dma_start(out=alr[:, :], in_=alphaR_dr[:, :]))
                tk["gin"] = Tok("D_" + pfx + "g_in", c.cnt["D_" + pfx + "g_in"])

            for tt in range(NTT):
                sidx, toff = tt // 4, (tt % 4) * 128

                if tt % 4 == 0:
                    @block.gpsimd
                    def _(g, sidx=sidx):
                        c.wait(g, tk.get(("gmm", sidx * 4 - 1)))
                        tk[("xld", sidx)] = c.dinc(pfx + "g_x", g.dma_start(
                            out=xls.ap().rearrange("p (kt n) -> p kt n", n=512),
                            in_=xln32_dr.rearrange("(kt p) n -> p kt n", p=128)[:, :, sidx * 512:(sidx + 1) * 512]))

                @block.tensor
                def _(t, tt=tt, sidx=sidx, toff=toff):
                    c.wait(t, tk[("xld", sidx)], tk["gin"])
                    c.wait(t, tk.get(("comb", tt - 1)))
                    for k in range(KT):
                        last = t.matmul(ps_g[:, :], xls[:, k * 512 + toff: k * 512 + toff + 128],
                                        gw[:, k * NE:(k + 1) * NE], start=(k == 0), stop=(k == KT - 1))
                    tk[("gmm", tt)] = c.inc("tensor", last)

                @block.vector
                def _(v, tt=tt):
                    c.wait(v, tk[("gmm", tt)])
                    t1 = c.inc("vector", v.reduce_max(rowsA[:, M1:M1 + 1], ps_g[:, :], AX.X))
                    c.wait(v, t1)
                    t2 = c.inc("vector", v.tensor_single_scalar(
                        c1[:, :], ps_g[:, :], rowsA[:, M1:M1 + 1], ALU.is_ge))
                    t3 = c.inc("vector", v.tensor_scalar(
                        rowsA[:, NM:NM + 1], rowsA[:, M1:M1 + 1], -1.0, None, ALU.mult))
                    tk[("nm", tt)] = [t2, t3]

                @block.scalar
                def _(sc, tt=tt):
                    c.wait(sc, tk[("nm", tt)])
                    tk[("exp", tt)] = c.inc("scalar", sc.activation(
                        eb[:, :], ps_g[:, :], AF.Exp, bias=rowsA[:, NM:NM + 1]))

                @block.vector
                def _(v, tt=tt):
                    c.wait(v, tk[("exp", tt)])
                    t1 = c.inc("vector", v.tensor_mul(em[:, :], eb[:, :], c1[:, :]))
                    c.wait(v, t1)
                    t2 = c.inc("vector", v.tensor_sub(em[:, :], eb[:, :], em[:, :]))
                    t2b = c.inc("vector", v.reduce_max(rowsA[:, TP:TP + 1], eb[:, :], AX.X))
                    c.wait(v, t2, t2b)
                    t3 = c.inc("vector", v.reduce_max(rowsA[:, M2:M2 + 1], em[:, :], AX.X))
                    c.wait(v, t3)
                    t4 = c.inc("vector", v.tensor_add(
                        rowsA[:, DEN:DEN + 1], rowsA[:, TP:TP + 1], rowsA[:, M2:M2 + 1]))
                    c.wait(v, t4)
                    t5 = c.inc("vector", v.reciprocal(rowsA[:, DEN:DEN + 1], rowsA[:, DEN:DEN + 1]))
                    c.wait(v, t5)
                    t6 = c.inc("vector", v.tensor_mul(rowsA[:, S2:S2 + 1], rowsA[:, M2:M2 + 1],
                                                      rowsA[:, DEN:DEN + 1]))
                    t6b = c.inc("vector", v.tensor_mul(rowsA[:, S1:S1 + 1], rowsA[:, TP:TP + 1],
                                                       rowsA[:, DEN:DEN + 1]))
                    t7 = c.inc("vector", v.tensor_single_scalar(c2[:, :], em[:, :],
                                                                rowsA[:, M2:M2 + 1], ALU.is_ge))
                    c.wait(v, t6, t6b, t7)
                    t8 = c.inc("vector", v.tensor_scalar(c1[:, :], c1[:, :],
                                                         rowsA[:, S1:S1 + 1], None, ALU.mult))
                    t9 = c.inc("vector", v.tensor_scalar(c2[:, :], c2[:, :],
                                                         rowsA[:, S2:S2 + 1], None, ALU.mult))
                    c.wait(v, t8, t9)
                    ta = c.inc("vector", v.tensor_add(c1[:, :], c1[:, :], c2[:, :]))
                    c.wait(v, ta)
                    tb = c.inc("vector", v.tensor_mul(c1[:, :], c1[:, :], alr[:, :]))
                    c.wait(v, tb)
                    tk[("comb", tt)] = c.inc("vector", v.tensor_copy(
                        combTok[:, tt * 8:(tt + 1) * 8], c1[:, :]))

            @block.sync
            def _(sy):
                c.wait(sy, tk[("comb", NTT - 1)])
            tk["combdone_"] = tk[("comb", NTT - 1)]

    bar_tk = tk["combdone_"]
    emit_barrier(nc, c)

    with nc.Block() as block:
        @block.sync
        def _(sy):
            t = c.dinc(pfx + "init", sy.dma_start(out=out_dr[:, :], in_=xres_dr[:, :]))
            c.wait(sy, t)
    init_tok = Tok("D_" + pfx + "init", 16)

    with contextlib.ExitStack() as stB:
        f1 = stB.enter_context(nc.sbuf_tensor(pfx + "f1", [128, KT * D], BF16))
        f2 = stB.enter_context(nc.sbuf_tensor(pfx + "f2", [128, KT * D], BF16))
        b1 = stB.enter_context(nc.sbuf_tensor(pfx + "b1", [128, KT], F32))
        b2r = stB.enter_context(nc.sbuf_tensor(pfx + "b2r", [1, D], BF16))
        xls = stB.enter_context(nc.sbuf_tensor(pfx + "xlsB", [128, KT * 512], BF16))
        hb = stB.enter_context(nc.sbuf_tensor(pfx + "hb", [128, KT * 512], BF16))
        rep = stB.enter_context(nc.sbuf_tensor(pfx + "rep", [128, 512], BF16))
        crow = stB.enter_context(nc.sbuf_tensor(pfx + "crow", [1, 512], BF16))
        osb = stB.enter_context(nc.sbuf_tensor(pfx + "osb", [128, KT * 512], F32))
        ps1 = [stB.enter_context(nc.psum_tensor(pfx + f"ps1{i}", [128, 512], F32)) for i in range(2)]
        ps2 = [stB.enter_context(nc.psum_tensor(pfx + f"ps2{i}", [128, 512], F32)) for i in range(2)]
        ps_r = stB.enter_context(nc.psum_tensor(pfx + "psr", [128, 512], F32))

        f1_t = fc1_dr.rearrange("e (kt p) m -> e p kt m", p=128)
        f2_t = fc2_dr.rearrange("e (kt p) m -> e p kt m", p=128)
        xln_t = xln_dr.rearrange("(kt p) n -> p kt n", p=128)
        out_t = out_dr.rearrange("(kt p) (s n) -> p kt s n", p=128, n=512)

        tk = {"combdone": bar_tk}
        with nc.Block() as block:
            for e in range(NE):
                @block.gpsimd
                def _(g, e=e):
                    c.wait(g, tk.get(("mm2last", e - 1)))
                    w1 = [c.dinc(pfx + "wf1", g.dma_start(
                        out=f1[:, q * 4 * D:(q + 1) * 4 * D].rearrange("p (kt m) -> p kt m", m=D),
                        in_=f1_t[e, :, q * 4:(q + 1) * 4, :])) for q in range(4)]
                    w2 = [c.dinc(pfx + "wf2", g.dma_start(
                        out=f2[:, q * 4 * D:(q + 1) * 4 * D].rearrange("p (kt m) -> p kt m", m=D),
                        in_=f2_t[e, :, q * 4:(q + 1) * 4, :])) for q in range(4)]
                    wb = [c.dinc(pfx + "wb1", g.dma_start(out=b1[:, :], in_=fc1bT_dr[e, :, :])),
                          c.dinc(pfx + "wb2", g.dma_start(out=b2r[:, :], in_=fc2b_dr[e, :][None, :]))]
                    tk[("w", e)] = [Tok("D_" + pfx + "wf1", c.cnt["D_" + pfx + "wf1"]),
                                    Tok("D_" + pfx + "wf2", c.cnt["D_" + pfx + "wf2"]),
                                    Tok("D_" + pfx + "wb1", c.cnt["D_" + pfx + "wb1"]),
                                    Tok("D_" + pfx + "wb2", c.cnt["D_" + pfx + "wb2"])]

                for s in range(NSTRIP):
                    @block.gpsimd
                    def _(g, e=e, s=s):
                        c.wait(g, tk.get(("mm1last", e, s - 1)) or tk.get(("mm1last", e - 1, NSTRIP - 1)))
                        tk[("xld", e, s)] = c.dinc(pfx + "m_x", g.dma_start(
                            out=xls.ap().rearrange("p (kt n) -> p kt n", n=512),
                            in_=xln_t[:, :, s * 512:(s + 1) * 512]))

                    for q in range(4):
                        @block.tensor
                        def _(t, e=e, s=s, q=q):
                            c.wait(t, tk["combdone"], ctok)
                            c.wait(t, tk.get(("crowev", e, s, q - 1))
                                   or tk.get(("reptr", e, s - 1))
                                   or tk.get(("reptr", e - 1, NSTRIP - 1)))
                            tk[("crowtr", e, s, q)] = c.inc("tensor", t.transpose(
                                ps_t[:, :], combTok[:, (s * 4 + q) * 8 + e:(s * 4 + q) * 8 + e + 1],
                                ident[:, :]))

                        @block.vector
                        def _(v, e=e, s=s, q=q):
                            c.wait(v, tk[("crowtr", e, s, q)])
                            if q == 0:
                                c.wait(v, tk.get(("reptr", e, s - 1))
                                       or tk.get(("reptr", e - 1, NSTRIP - 1)))
                            tk[("crowev", e, s, q)] = c.inc("vector", v.tensor_copy(
                                crow[0:1, q * 128:(q + 1) * 128], ps_t[0:1, :]))

                    @block.tensor
                    def _(t, e=e, s=s):
                        c.wait(t, [tk[("crowev", e, s, q)] for q in range(4)])
                        c.wait(t, tk.get(("repev", e, s - 1)) or tk.get(("repev", e - 1, NSTRIP - 1)))
                        tk[("reptr", e, s)] = c.inc("tensor", t.matmul(
                            ps_r[:, :], onesrow[:, :], crow[0:1, :], start=True, stop=True))

                    @block.vector
                    def _(v, e=e, s=s):
                        c.wait(v, tk[("reptr", e, s)])
                        tk[("repev", e, s)] = c.inc("vector", v.tensor_copy(rep[:, :], ps_r[:, :]))

                    for it in range(KT):
                        @block.tensor
                        def _(t, e=e, s=s, it=it):
                            c.wait(t, tk[("xld", e, s)], tk[("w", e)])
                            c.wait(t, tk.get(("h", e, s, it - 2)))
                            for k in range(KT):
                                last = t.matmul(ps1[it % 2][:, :], f1[:, k * D + it * 128: k * D + (it + 1) * 128],
                                                xls[:, k * 512:(k + 1) * 512],
                                                start=(k == 0), stop=(k == KT - 1))
                            tk[("mm1", e, s, it)] = c.inc("tensor", last)

                        @block.scalar
                        def _(sc, e=e, s=s, it=it):
                            c.wait(sc, tk[("mm1", e, s, it)])
                            c.wait(sc, tk.get(("mm2last", e, s - 1)) or tk.get(("mm2last", e - 1)))
                            tk[("gel", e, s, it)] = c.inc("scalar", sc.activation(
                                hb[:, it * 512:(it + 1) * 512], ps1[it % 2][:, :], AF.Gelu,
                                bias=b1[:, it:it + 1]))

                        @block.vector
                        def _(v, e=e, s=s, it=it):
                            c.wait(v, tk[("gel", e, s, it)], tk[("repev", e, s)])
                            tk[("h", e, s, it)] = c.inc("vector", v.tensor_mul(
                                hb[:, it * 512:(it + 1) * 512], hb[:, it * 512:(it + 1) * 512], rep[:, :]))

                    for m in range(KT):
                        @block.tensor
                        def _(t, e=e, s=s, m=m):
                            c.wait(t, [tk[("h", e, s, it)] for it in range(KT)])
                            c.wait(t, tk.get(("oev", e, s, m - 2)))
                            t.matmul(ps2[m % 2][:, :], b2r[0:1, m * 128:(m + 1) * 128],
                                     crow[0:1, :], start=True, stop=False)
                            for k in range(KT):
                                last = t.matmul(ps2[m % 2][:, :], f2[:, k * D + m * 128: k * D + (m + 1) * 128],
                                                hb[:, k * 512:(k + 1) * 512],
                                                start=False, stop=(k == KT - 1))
                            tk[("mm2", e, s, m)] = c.inc("tensor", last)

                        @block.vector
                        def _(v, e=e, s=s, m=m):
                            c.wait(v, tk[("mm2", e, s, m)])
                            if m == 0:
                                c.wait(v, tk.get(("acc", e, s - 1)) or tk.get(("acc", e - 1, NSTRIP - 1)))
                            tk[("oev", e, s, m)] = c.inc("vector", v.tensor_copy(
                                osb[:, m * 512:(m + 1) * 512], ps2[m % 2][:, :]))

                    tk[("mm1last", e, s)] = tk[("mm1", e, s, KT - 1)]
                    tk[("mm2last", e, s)] = tk[("mm2", e, s, KT - 1)]

                    @block.gpsimd
                    def _(g, e=e, s=s):
                        c.wait(g, [tk[("oev", e, s, m)] for m in range(KT)])
                        if e == 0:
                            c.wait(g, init_tok)
                        else:
                            c.wait(g, tk.get(("acc", e - 1, s)))
                        tk[("acc", e, s)] = c.dinc(pfx + f"acc{s}", g.dma_start(
                            out=out_t[:, :, s, :],
                            in_=osb.ap().rearrange("p (kt n) -> p kt n", n=512),
                            accum_op=ALU.add))

                tk[("mm2last", e)] = tk[("mm2last", e, NSTRIP - 1)]

            @block.sync
            def _(sy):
                c.wait(sy, [tk[("acc", NE - 1, s)] for s in range(NSTRIP)])
    return



def emit_gemm(nc, c, st, pfx, wT_dr, IN, OUT, x_dr, out_dr, residual_dr=None,
              scale_fn=None, in_dtype=BF16):
    KTI = IN // 128
    MT = OUT // 128
    with contextlib.ExitStack() as stA:
        xsb = stA.enter_context(nc.sbuf_tensor(pfx + "x", [128, KTI * NTOK], BF16))
        wt = [stA.enter_context(nc.sbuf_tensor(pfx + f"w{i}", [128, KTI * 128], BF16))
              for i in range(2)]
        ob = stA.enter_context(nc.sbuf_tensor(pfx + "ob", [128, NTOK],
                                              F32 if residual_dr is not None else BF16))
        res = [stA.enter_context(nc.sbuf_tensor(pfx + f"r{i}", [128, NTOK], F32))
               for i in range(2)] if residual_dr is not None else None
        ps = [stA.enter_context(nc.psum_tensor(pfx + f"ps{i}", [128, 512], F32))
              for i in range(4)]

        x_t = x_dr.rearrange("(kt p) n -> p kt n", p=128)
        w_t = wT_dr.rearrange("(kt p) (mt m) -> p kt mt m", p=128, m=128)
        o_t = out_dr.rearrange("(mt p) n -> p mt n", p=128)
        r_t = residual_dr.rearrange("(mt p) n -> p mt n", p=128) if residual_dr is not None else None

        tk = {}
        with nc.Block() as block:
            @block.gpsimd
            def _(g):
                nq = max(1, KTI // 4)
                tk["x"] = [c.dinc(pfx + "x", g.dma_start(
                    out=xsb[:, q * (KTI // nq) * NTOK:(q + 1) * (KTI // nq) * NTOK].rearrange(
                        "p (kt n) -> p kt n", n=NTOK),
                    in_=x_t[:, q * (KTI // nq):(q + 1) * (KTI // nq), :])) for q in range(nq)]
                tk["x"] = Tok("D_" + pfx + "x", c.cnt["D_" + pfx + "x"])

            for mt in range(MT):
                buf = mt % 2

                @block.gpsimd
                def _(g, mt=mt, buf=buf):
                    c.wait(g, tk.get(("mml", mt - 2)))
                    tk[("w", mt)] = c.dinc(pfx + f"w{buf}", g.dma_start(
                        out=wt[buf].ap().rearrange("p (kt m) -> p kt m", m=128),
                        in_=w_t[:, :, mt, :]))
                    if res is not None:
                        c.wait(g, tk.get(("ev", mt - 2, NSTRIP - 1)))
                        tk[("r", mt)] = c.dinc(pfx + f"r{buf}", g.dma_start(
                            out=res[buf].ap(), in_=r_t[:, mt, :]))

                for s in range(NSTRIP):
                    @block.tensor
                    def _(t, mt=mt, buf=buf, s=s):
                        if s == 0:
                            c.wait(t, tk[("w", mt)], tk["x"])
                        bank = (mt * NSTRIP + s) % 4
                        c.wait(t, tk.get(("ev", mt, s - 4)) or
                               (tk.get(("ev", mt - 1, NSTRIP + s - 4)) if s < 4 else None))
                        for k in range(KTI):
                            last = t.matmul(ps[bank][:, :], wt[buf][:, k * 128:(k + 1) * 128],
                                            xsb[:, k * NTOK + s * 512: k * NTOK + (s + 1) * 512],
                                            start=(k == 0), stop=(k == KTI - 1))
                        tk[("mm", mt, s)] = c.inc("tensor", last)
                        if s == NSTRIP - 1:
                            tk[("mml", mt)] = tk[("mm", mt, s)]

                    if res is not None:
                        @block.vector
                        def _(v, mt=mt, buf=buf, s=s):
                            bank = (mt * NSTRIP + s) % 4
                            c.wait(v, tk[("mm", mt, s)], tk.get(("r", mt)))
                            if s == 0:
                                c.wait(v, tk.get(("ost", mt - 1)))
                            sc = 1.0 if scale_fn is None else scale_fn(mt)
                            tk[("ev", mt, s)] = c.inc("vector", v.scalar_tensor_tensor(
                                ob[:, s * 512:(s + 1) * 512], ps[bank][:, :], sc,
                                res[buf][:, s * 512:(s + 1) * 512], ALU.mult, ALU.add))
                    else:
                        @block.scalar
                        def _(sc_, mt=mt, s=s):
                            bank = (mt * NSTRIP + s) % 4
                            c.wait(sc_, tk[("mm", mt, s)])
                            if s == 0:
                                c.wait(sc_, tk.get(("ost", mt - 1)))
                            sc = 1.0 if scale_fn is None else scale_fn(mt)
                            tk[("ev", mt, s)] = c.inc("scalar", sc_.mul(
                                ob[:, s * 512:(s + 1) * 512], ps[bank][:, :], sc))

                @block.sync
                def _(sy, mt=mt):
                    c.wait(sy, [tk[("ev", mt, s)] for s in range(NSTRIP)])
                    tk[("ost", mt)] = c.dinc(pfx + "o", sy.dma_start(
                        out=o_t[:, mt, :], in_=ob.ap()))

            @block.sync
            def _(sy):
                c.wait(sy, tk[("ost", MT - 1)])
    return



def emit_attn(nc, c, st, pfx, qkv_dr, out_dr, ident, ctok):
    NH = 16
    QT = T // 128
    with contextlib.ExitStack() as stA:
        qt_ = stA.enter_context(nc.sbuf_tensor(pfx + "q", [128, T], BF16))
        kt_ = stA.enter_context(nc.sbuf_tensor(pfx + "k", [128, T], BF16))
        vr = stA.enter_context(nc.sbuf_tensor(pfx + "v", [128, T], BF16))
        pb = stA.enter_context(nc.sbuf_tensor(pfx + "pb", [128, T], BF16))
        pbT = stA.enter_context(nc.sbuf_tensor(pfx + "pbT", [128, T], BF16))
        at = stA.enter_context(nc.sbuf_tensor(pfx + "at", [128, T], BF16))
        rows = stA.enter_context(nc.sbuf_tensor(pfx + "rows", [128, 2], F32))
        mctT = stA.enter_context(nc.sbuf_tensor(pfx + "mctT", [128, 128], BF16))
        ps_s = stA.enter_context(nc.psum_tensor(pfx + "pss", [128, T], F32))
        ps_tr = [stA.enter_context(nc.psum_tensor(pfx + f"pstr{i}", [128, 128], BF16))
                 for i in range(2)]
        ps_a = stA.enter_context(nc.psum_tensor(pfx + "psa", [128, 128], F32))

        tk = {}
        with nc.Block() as block:
            @block.gpsimd
            def _(g):
                t0 = c.inc("gpsimd", g.memset(mctT[:, :], 0.0))
                c.wait(g, t0)
                tk["mask"] = c.inc("gpsimd", g.affine_select(
                    out=mctT[:, :], in_=mctT[:, :], compare_op=ALU.is_ge,
                    fill=-1e30, base=0, pattern=[[1, 128]], channel_multiplier=-1))

            for b in range(BPC):
                for h in range(NH):
                    bh = b * NH + h

                    @block.gpsimd
                    def _(g, b=b, h=h, bh=bh):
                        c.wait(g, tk.get(("aev", bh - 1, QT - 1)))
                        tk[("lq", bh)] = c.dinc(pfx + "lq", g.dma_start(
                            out=qt_[:, :], in_=qkv_dr[h * 128:(h + 1) * 128, b * T:(b + 1) * T]))
                        tk[("lk", bh)] = c.dinc(pfx + "lk", g.dma_start(
                            out=kt_[:, :], in_=qkv_dr[2048 + h * 128:2048 + (h + 1) * 128,
                                                      b * T:(b + 1) * T]))

                    @block.scalar
                    def _(sc, b=b, h=h, bh=bh):
                        c.wait(sc, tk.get(("amml", bh - 1)))
                        for j in range(QT):
                            c.dinc(pfx + "lv", sc.dma_start(
                                out=vr[:, j * 128:(j + 1) * 128],
                                in_=qkv_dr[4096 + h * 128:4096 + (h + 1) * 128,
                                           b * T + j * 128: b * T + (j + 1) * 128],
                                transpose=True))
                        tk[("lv", bh)] = Tok("D_" + pfx + "lv", c.cnt["D_" + pfx + "lv"])

                    for q in range(QT):
                        kend = (q + 1) * 128

                        @block.tensor
                        def _(t, bh=bh, q=q):
                            c.wait(t, tk[("lq", bh)], tk[("lk", bh)], tk["mask"], ctok)
                            c.wait(t, tk.get(("exp", bh, q - 1)) or tk.get(("exp", bh - 1, QT - 1)))
                            for ks in range(0, kend, 512):
                                n = min(512, kend - ks)
                                t.matmul(ps_s[:, ks:ks + n], qt_[:, q * 128:(q + 1) * 128],
                                         kt_[:, ks:ks + n], start=True, stop=(ks + n < kend))
                            last = t.matmul(ps_s[:, q * 128:kend], mctT[:, :], ident[:, :],
                                            start=False, stop=True)
                            tk[("smm", bh, q)] = c.inc("tensor", last)

                        @block.vector
                        def _(v, bh=bh, q=q):
                            c.wait(v, tk[("smm", bh, q)])
                            tk[("nm", bh, q)] = c.inc("vector", v.reduce_max(
                                rows[:, 0:1], ps_s[:, 0:kend], AX.X, negate=True))

                        @block.scalar
                        def _(sc, bh=bh, q=q):
                            c.wait(sc, tk[("nm", bh, q)])
                            c.wait(sc, tk.get(("trl", bh, q - 1)) or tk.get(("trl", bh - 1, QT - 1)))
                            tk[("exp", bh, q)] = c.inc("scalar", sc.activation(
                                pb[:, 0:kend], ps_s[:, 0:kend], AF.Exp, bias=rows[:, 0:1],
                                accum_out=rows[:, 1:2]))

                        @block.vector
                        def _(v, bh=bh, q=q):
                            c.wait(v, tk[("exp", bh, q)])
                            t1 = c.inc("vector", v.reciprocal(rows[:, 1:2], rows[:, 1:2]))
                            c.wait(v, t1)
                            tk[("pscale", bh, q)] = c.inc("vector", v.tensor_scalar(
                                pb[:, 0:kend], pb[:, 0:kend], rows[:, 1:2], None, ALU.mult))

                        for j in range(q + 1):
                            @block.tensor
                            def _(t, bh=bh, q=q, j=j):
                                c.wait(t, tk[("pscale", bh, q)])
                                c.wait(t, tk.get(("trev", bh, q, j - 2)) or
                                       tk.get(("trev", bh, q - 1, q - 1 - (1 - j))) or
                                       tk.get(("trev", bh - 1, QT - 1, QT - 2 + j)))
                                tk[("tr", bh, q, j)] = c.inc("tensor", t.transpose(
                                    ps_tr[j % 2][:, :], pb[:, j * 128:(j + 1) * 128], ident[:, :]))
                                if j == q:
                                    tk[("trl", bh, q)] = tk[("tr", bh, q, j)]

                            @block.vector
                            def _(v, bh=bh, q=q, j=j):
                                c.wait(v, tk[("tr", bh, q, j)])
                                if j <= 1:
                                    c.wait(v, tk.get(("amml", bh, q - 1)) or
                                           tk.get(("amml", bh - 1, QT - 1)))
                                tk[("trev", bh, q, j)] = c.inc("vector", v.tensor_copy(
                                    pbT[:, j * 128:(j + 1) * 128], ps_tr[j % 2][:, :]))

                            @block.tensor
                            def _(t, bh=bh, q=q, j=j):
                                c.wait(t, tk[("trev", bh, q, j)], tk[("lv", bh)])
                                if j == 0:
                                    c.wait(t, tk.get(("aev", bh, q - 1)) or
                                           tk.get(("aev", bh - 1, QT - 1)))
                                tk[("amm", bh, q, j)] = c.inc("tensor", t.matmul(
                                    ps_a[:, :], vr[:, j * 128:(j + 1) * 128],
                                    pbT[:, j * 128:(j + 1) * 128],
                                    start=(j == 0), stop=(j == q)))
                                if j == q:
                                    tk[("amml", bh, q)] = tk[("amm", bh, q, j)]
                                    if q == QT - 1:
                                        tk[("amml", bh)] = tk[("amm", bh, q, j)]

                        @block.scalar
                        def _(sc, bh=bh, q=q):
                            c.wait(sc, tk[("amml", bh, q)])
                            if q == 0:
                                c.wait(sc, tk.get(("ast", bh - 1)))
                            tk[("aev", bh, q)] = c.inc("scalar", sc.copy(
                                at[:, q * 128:(q + 1) * 128], ps_a[:, :]))

                    @block.sync
                    def _(sy, b=b, h=h, bh=bh):
                        c.wait(sy, [tk[("aev", bh, q)] for q in range(QT)])
                        tk[("ast", bh)] = c.dinc(pfx + "ao", sy.dma_start(
                            out=out_dr[h * 128:(h + 1) * 128, b * T:(b + 1) * T],
                            in_=at[:, :]))

            @block.sync
            def _(sy):
                c.wait(sy, tk[("ast", BPC * NH - 1)])
    return



def emit_gemm(nc, c, st, pfx, wT_dr, IN, OUT, x_dr, out_dr, residual_dr=None,
              scale_fn=None, in_dtype=BF16):
    KTI = IN // 128
    MT = OUT // 128
    with contextlib.ExitStack() as stA:
        xsb = stA.enter_context(nc.sbuf_tensor(pfx + "x", [128, KTI * NTOK], BF16))
        wt = [stA.enter_context(nc.sbuf_tensor(pfx + f"w{i}", [128, KTI * 128], BF16))
              for i in range(2)]
        ob = stA.enter_context(nc.sbuf_tensor(pfx + "ob", [128, NTOK],
                                              F32 if residual_dr is not None else BF16))
        res = [stA.enter_context(nc.sbuf_tensor(pfx + f"r{i}", [128, NTOK], F32))
               for i in range(2)] if residual_dr is not None else None
        ps = [stA.enter_context(nc.psum_tensor(pfx + f"ps{i}", [128, 512], F32))
              for i in range(4)]

        x_t = x_dr.rearrange("(kt p) n -> p kt n", p=128)
        w_t = wT_dr.rearrange("(kt p) (mt m) -> p kt mt m", p=128, m=128)
        o_t = out_dr.rearrange("(mt p) n -> p mt n", p=128)
        r_t = residual_dr.rearrange("(mt p) n -> p mt n", p=128) if residual_dr is not None else None

        tk = {}
        with nc.Block() as block:
            @block.gpsimd
            def _(g):
                nq = max(1, KTI // 4)
                tk["x"] = [c.dinc(pfx + "x", g.dma_start(
                    out=xsb[:, q * (KTI // nq) * NTOK:(q + 1) * (KTI // nq) * NTOK].rearrange(
                        "p (kt n) -> p kt n", n=NTOK),
                    in_=x_t[:, q * (KTI // nq):(q + 1) * (KTI // nq), :])) for q in range(nq)]
                tk["x"] = Tok("D_" + pfx + "x", c.cnt["D_" + pfx + "x"])

            for mt in range(MT):
                buf = mt % 2

                @block.gpsimd
                def _(g, mt=mt, buf=buf):
                    c.wait(g, tk.get(("mml", mt - 2)))
                    tk[("w", mt)] = c.dinc(pfx + f"w{buf}", g.dma_start(
                        out=wt[buf].ap().rearrange("p (kt m) -> p kt m", m=128),
                        in_=w_t[:, :, mt, :]))
                    if res is not None:
                        c.wait(g, tk.get(("ev", mt - 2, NSTRIP - 1)))
                        tk[("r", mt)] = c.dinc(pfx + f"r{buf}", g.dma_start(
                            out=res[buf].ap(), in_=r_t[:, mt, :]))

                for s in range(NSTRIP):
                    @block.tensor
                    def _(t, mt=mt, buf=buf, s=s):
                        if s == 0:
                            c.wait(t, tk[("w", mt)], tk["x"])
                        bank = (mt * NSTRIP + s) % 4
                        c.wait(t, tk.get(("ev", mt, s - 4)) or
                               (tk.get(("ev", mt - 1, NSTRIP + s - 4)) if s < 4 else None))
                        for k in range(KTI):
                            last = t.matmul(ps[bank][:, :], wt[buf][:, k * 128:(k + 1) * 128],
                                            xsb[:, k * NTOK + s * 512: k * NTOK + (s + 1) * 512],
                                            start=(k == 0), stop=(k == KTI - 1))
                        tk[("mm", mt, s)] = c.inc("tensor", last)
                        if s == NSTRIP - 1:
                            tk[("mml", mt)] = tk[("mm", mt, s)]

                    if res is not None:
                        @block.vector
                        def _(v, mt=mt, buf=buf, s=s):
                            bank = (mt * NSTRIP + s) % 4
                            c.wait(v, tk[("mm", mt, s)], tk.get(("r", mt)))
                            if s == 0:
                                c.wait(v, tk.get(("ost", mt - 1)))
                            sc = 1.0 if scale_fn is None else scale_fn(mt)
                            tk[("ev", mt, s)] = c.inc("vector", v.scalar_tensor_tensor(
                                ob[:, s * 512:(s + 1) * 512], ps[bank][:, :], sc,
                                res[buf][:, s * 512:(s + 1) * 512], ALU.mult, ALU.add))
                    else:
                        @block.scalar
                        def _(sc_, mt=mt, s=s):
                            bank = (mt * NSTRIP + s) % 4
                            c.wait(sc_, tk[("mm", mt, s)])
                            if s == 0:
                                c.wait(sc_, tk.get(("ost", mt - 1)))
                            sc = 1.0 if scale_fn is None else scale_fn(mt)
                            tk[("ev", mt, s)] = c.inc("scalar", sc_.mul(
                                ob[:, s * 512:(s + 1) * 512], ps[bank][:, :], sc))

                @block.sync
                def _(sy, mt=mt):
                    c.wait(sy, [tk[("ev", mt, s)] for s in range(NSTRIP)])
                    tk[("ost", mt)] = c.dinc(pfx + "o", sy.dma_start(
                        out=o_t[:, mt, :], in_=ob.ap()))

            @block.sync
            def _(sy):
                c.wait(sy, tk[("ost", MT - 1)])
    return



def emit_attn(nc, c, st, pfx, qkv_dr, out_dr, ident, ctok):
    NH = 16
    QT = T // 128
    with contextlib.ExitStack() as stA:
        qt_ = stA.enter_context(nc.sbuf_tensor(pfx + "q", [128, T], BF16))
        kt_ = stA.enter_context(nc.sbuf_tensor(pfx + "k", [128, T], BF16))
        vr = stA.enter_context(nc.sbuf_tensor(pfx + "v", [128, T], BF16))
        pb = stA.enter_context(nc.sbuf_tensor(pfx + "pb", [128, T], BF16))
        pbT = stA.enter_context(nc.sbuf_tensor(pfx + "pbT", [128, T], BF16))
        at = stA.enter_context(nc.sbuf_tensor(pfx + "at", [128, T], BF16))
        rows = stA.enter_context(nc.sbuf_tensor(pfx + "rows", [128, 2], F32))
        maskc = stA.enter_context(nc.sbuf_tensor(pfx + "maskc", [128, 128], F32))
        ps_s = stA.enter_context(nc.psum_tensor(pfx + "pss", [128, T], F32))
        ps_tr = stA.enter_context(nc.psum_tensor(pfx + "pstr", [128, 128], BF16))
        ps_a = stA.enter_context(nc.psum_tensor(pfx + "psa", [128, 128], F32))

        tk = {}
        with nc.Block() as block:
            @block.gpsimd
            def _(g):
                t0 = c.inc("gpsimd", g.memset(maskc[:, :], 0.0))
                c.wait(g, t0)
                tk["mask"] = c.inc("gpsimd", g.affine_select(
                    out=maskc[:, :], in_=maskc[:, :], compare_op=ALU.is_ge,
                    fill=-1e30, base=0, pattern=[[-1, 128]], channel_multiplier=1))

            for b in range(BPC):
                for h in range(NH):
                    bh = b * NH + h

                    @block.gpsimd
                    def _(g, b=b, h=h, bh=bh):
                        c.wait(g, tk.get(("aev", bh - 1, QT - 1)))
                        tk[("lq", bh)] = c.dinc(pfx + "lq", g.dma_start(
                            out=qt_[:, :], in_=qkv_dr[h * 128:(h + 1) * 128, b * T:(b + 1) * T]))
                        tk[("lk", bh)] = c.dinc(pfx + "lk", g.dma_start(
                            out=kt_[:, :], in_=qkv_dr[2048 + h * 128:2048 + (h + 1) * 128,
                                                      b * T:(b + 1) * T]))

                    @block.sync
                    def _(sy, b=b, h=h, bh=bh):
                        c.wait(sy, tk.get(("amml", bh - 1)))
                        vt = []
                        for j in range(QT):
                            vt.append(c.dinc(pfx + "lv", sy.dma_start(
                                out=vr[:, j * 128:(j + 1) * 128],
                                in_=qkv_dr[4096 + h * 128:4096 + (h + 1) * 128,
                                           b * T + j * 128: b * T + (j + 1) * 128],
                                transpose=True)))
                        tk[("lv", bh)] = Tok("D_" + pfx + "lv", c.cnt["D_" + pfx + "lv"])

                    for q in range(QT):
                        kend = (q + 1) * 128

                        @block.tensor
                        def _(t, bh=bh, q=q):
                            c.wait(t, tk[("lq", bh)], tk[("lk", bh)])
                            c.wait(t, tk.get(("exp", bh, q - 1)) or tk.get(("exp", bh - 1, QT - 1)))
                            last = None
                            for ks in range(0, kend, 512):
                                n = min(512, kend - ks)
                                last = t.matmul(ps_s[:, ks:ks + n], qt_[:, q * 128:(q + 1) * 128],
                                                kt_[:, ks:ks + n], start=True, stop=True)
                            tk[("smm", bh, q)] = c.inc("tensor", last)

                        @block.vector
                        def _(v, bh=bh, q=q):
                            c.wait(v, tk[("smm", bh, q)], tk["mask"])
                            t1 = c.inc("vector", v.tensor_add(
                                ps_s[:, q * 128:(q + 1) * 128], ps_s[:, q * 128:(q + 1) * 128],
                                maskc[:, :]))
                            c.wait(v, t1)
                            tk[("nm", bh, q)] = c.inc("vector", v.reduce_max(
                                rows[:, 0:1], ps_s[:, 0:kend], AX.X, negate=True))

                        @block.scalar
                        def _(sc, bh=bh, q=q):
                            c.wait(sc, tk[("nm", bh, q)])
                            c.wait(sc, tk.get(("trl", bh, q - 1)) or tk.get(("trl", bh - 1, QT - 1)))
                            tk[("exp", bh, q)] = c.inc("scalar", sc.activation(
                                pb[:, 0:kend], ps_s[:, 0:kend], AF.Exp, bias=rows[:, 0:1],
                                accum_out=rows[:, 1:2]))

                        @block.vector
                        def _(v, bh=bh, q=q):
                            c.wait(v, tk[("exp", bh, q)])
                            t1 = c.inc("vector", v.reciprocal(rows[:, 1:2], rows[:, 1:2]))
                            c.wait(v, t1)
                            tk[("pscale", bh, q)] = c.inc("vector", v.tensor_scalar(
                                pb[:, 0:kend], pb[:, 0:kend], rows[:, 1:2], None, ALU.mult))

                        for j in range(q + 1):
                            @block.tensor
                            def _(t, bh=bh, q=q, j=j):
                                c.wait(t, tk[("pscale", bh, q)])
                                c.wait(t, tk.get(("trev", bh, q, j - 1)) or
                                       tk.get(("trev", bh, q - 1, q - 1)) or
                                       tk.get(("trev", bh - 1, QT - 1, QT - 1)))
                                tk[("tr", bh, q, j)] = c.inc("tensor", t.transpose(
                                    ps_tr[:, :], pb[:, j * 128:(j + 1) * 128], ident[:, :]))
                                if j == q:
                                    tk[("trl", bh, q)] = tk[("tr", bh, q, j)]

                            @block.vector
                            def _(v, bh=bh, q=q, j=j):
                                c.wait(v, tk[("tr", bh, q, j)])
                                if j == 0:
                                    c.wait(v, tk.get(("amml", bh, q - 1)) or
                                           tk.get(("amml", bh - 1, QT - 1)))
                                tk[("trev", bh, q, j)] = c.inc("vector", v.tensor_copy(
                                    pbT[:, j * 128:(j + 1) * 128], ps_tr[:, :]))

                        @block.tensor
                        def _(t, bh=bh, q=q):
                            c.wait(t, [tk[("trev", bh, q, j)] for j in range(q + 1)],
                                   tk[("lv", bh)])
                            c.wait(t, tk.get(("aev", bh, q - 1)) or tk.get(("aev", bh - 1, QT - 1)))
                            for j in range(q + 1):
                                last = t.matmul(ps_a[:, :], vr[:, j * 128:(j + 1) * 128],
                                                pbT[:, j * 128:(j + 1) * 128],
                                                start=(j == 0), stop=(j == q))
                            tk[("amm", bh, q)] = c.inc("tensor", last)
                            if q == QT - 1:
                                tk[("amml", bh)] = tk[("amm", bh, q)]
                            tk[("amml", bh, q)] = tk[("amm", bh, q)]

                        @block.vector
                        def _(v, bh=bh, q=q):
                            c.wait(v, tk[("amm", bh, q)])
                            if q == 0:
                                c.wait(v, tk.get(("ast", bh - 1)))
                            tk[("aev", bh, q)] = c.inc("vector", v.tensor_copy(
                                at[:, q * 128:(q + 1) * 128], ps_a[:, :]))

                    @block.sync
                    def _(sy, b=b, h=h, bh=bh):
                        c.wait(sy, [tk[("aev", bh, q)] for q in range(QT)])
                        tk[("ast", bh)] = c.dinc(pfx + "ao", sy.dma_start(
                            out=out_dr[h * 128:(h + 1) * 128, b * T:(b + 1) * T],
                            in_=at[:, :]))

            @block.sync
            def _(sy):
                c.wait(sy, tk[("ast", BPC * NH - 1)])
    return



def emit_l0proj(nc, c, st, pfx, xln_dr, wqT, wkT, wvT, convq, convk, convv,
                wf1T, wf2T, dtbT_dr, alog_dr, wbT, wg1T, bg1_dr, wg2T, bg2T_dr,
                qn_dr, kni_dr, vni_dr, la_dr, beta_dr, gate_dr,
                onescol, onesrow, cmap, ctok, identb=None):
    SCALE_NONE = None
    xln = st.enter_context(nc.sbuf_tensor(pfx + "xln", [128, KT * NTOK], BF16))
    xln_t = xln_dr.rearrange("(kt p) n -> p kt n", p=128)
    tk = {}
    with nc.Block() as block:
        @block.gpsimd
        def _(g):
            for q in range(4):
                c.dinc(pfx + "xin", g.dma_start(
                    out=xln[:, q * 4 * NTOK:(q + 1) * 4 * NTOK].rearrange("p (kt n) -> p kt n", n=NTOK),
                    in_=xln_t[:, q * 4:(q + 1) * 4, :]))
            tk["x"] = Tok("D_" + pfx + "xin", c.cnt["D_" + pfx + "xin"])

    with contextlib.ExitStack() as s1:
        wf1s = s1.enter_context(nc.sbuf_tensor(pfx + "wf1", [128, KT * 128], BF16))
        wf2s = s1.enter_context(nc.sbuf_tensor(pfx + "wf2", [128, D], BF16))
        wbs = s1.enter_context(nc.sbuf_tensor(pfx + "wbs", [128, KT * 64], BF16))
        wg1s = s1.enter_context(nc.sbuf_tensor(pfx + "wg1", [128, KT * 128], BF16))
        wg2s = s1.enter_context(nc.sbuf_tensor(pfx + "wg2", [128, D], BF16))
        dtb = s1.enter_context(nc.sbuf_tensor(pfx + "dtb", [128, 16], F32))
        bg2 = s1.enter_context(nc.sbuf_tensor(pfx + "bg2", [128, 16], F32))
        bg1 = s1.enter_context(nc.sbuf_tensor(pfx + "bg1", [128, 1], F32))
        alr = s1.enter_context(nc.sbuf_tensor(pfx + "alr", [1, 32], F32))
        alrb = s1.enter_context(nc.sbuf_tensor(pfx + "alrb", [1, 16], BF16))
        calog = s1.enter_context(nc.sbuf_tensor(pfx + "calog", [128, 16], F32))
        t1 = s1.enter_context(nc.sbuf_tensor(pfx + "t1", [128, NTOK], BF16))
        t2 = s1.enter_context(nc.sbuf_tensor(pfx + "t2", [128, NTOK], BF16))
        spb = s1.enter_context(nc.sbuf_tensor(pfx + "spb", [128, 512 * 4], F32))
        lab = s1.enter_context(nc.sbuf_tensor(pfx + "lab", [128, NTOK], F32))
        gb = s1.enter_context(nc.sbuf_tensor(pfx + "gb", [128, NTOK], BF16))
        btb = s1.enter_context(nc.sbuf_tensor(pfx + "btb", [64, NTOK], BF16))
        psA = [s1.enter_context(nc.psum_tensor(pfx + f"psA{i}", [128, 512], F32)) for i in range(2)]
        psB = [s1.enter_context(nc.psum_tensor(pfx + f"psB{i}", [128, 512], F32)) for i in range(2)]
        psC = [s1.enter_context(nc.psum_tensor(pfx + f"psC{i}", [64, 512], F32)) for i in range(2)]
        psr = s1.enter_context(nc.psum_tensor(pfx + "psr", [128, 16], F32))

        with nc.Block() as block:
            @block.gpsimd
            def _(g):
                c.dinc(pfx + "s1w", g.dma_start(
                    out=wf1s.ap().rearrange("p (kt m) -> p kt m", m=128),
                    in_=wf1T.rearrange("(kt p) m -> p kt m", p=128)))
                c.dinc(pfx + "s1w", g.dma_start(out=wf2s[:, :], in_=wf2T[:, :]))
                c.dinc(pfx + "s1w", g.dma_start(
                    out=wbs.ap().rearrange("p (kt m) -> p kt m", m=64),
                    in_=wbT.rearrange("(kt p) m -> p kt m", p=128)))
                c.dinc(pfx + "s1w", g.dma_start(
                    out=wg1s.ap().rearrange("p (kt m) -> p kt m", m=128),
                    in_=wg1T.rearrange("(kt p) m -> p kt m", p=128)))
                c.dinc(pfx + "s1w", g.dma_start(out=wg2s[:, :], in_=wg2T[:, :]))
                c.dinc(pfx + "s1w", g.dma_start(out=dtb[:, :], in_=dtbT_dr[:, :]))
                c.dinc(pfx + "s1w", g.dma_start(out=bg2[:, :], in_=bg2T_dr[:, :]))
                c.dinc(pfx + "s1w", g.dma_start(out=bg1[:, :], in_=bg1_dr[:, None]))
                c.dinc(pfx + "s1w", g.dma_start(out=alr[0:1, 0:16], in_=alog_dr[None, :]))
                tk["s1w"] = Tok("D_" + pfx + "s1w", c.cnt["D_" + pfx + "s1w"])

            @block.scalar
            def _(sc):
                c.wait(sc, tk["s1w"])
                tk["al1"] = c.inc("scalar", sc.activation(alr[0:1, 16:32], alr[0:1, 0:16], AF.Exp))

            @block.vector
            def _(v):
                c.wait(v, tk["al1"])
                t_ = c.inc("vector", v.tensor_scalar(alr[0:1, 16:32], alr[0:1, 16:32], -1.0, None, ALU.mult))
                c.wait(v, t_)
                tk["al2"] = c.inc("vector", v.tensor_copy(alrb[0:1, :], alr[0:1, 16:32]))

            @block.tensor
            def _(t):
                c.wait(t, tk["al2"], ctok)
                tk["al3"] = c.inc("tensor", t.matmul(psr[:, :], onesrow[:, :], alrb[0:1, :],
                                                     start=True, stop=True))

            @block.vector
            def _(v):
                c.wait(v, tk["al3"])
                tk["calog"] = c.inc("vector", v.tensor_copy(calog[:, :], psr[:, :]))

            for s in range(NSTRIP):
                @block.tensor
                def _(t, s=s):
                    c.wait(t, tk["s1w"], tk["x"])
                    c.wait(t, tk.get(("ev1", s - 2)), tk.get(("ev2", s - 2)), tk.get(("evb", s - 2)))
                    for k in range(KT):
                        l1 = t.matmul(psA[s % 2][:, :], wf1s[:, k * 128:(k + 1) * 128],
                                      xln[:, k * NTOK + s * 512: k * NTOK + (s + 1) * 512],
                                      start=(k == 0), stop=(k == KT - 1))
                    tk[("mm1", s)] = c.inc("tensor", l1)
                    for k in range(KT):
                        l2 = t.matmul(psB[s % 2][:, :], wg1s[:, k * 128:(k + 1) * 128],
                                      xln[:, k * NTOK + s * 512: k * NTOK + (s + 1) * 512],
                                      start=(k == 0), stop=(k == KT - 1))
                    tk[("mm2", s)] = c.inc("tensor", l2)
                    for k in range(KT):
                        l3 = t.matmul(psC[s % 2][:, :], wbs[:, k * 64:(k + 1) * 64],
                                      xln[:, k * NTOK + s * 512: k * NTOK + (s + 1) * 512],
                                      start=(k == 0), stop=(k == KT - 1))
                    tk[("mm3", s)] = c.inc("tensor", l3)

                @block.scalar
                def _(sc, s=s):
                    c.wait(sc, tk[("mm1", s)])
                    tk[("ev1", s)] = c.inc("scalar", sc.copy(t1[:, s * 512:(s + 1) * 512], psA[s % 2][:, :]))
                    c.wait(sc, tk[("mm2", s)])
                    tk[("ev2", s)] = c.inc("scalar", sc.activation(
                        t2[:, s * 512:(s + 1) * 512], psB[s % 2][:, :], AF.Identity, bias=bg1[:, 0:1]))
                    c.wait(sc, tk[("mm3", s)])
                    tk[("evb", s)] = c.inc("scalar", sc.activation(
                        btb[0:64, s * 512:(s + 1) * 512], psC[s % 2][0:64, :], AF.Sigmoid))

            @block.sync
            def _(sy):
                c.wait(sy, [tk[("evb", s)] for s in range(NSTRIP)])
                tk["bst"] = c.dinc(pfx + "bst", sy.dma_start(out=beta_dr[:, :], in_=btb[:, :]))

            for h in range(16):
                for s in range(NSTRIP):
                    @block.tensor
                    def _(t, h=h, s=s):
                        c.wait(t, [tk[("ev1", ss)] for ss in range(NSTRIP)])
                        c.wait(t, tk.get(("spl", h, s - 2)) or tk.get(("spl", h - 1, NSTRIP - 2 + s)))
                        tk[("mla", h, s)] = c.inc("tensor", t.matmul(
                            psA[s % 2][:, :], wf2s[:, h * 128:(h + 1) * 128],
                            t1[:, s * 512:(s + 1) * 512], start=True, stop=True))

                    @block.scalar
                    def _(sc, h=h, s=s):
                        c.wait(sc, tk[("mla", h, s)])
                        c.wait(sc, tk.get(("laf", h, s - 1)) or tk.get(("laf", h - 1, NSTRIP - 1)))
                        e1 = c.inc("scalar", sc.activation(
                            spb[:, 0:512], psA[s % 2][:, :], AF.Relu, bias=dtb[:, h:h + 1]))
                        e2 = c.inc("scalar", sc.activation(
                            spb[:, 512:1024], psA[s % 2][:, :], AF.Abs, bias=dtb[:, h:h + 1]))
                        c.wait(sc, e2)
                        e3 = c.inc("scalar", sc.activation(
                            spb[:, 1024:1536], spb[:, 512:1024], AF.Exp, scale=-1.0))
                        c.wait(sc, e3)
                        e4 = c.inc("scalar", sc.activation(
                            spb[:, 1536:2048], spb[:, 1024:1536], AF.Ln, bias=cmap[1.0]))
                        tk[("spl", h, s)] = [e1, e4]

                    @block.vector
                    def _(v, h=h, s=s):
                        c.wait(v, tk[("spl", h, s)], tk["calog"])
                        t_ = c.inc("vector", v.tensor_add(spb[:, 0:512], spb[:, 0:512],
                                                          spb[:, 1536:2048]))
                        c.wait(v, t_)
                        c.wait(v, tk.get(("last", h - 1)))
                        tk[("laf", h, s)] = c.inc("vector", v.tensor_scalar(
                            lab[:, s * 512:(s + 1) * 512], spb[:, 0:512],
                            calog[:, h:h + 1], None, ALU.mult))

                @block.sync
                def _(sy, h=h):
                    c.wait(sy, [tk[("laf", h, s)] for s in range(NSTRIP)])
                    tk[("last", h)] = c.dinc(pfx + "last", sy.dma_start(
                        out=la_dr[h, :, :], in_=lab[:, :]))

            for h in range(16):
                for s in range(NSTRIP):
                    @block.tensor
                    def _(t, h=h, s=s):
                        c.wait(t, [tk[("ev2", ss)] for ss in range(NSTRIP)])
                        c.wait(t, tk.get(("gev", h, s - 2)) or tk.get(("gev", h - 1, NSTRIP - 2 + s)))
                        tk[("mg", h, s)] = c.inc("tensor", t.matmul(
                            psB[s % 2][:, :], wg2s[:, h * 128:(h + 1) * 128],
                            t2[:, s * 512:(s + 1) * 512], start=True, stop=True))

                    @block.scalar
                    def _(sc, h=h, s=s):
                        c.wait(sc, tk[("mg", h, s)])
                        if s == 0:
                            c.wait(sc, tk.get(("gst", h - 1)))
                        tk[("gev", h, s)] = c.inc("scalar", sc.activation(
                            gb[:, s * 512:(s + 1) * 512], psB[s % 2][:, :], AF.Identity,
                            bias=bg2[:, h:h + 1]))

                @block.sync
                def _(sy, h=h):
                    c.wait(sy, [tk[("gev", h, s)] for s in range(NSTRIP)])
                    tk[("gst", h)] = c.dinc(pfx + "gst", sy.dma_start(
                        out=gate_dr[h * 128:(h + 1) * 128, :], in_=gb[:, :]))

    emit_barrier(nc, c)

    with contextlib.ExitStack() as s2:
        wpl = [s2.enter_context(nc.sbuf_tensor(pfx + f"wpl{i}", [128, KT * 128], BF16)) for i in range(2)]
        cvw = s2.enter_context(nc.sbuf_tensor(pfx + "cvw", [128, (16 + 64 + 64) * 4], F32))
        dgj = [s2.enter_context(nc.sbuf_tensor(pfx + f"dgj{i}", [128, 4 * 128], BF16))
               for i in range(2)]
        praw2 = [s2.enter_context(nc.sbuf_tensor(pfx + f"praw{i}", [128, NTOK + 8], BF16))
                 for i in range(2)]
        pcv = s2.enter_context(nc.sbuf_tensor(pfx + "pcv", [128, NTOK], BF16))
        kist = s2.enter_context(nc.sbuf_tensor(pfx + "kist", [128, 2 * 8192], BF16))
        rowb = s2.enter_context(nc.sbuf_tensor(pfx + "rowb", [1, NTOK], BF16))
        pspl = [s2.enter_context(nc.psum_tensor(pfx + f"pspl{i}", [128, 512], F32)) for i in range(4)]
        psn = s2.enter_context(nc.psum_tensor(pfx + "psn", [1, NTOK // 4], F32))
        psrep = [s2.enter_context(nc.psum_tensor(pfx + f"psrep{i}", [128, 512], F32)) for i in range(2)]

        with nc.Block() as block:
            @block.gpsimd
            def _(g):
                c.dinc(pfx + "cv", g.dma_start(
                    out=cvw[:, 0:64].rearrange("p (pl j) -> p pl j", j=4),
                    in_=convq.rearrange("(pl p) j -> p pl j", p=128)))
                c.dinc(pfx + "cv", g.dma_start(
                    out=cvw[:, 64:320].rearrange("p (pl j) -> p pl j", j=4),
                    in_=convk.rearrange("(pl p) j -> p pl j", p=128)))
                c.dinc(pfx + "cv", g.dma_start(
                    out=cvw[:, 320:576].rearrange("p (pl j) -> p pl j", j=4),
                    in_=convv.rearrange("(pl p) j -> p pl j", p=128)))
                tk["cv"] = Tok("D_" + pfx + "cv", c.cnt["D_" + pfx + "cv"])
                z1 = c.inc("gpsimd", g.memset(praw2[0][:, 0:4], 0.0))
                z2 = c.inc("gpsimd", g.memset(praw2[0][:, 2052:2056], 0.0))
                z3 = c.inc("gpsimd", g.memset(praw2[1][:, 0:4], 0.0))
                z4 = c.inc("gpsimd", g.memset(praw2[1][:, 2052:2056], 0.0))
                tk["pz"] = [z1, z2, z3, z4]

            planes = []
            for h in range(16):
                planes.append(("q", h, 0, wqT, h, h))
            for h in range(16):
                for i in range(4):
                    planes.append(("k", h, i, wkT, h * 4 + i, 16 + h * 4 + i))
                for i in range(4):
                    planes.append(("v", h, i, wvT, h * 4 + i, 80 + h * 4 + i))

            pending = []
            for ip, (kind, h, i, wT, mcol, ccol) in enumerate(planes):
                buf = ip % 2
                praw = praw2[ip % 2]
                w_t = wT.rearrange("(kt p) (mt m) -> p kt mt m", p=128, m=128)

                @block.gpsimd
                def _(g, ip=ip, buf=buf, w_t=w_t, mcol=mcol):
                    c.wait(g, tk.get(("pmml", ip - 2)))
                    tk[("wld", ip)] = c.dinc(pfx + f"wpl{buf}", g.dma_start(
                        out=wpl[buf].ap().rearrange("p (kt m) -> p kt m", m=128),
                        in_=w_t[:, :, mcol, :]))

                for s in range(NSTRIP):
                    @block.tensor
                    def _(t, ip=ip, buf=buf, s=s):
                        c.wait(t, tk[("wld", ip)], tk["x"])
                        c.wait(t, tk.get(("prev", ip, s - 2)) or tk.get(("prev", ip - 1, NSTRIP - 2 + s)))
                        for k in range(KT):
                            last = t.matmul(pspl[s % 2][:, :], wpl[buf][:, k * 128:(k + 1) * 128],
                                            xln[:, k * NTOK + s * 512: k * NTOK + (s + 1) * 512],
                                            start=(k == 0), stop=(k == KT - 1))
                        tk[("pmm", ip, s)] = c.inc("tensor", last)
                        if s == NSTRIP - 1:
                            tk[("pmml", ip)] = tk[("pmm", ip, s)]

                    @block.scalar
                    def _(sc, ip=ip, s=s):
                        c.wait(sc, tk[("pmm", ip, s)])
                        if s == 0:
                            c.wait(sc, tk.get(("praw_free", ip - 2)))
                        b_ = s // 4
                        off = 4 + b_ * 2052 + (s % 4) * 512
                        tk[("prev", ip, s)] = c.inc("scalar", sc.copy(
                            praw[:, off:off + 512], pspl[s % 2][:, :]))

                while pending:
                    pending.pop(0)()

                @block.vector
                def _(v, ip=ip, ccol=ccol):
                    c.wait(v, tk["cv"], ctok)
                    c.wait(v, tk.get(("convmm", ip - 2)))
                    last = None
                    for j in range(4):
                        last = c.inc("vector", v.tensor_scalar(
                            dgj[ip % 2][:, j * 128:(j + 1) * 128], identb[:, :],
                            cvw[:, ccol * 4 + j:ccol * 4 + j + 1], None, ALU.mult))
                    tk[("dg", ip)] = last

                for s8 in range(NSTRIP):
                    @block.tensor
                    def _(t, ip=ip, s8=s8):
                        b = s8 // 4
                        off = 4 + b * 2052 + (s8 % 4) * 512
                        c.wait(t, tk[("dg", ip)], tk[("prev", ip, s8)], tk["pz"])
                        c.wait(t, tk.get(("cvev", ip, s8 - 2)) or
                               tk.get(("cvev", ip - 1, NSTRIP - 2 + s8)))
                        last = None
                        for j in range(4):
                            sh = 3 - j
                            last = t.matmul(pspl[2 + s8 % 2][:, :],
                                            dgj[ip % 2][:, j * 128:(j + 1) * 128],
                                            praw[:, off - sh:off - sh + 512],
                                            start=(j == 0), stop=(j == 3))
                        tk[("cvmm", ip, s8)] = c.inc("tensor", last)

                    @block.scalar
                    def _(sc, ip=ip, s8=s8):
                        c.wait(sc, tk[("cvmm", ip, s8)])
                        if s8 == 0:
                            c.wait(sc, tk.get(("sq", ip - 1)))
                            c.wait(sc, tk.get(("pncp", ip - 1)), tk.get(("pst", ip - 1)))
                        tk[("cvev", ip, s8)] = c.inc("scalar", sc.activation(
                            pcv[:, s8 * 512:(s8 + 1) * 512], pspl[2 + s8 % 2][:, :], AF.Silu))
                tk[("conv", ip)] = [tk[("cvev", ip, s8)] for s8 in range(NSTRIP)]
                tk[("convmm", ip)] = tk[("cvmm", ip, NSTRIP - 1)]
                tk[("silu", ip)] = tk[("conv", ip)]

                if kind == "v":
                    tk[("praw_free", ip)] = tk[("convmm", ip)]

                def deferred(ip=ip, kind=kind, h=h, i=i, ccol=ccol, praw=praw, pcv=pcv):
                    if kind in ("q", "k"):
                        @block.vector
                        def _(v, ip=ip):
                            c.wait(v, tk[("silu", ip)])
                            c.inc("vector", v.tensor_mul(
                                praw[:, 4:4 + T], pcv[:, 0:T], pcv[:, 0:T]))
                            tk[("sq", ip)] = c.inc("vector", v.tensor_mul(
                                praw[:, 2056:2056 + T], pcv[:, T:2 * T], pcv[:, T:2 * T]))

                        for hf in range(4):
                            @block.tensor
                            def _(t, ip=ip, hf=hf):
                                if hf == 0:
                                    c.wait(t, tk[("sq", ip)], ctok)
                                c.wait(t, tk.get(("nrm", ip, hf - 1)) or tk.get(("nrm", ip - 1, 3)))
                                last = None
                                for s_ in range(2 * hf, 2 * hf + 2):
                                    offq = 4 + (s_ // 4) * 2052 + (s_ % 4) * 512
                                    last = t.matmul(psn[0:1, (s_ - 2 * hf) * 512:(s_ - 2 * hf + 1) * 512],
                                                    onescol[:, :], praw[:, offq:offq + 512],
                                                    start=True, stop=True)
                                tk[("ssm", ip, hf)] = c.inc("tensor", last)
                                if hf == 3:
                                    tk[("praw_free", ip)] = tk[("ssm", ip, hf)]

                            @block.scalar
                            def _(sc, ip=ip, hf=hf):
                                c.wait(sc, tk[("ssm", ip, hf)])
                                t_ = c.inc("scalar", sc.activation(
                                    psn[0:1, :], psn[0:1, :], AF.Ln, bias=cmap[1e-24][0:1, 0:1]))
                                c.wait(sc, t_)
                                tk[("srt", ip, hf)] = c.inc("scalar", sc.activation(
                                    psn[0:1, :], psn[0:1, :], AF.Exp, scale=-0.5))

                            @block.vector
                            def _(v, ip=ip, hf=hf):
                                c.wait(v, tk[("srt", ip, hf)])
                                tk[("nrm", ip, hf)] = c.inc("vector", v.tensor_copy(
                                    rowb[0:1, hf * 1024:(hf + 1) * 1024], psn[0:1, :]))

                        for s in range(NSTRIP):
                            @block.tensor
                            def _(t, ip=ip, s=s):
                                c.wait(t, tk[("nrm", ip, s // 2)])
                                c.wait(t, tk.get(("pnm", ip, s - 2)) or
                                       tk.get(("pnm", ip - 1, NSTRIP - 2 + s)))
                                tk[("rep", ip, s)] = c.inc("tensor", t.matmul(
                                    psrep[s % 2][:, :], onesrow[:, :], rowb[0:1, s * 512:(s + 1) * 512],
                                    start=True, stop=True))

                            @block.vector
                            def _(v, ip=ip, s=s):
                                c.wait(v, tk[("rep", ip, s)], tk[("conv", ip)], tk[("silu", ip)])
                                if s == 0:
                                    c.wait(v, tk.get(("pncp", ip - 1)) or tk.get(("pst", ip - 1)))
                                tk[("pnm", ip, s)] = c.inc("vector", v.tensor_mul(
                                    pcv[:, s * 512:(s + 1) * 512], pcv[:, s * 512:(s + 1) * 512],
                                    psrep[s % 2][:, :]))
                        src = pcv
                        src_tok = lambda ip=ip: [tk[("pnm", ip, s)] for s in range(NSTRIP)]
                    else:
                        src = pcv
                        src_tok = lambda ip=ip: [tk[("silu", ip)]]

                    if kind == "q":
                        @block.sync
                        def _(sy, ip=ip, h=h, src=src):
                            c.wait(sy, src_tok())
                            tk[("pst", ip)] = c.dinc(pfx + "qst", sy.dma_start(
                                out=qn_dr[h, :, :], in_=src.ap()))
                    else:
                        @block.vector
                        def _(v, ip=ip, i=i, src=src):
                            c.wait(v, src_tok())
                            if i == 0:
                                c.wait(v, tk.get(("ist", ip - 1)) or tk.get(("ist", ip - 4)))
                            ts = []
                            for b in range(BPC):
                                ts.append(c.inc("vector", v.tensor_copy(
                                    kist.ap().rearrange("p (b t i) -> p b t i", b=2, i=4)[:, b, :, i],
                                    src[:, b * T:(b + 1) * T])))
                            tk[("icp", ip)] = ts
                            tk[("pncp", ip)] = ts

                        if i == 3:
                            @block.sync
                            def _(sy, ip=ip, h=h, kind=kind):
                                c.wait(sy, [tk[("icp", ip - d)] for d in range(4)])
                                dst = kni_dr if kind == "k" else vni_dr
                                t0 = c.dinc(pfx + "ist", sy.dma_start(
                                    out=dst[0, h, :, :], in_=kist[:, 0:8192]))
                                t1 = c.dinc(pfx + "ist", sy.dma_start(
                                    out=dst[1, h, :, :], in_=kist[:, 8192:16384]))
                                tk[("ist", ip)] = [t0, t1]


                pending.append(deferred)
            for fn_ in pending:
                fn_()

            @block.sync
            def _(sy):
                c.wait(sy, tk[("ist", len(planes) - 1)])
    return



MKSCALE = float(128.0 ** -0.5)


def emit_mkda(nc, c, st, pfx, qn_dr, kni_dr, vni_dr, la_dr, beta_dr, o_dr,
              mskK_dr, mskQ_dr, onesrow, identb, ctok, bh_list=None):
    NCH = T // 64
    if bh_list is None:
        bh_list = [(b, h) for b in range(BPC) for h in range(16)]
    kI = st.enter_context(nc.sbuf_tensor(pfx + "kI", [128, 4 * T], BF16))
    qnh = st.enter_context(nc.sbuf_tensor(pfx + "qnh", [128, T], BF16))
    laG = st.enter_context(nc.sbuf_tensor(pfx + "laG", [128, T], F32))
    G = st.enter_context(nc.sbuf_tensor(pfx + "G", [128, T], F32))
    tmpB = st.enter_context(nc.sbuf_tensor(pfx + "tmpB", [128, 32 * 48], F32))
    brep = st.enter_context(nc.sbuf_tensor(pfx + "brep", [128, 4 * T], BF16))
    w_ = st.enter_context(nc.sbuf_tensor(pfx + "w", [128, 4 * T], BF16))
    khA = st.enter_context(nc.sbuf_tensor(pfx + "khA", [128, 4 * T], BF16))
    kexp = st.enter_context(nc.sbuf_tensor(pfx + "kexp", [128, 4 * T], BF16))
    kdec = st.enter_context(nc.sbuf_tensor(pfx + "kdec", [128, 4 * T], BF16))
    whA = st.enter_context(nc.sbuf_tensor(pfx + "whA", [128, 4 * T], BF16))
    whB = [None] + [st.enter_context(nc.sbuf_tensor(pfx + f"whB{i}", [128, NCH * 64 * i], BF16))
                    for i in (1, 2, 3)]
    ecB = [None] + [st.enter_context(nc.sbuf_tensor(pfx + f"ecB{i}", [128, NCH * 16 * i], BF16))
                    for i in (1, 2, 3)]
    qhA = st.enter_context(nc.sbuf_tensor(pfx + "qhA", [128, T], BF16))
    qgn = st.enter_context(nc.sbuf_tensor(pfx + "qgn", [128, T], BF16))
    e_r = st.enter_context(nc.sbuf_tensor(pfx + "e_r", [128, T], BF16))
    e_c = st.enter_context(nc.sbuf_tensor(pfx + "e_c", [128, T], BF16))
    e_g = st.enter_context(nc.sbuf_tensor(pfx + "e_g", [128, T], BF16))
    e_d = st.enter_context(nc.sbuf_tensor(pfx + "e_d", [128, T], BF16))
    glr = st.enter_context(nc.sbuf_tensor(pfx + "glr", [128, 2 * NCH], F32))
    negS = st.enter_context(nc.sbuf_tensor(pfx + "negS", [128, 128], F32))
    negSb = st.enter_context(nc.sbuf_tensor(pfx + "negSb", [128, 128], BF16))
    msm = st.enter_context(nc.sbuf_tensor(pfx + "msm", [128, 2 * 320], BF16))
    mskK = st.enter_context(nc.sbuf_tensor(pfx + "mskK", [128, 64], BF16))
    mskQ = st.enter_context(nc.sbuf_tensor(pfx + "mskQ", [128, 16], BF16))
    a0s = [st.enter_context(nc.sbuf_tensor(pfx + f"a0s{i}", [128, 128], BF16)) for i in range(2)]
    aB = [st.enter_context(nc.sbuf_tensor(pfx + f"aB{i}", [128, 128], BF16)) for i in range(2)]
    bL = [None] + [st.enter_context(nc.sbuf_tensor(pfx + f"bL{i}", [128, 128], BF16)) for i in range(1, 7)]
    uB = [st.enter_context(nc.sbuf_tensor(pfx + f"uB{i}", [128, 128], BF16)) for i in range(2)]
    u1f = st.enter_context(nc.sbuf_tensor(pfx + "u1f", [128, 128], BF16))
    vrow = st.enter_context(nc.sbuf_tensor(pfx + "vrow", [128, 256], BF16))
    kdT = st.enter_context(nc.sbuf_tensor(pfx + "kdT", [128, 256], BF16))
    obh = st.enter_context(nc.sbuf_tensor(pfx + "obh", [128, 16 * 128], BF16))

    pm = [st.enter_context(nc.psum_tensor(pfx + f"pm{i}", [128, 320], F32)) for i in range(2)]
    ps_tr = st.enter_context(nc.psum_tensor(pfx + "pstr", [128, 128], BF16))
    psq = st.enter_context(nc.psum_tensor(pfx + "psq", [128, 128], F32))
    pu = [st.enter_context(nc.psum_tensor(pfx + f"pu{i}", [128, 128], F32)) for i in range(2)]
    po = st.enter_context(nc.psum_tensor(pfx + "po", [64, 128], F32))
    psS = st.enter_context(nc.psum_tensor(pfx + "psS", [128, 128], F32))

    def Gv(tsz):
        return G.ap().rearrange("p (c t) -> p c t", t=tsz)

    tk = {}
    with nc.Block() as block:
        @block.gpsimd
        def _(g):
            c.dinc(pfx + "msk", g.dma_start(out=mskK[:, :], in_=mskK_dr[:, :]))
            c.dinc(pfx + "msk", g.dma_start(out=mskQ[:, :], in_=mskQ_dr[:, :]))
            tk["msk"] = Tok("D_" + pfx + "msk", c.cnt["D_" + pfx + "msk"])

        for ibh, (b, h) in enumerate(bh_list):
            @block.gpsimd
            def _(g, b=b, h=h, ibh=ibh):
                c.wait(g, tk.get(("ost", ibh - 1)))
                c.dinc(pfx + "in", g.dma_start(out=kI[:, :], in_=kni_dr[b, h, :, :]))
                c.dinc(pfx + "in", g.dma_start(out=qnh[:, :], in_=qn_dr[h, :, b * T:(b + 1) * T]))
                c.dinc(pfx + "in", g.dma_start(out=laG[:, :], in_=la_dr[h, :, b * T:(b + 1) * T]))
                for i in range(4):
                    c.dinc(pfx + "in", g.dma_start(
                        out=brep[:, i * T:(i + 1) * T],
                        in_=beta_dr[h * 4 + i:h * 4 + i + 1,
                                    b * T:(b + 1) * T].partition_broadcast(128)))
                tk[("in", ibh)] = Tok("D_" + pfx + "in", c.cnt["D_" + pfx + "in"])

            @block.vector
            def _(v, ibh=ibh):
                c.wait(v, tk[("in", ibh)])
                for cc in range(NCH):
                    c.inc("vector", v.tensor_tensor_scan(
                        G[:, cc * 64:(cc + 1) * 64], laG[:, cc * 64:(cc + 1) * 64],
                        laG[:, cc * 64:(cc + 1) * 64], 0.0, ALU.add, ALU.bypass))
                tk[("G", ibh)] = c.mark("vector")
                c.wait(v, tk[("G", ibh)])
                t1 = c.inc("vector", v.tensor_copy(
                    glr[:, 0:NCH].rearrange("p (c o) -> p c o", o=1), Gv(64)[:, :, 63:64]))
                t2 = c.inc("vector", v.tensor_sub(
                    laG.ap().rearrange("p (blk t) -> p blk t", t=16),
                    G.ap().rearrange("p (blk t) -> p blk t", t=16),
                    G.ap().rearrange("p (blk t) -> p blk t", t=16)[:, :, 0:1]
                     .broadcast_to([128, 128, 16])))
                tk[("dpre", ibh)] = [t1, t2]

            @block.scalar
            def _(sc, ibh=ibh):
                c.wait(sc, tk[("dpre", ibh)])
                c.inc("scalar", sc.activation(e_r[:, :], laG[:, :], AF.Exp))
                c.inc("scalar", sc.activation(e_c[:, :], laG[:, :], AF.Exp, scale=-1.0))
                c.inc("scalar", sc.activation(e_g[:, :], G[:, :], AF.Exp))
                c.inc("scalar", sc.activation(glr[:, NCH:2 * NCH], glr[:, 0:NCH], AF.Exp))
                tk[("exp1", ibh)] = c.mark("scalar")

            @block.vector
            def _(v, ibh=ibh):
                c.wait(v, tk[("exp1", ibh)])
                tk[("dd", ibh)] = c.inc("vector", v.tensor_sub(
                    laG.ap().rearrange("p (c t) -> p c t", t=64),
                    Gv(64)[:, :, 63:64].broadcast_to([128, NCH, 64]),
                    Gv(64)))

            @block.scalar
            def _(sc, ibh=ibh):
                c.wait(sc, tk[("dd", ibh)])
                tk[("e_dx", ibh)] = c.inc("scalar", sc.activation(e_d[:, :], laG[:, :], AF.Exp))

            for I in (1, 2, 3):
                @block.vector
                def _(v, ibh=ibh, I=I):
                    c.wait(v, tk[("G", ibh)])
                    if I > 1:
                        c.wait(v, tk[("ecBx", ibh, I - 1)])
                    tk[("dB", ibh, I)] = c.inc("vector", v.tensor_sub(
                        tmpB[:, 0:NCH * 16 * I].rearrange("p (cc s) -> p cc s", s=16 * I),
                        Gv(64)[:, :, 16 * I:16 * I + 1].broadcast_to([128, NCH, 16 * I]),
                        Gv(64)[:, :, 0:16 * I]))

                @block.scalar
                def _(sc, ibh=ibh, I=I):
                    c.wait(sc, tk[("dB", ibh, I)])
                    tk[("ecBx", ibh, I)] = c.inc("scalar", sc.activation(
                        ecB[I][:, :], tmpB[:, 0:NCH * 16 * I], AF.Exp))

            @block.vector
            def _(v, ibh=ibh):
                c.wait(v, tk[("exp1", ibh)], tk[("in", ibh)])
                t1 = c.inc("vector", v.tensor_mul(
                    w_.ap().rearrange("p (t i) -> p t i", i=4),
                    kI.ap().rearrange("p (t i) -> p t i", i=4),
                    brep.ap().rearrange("p (i t) -> p t i", t=T)))
                t2 = c.inc("vector", v.tensor_mul(
                    khA.ap().rearrange("p (t i) -> p t i", i=4),
                    kI.ap().rearrange("p (t i) -> p t i", i=4),
                    e_r.ap().rearrange("p (t o) -> p t o", o=1).broadcast_to([128, T, 4])))
                c.wait(v, t1)
                t3 = c.inc("vector", v.tensor_mul(
                    whA.ap().rearrange("p (t i) -> p t i", i=4),
                    w_.ap().rearrange("p (t i) -> p t i", i=4),
                    e_c.ap().rearrange("p (t o) -> p t o", o=1).broadcast_to([128, T, 4])))
                c.wait(v, tk[("e_dx", ibh)])
                t4 = c.inc("vector", v.tensor_mul(
                    kdec.ap().rearrange("p (t i) -> p t i", i=4),
                    w_.ap().rearrange("p (t i) -> p t i", i=4),
                    e_d.ap().rearrange("p (t o) -> p t o", o=1).broadcast_to([128, T, 4])))
                t5 = c.inc("vector", v.tensor_mul(qhA[:, :], qnh[:, :], e_r[:, :]))
                t6 = c.inc("vector", v.tensor_mul(qgn[:, :], qnh[:, :], e_g[:, :]))
                c.wait(v, t6)
                t7 = c.inc("vector", v.tensor_scalar(qgn[:, :], qgn[:, :], -1.0, None, ALU.mult))
                tk[("modv", ibh)] = [t2, t3, t4, t5, t7]

            @block.gpsimd
            def _(g, ibh=ibh):
                c.wait(g, tk[("exp1", ibh)])
                t1 = c.inc("gpsimd", g.tensor_mul(
                    kexp.ap().rearrange("p (t i) -> p t i", i=4),
                    kI.ap().rearrange("p (t i) -> p t i", i=4),
                    e_g.ap().rearrange("p (t o) -> p t o", o=1).broadcast_to([128, T, 4])))
                tk[("modg", ibh)] = [t1]

            for I in (1, 2, 3):
                @block.gpsimd
                def _(g, ibh=ibh, I=I):
                    c.wait(g, tk[("ecBx", ibh, I)], tk[("modv", ibh)])
                    tk[("modB", ibh, I)] = c.inc("gpsimd", g.tensor_mul(
                        whB[I].ap().rearrange("p (cc s i) -> p cc s i", s=16 * I, i=4),
                        w_.ap().rearrange("p (cc m) -> p cc m", m=256)[:, :, 0:64 * I]
                           .rearrange("p cc (s i) -> p cc s i", i=4),
                        ecB[I].ap().rearrange("p (cc s o) -> p cc s o", s=16 * I, o=1)
                              .broadcast_to([128, NCH, 16 * I, 4])))

            @block.vector
            def _(v, ibh=ibh):
                c.wait(v, tk[("in", ibh)], tk.get(("sup_last", ibh - 1)))
                t1 = c.inc("vector", v.memset(negS[:, :], 0.0))
                c.wait(v, t1)
                tk[("S0", ibh)] = c.inc("vector", v.tensor_copy(negSb[:, :], negS[:, :]))

            prep_tok = ibh

            for cc in range(NCH):
                mc = cc * 256
                tc = cc * 64

                @block.tensor
                def _(t, ibh=ibh, cc=cc, mc=mc, tc=tc):
                    c.wait(t, tk[("modv", ibh)], tk[("modg", ibh)],
                           [tk[("modB", ibh, I)] for I in (1, 2, 3)], tk["msk"])
                    c.wait(t, tk.get(("fevac", ibh, cc - 1)))
                    for j in range(4):
                        rt, base = j // 2, 64 * (j % 2)
                        t.matmul(pm[rt][base:base + 64, 64 * j:64 * j + 64],
                                 whA[:, mc + 64 * j: mc + 64 * j + 64],
                                 khA[:, mc + 64 * j: mc + 64 * j + 64], start=True, stop=True)
                        t.matmul(pm[rt][base:base + 64, 256 + 16 * j:256 + 16 * j + 16],
                                 whA[:, mc + 64 * j: mc + 64 * j + 64],
                                 qhA[:, tc + 16 * j: tc + 16 * j + 16], start=True, stop=True)
                    for I in (1, 2, 3):
                        wB = whB[I]
                        mb = cc * 64 * I
                        if I < 3:
                            t.matmul(pm[0][0:64 * I, 64 * I:64 * I + 64],
                                     wB[:, mb: mb + 64 * I],
                                     khA[:, mc + 64 * I: mc + 64 * I + 64], start=True, stop=True)
                            t.matmul(pm[0][0:64 * I, 256 + 16 * I:256 + 16 * I + 16],
                                     wB[:, mb: mb + 64 * I],
                                     qhA[:, tc + 16 * I: tc + 16 * I + 16], start=True, stop=True)
                        else:
                            t.matmul(pm[0][0:128, 192:256], wB[:, mb: mb + 128],
                                     khA[:, mc + 192: mc + 256], start=True, stop=True)
                            t.matmul(pm[1][0:64, 192:256], wB[:, mb + 128: mb + 192],
                                     khA[:, mc + 192: mc + 256], start=True, stop=True)
                            t.matmul(pm[0][0:128, 304:320], wB[:, mb: mb + 128],
                                     qhA[:, tc + 48: tc + 64], start=True, stop=True)
                            last = t.matmul(pm[1][0:64, 304:320], wB[:, mb + 128: mb + 192],
                                            qhA[:, tc + 48: tc + 64], start=True, stop=True)
                    tk[("fmm", ibh, cc)] = c.inc("tensor", last)

                @block.gpsimd
                def _(g, ibh=ibh, cc=cc):
                    c.wait(g, tk.get(("chain_done", ibh, cc - 1)))
                    tk[("msz", ibh, cc)] = c.inc("gpsimd", g.memset(msm[:, :], 0.0))

                @block.vector
                def _(v, ibh=ibh, cc=cc):
                    c.wait(v, tk[("fmm", ibh, cc)], tk[("msz", ibh, cc)])
                    ts = []
                    for j in range(4):
                        rt, base = j // 2, 64 * (j % 2)
                        ts.append(c.inc("vector", v.tensor_mul(
                            msm[base:base + 64, rt * 320 + 64 * j: rt * 320 + 64 * j + 64],
                            pm[rt][base:base + 64, 64 * j:64 * j + 64], mskK[base:base + 64, :])))
                        ts.append(c.inc("vector", v.tensor_mul(
                            msm[base:base + 64, rt * 320 + 256 + 16 * j: rt * 320 + 256 + 16 * j + 16],
                            pm[rt][base:base + 64, 256 + 16 * j:256 + 16 * j + 16],
                            mskQ[base:base + 64, :])))
                    tk[("evA", ibh, cc)] = ts

                @block.scalar
                def _(sc, ibh=ibh, cc=cc):
                    c.wait(sc, tk[("fmm", ibh, cc)], tk[("msz", ibh, cc)])
                    ts = []
                    ts.append(c.inc("scalar", sc.mul(msm[0:64, 64:256], pm[0][0:64, 64:256], -1.0)))
                    ts.append(c.inc("scalar", sc.mul(msm[64:128, 128:256], pm[0][64:128, 128:256], -1.0)))
                    ts.append(c.inc("scalar", sc.mul(msm[0:64, 320 + 192:320 + 256],
                                                     pm[1][0:64, 192:256], -1.0)))
                    ts.append(c.inc("scalar", sc.copy(msm[0:64, 272:320], pm[0][0:64, 272:320])))
                    ts.append(c.inc("scalar", sc.copy(msm[64:128, 288:320], pm[0][64:128, 288:320])))
                    ts.append(c.inc("scalar", sc.copy(msm[0:64, 320 + 304:320 + 320],
                                                      pm[1][0:64, 304:320])))
                    tk[("evB", ibh, cc)] = ts
                    tk[("fevac", ibh, cc)] = [tk[("evA", ibh, cc)], ts]

                @block.tensor
                def _(t, ibh=ibh, cc=cc, mc=mc):
                    c.wait(t, tk[("fevac", ibh, cc)], ctok)
                    c.wait(t, tk.get(("trev", ibh, cc - 1)))
                    t1 = c.inc("tensor", t.transpose(ps_tr[:, :], msm[:, 0:128], identb[:, :]))
                    tk[("tr0", ibh, cc)] = t1

                @block.vector
                def _(v, ibh=ibh, cc=cc):
                    c.wait(v, tk[("tr0", ibh, cc)])
                    tk[("trev0", ibh, cc)] = c.inc("vector", v.tensor_copy(a0s[0][:, :], ps_tr[:, :]))

                @block.tensor
                def _(t, ibh=ibh, cc=cc):
                    c.wait(t, tk[("trev0", ibh, cc)])
                    tk[("tr1", ibh, cc)] = c.inc("tensor", t.transpose(
                        ps_tr[:, :], msm[:, 320 + 128:320 + 256], identb[:, :]))

                @block.vector
                def _(v, ibh=ibh, cc=cc):
                    c.wait(v, tk[("tr1", ibh, cc)])
                    tk[("trev1", ibh, cc)] = c.inc("vector", v.tensor_copy(a0s[1][:, :], ps_tr[:, :]))

                @block.tensor
                def _(t, ibh=ibh, cc=cc, mc=mc):
                    c.wait(t, tk[("trev1", ibh, cc)], tk[("modv", ibh)])
                    tk[("tr2", ibh, cc)] = c.inc("tensor", t.transpose(
                        ps_tr[:, :], kdec[:, mc:mc + 128], identb[:, :]))

                @block.scalar
                def _(sc, ibh=ibh, cc=cc):
                    c.wait(sc, tk[("tr2", ibh, cc)], tk.get(("sup", ibh, cc - 1)))
                    tk[("trev2", ibh, cc)] = c.inc("scalar", sc.mul(kdT[:, 0:128], ps_tr[:, :], -1.0))

                @block.tensor
                def _(t, ibh=ibh, cc=cc, mc=mc):
                    c.wait(t, tk[("trev2", ibh, cc)])
                    tk[("tr3", ibh, cc)] = c.inc("tensor", t.transpose(
                        ps_tr[:, :], kdec[:, mc + 128:mc + 256], identb[:, :]))

                @block.scalar
                def _(sc, ibh=ibh, cc=cc):
                    c.wait(sc, tk[("tr3", ibh, cc)])
                    tk[("trev3", ibh, cc)] = c.inc("scalar", sc.mul(kdT[:, 128:256], ps_tr[:, :], -1.0))
                    tk[("trev", ibh, cc)] = [tk[("trev0", ibh, cc)], tk[("trev1", ibh, cc)],
                                             tk[("trev2", ibh, cc)], tk[("trev3", ibh, cc)]]

                @block.scalar
                def _(sc, ibh=ibh, cc=cc, b=b, h=h, mc=mc):
                    c.wait(sc, tk.get(("vcons", ibh, cc - 1)))
                    t0 = c.dinc(pfx + "vld", sc.dma_start(
                        out=vrow[:, 0:128], in_=vni_dr[b, h, :, mc:mc + 128], transpose=True))
                    t1 = c.dinc(pfx + "vld", sc.dma_start(
                        out=vrow[:, 128:256], in_=vni_dr[b, h, :, mc + 128:mc + 256], transpose=True))
                    tk[("vld", ibh, cc)] = Tok("D_" + pfx + "vld", c.cnt["D_" + pfx + "vld"])

                def chain_block(blk, ibh=ibh, cc=cc, mc=mc):
                    b0ap = msm[:, 0:128] if blk == 0 else msm[:, 320 + 128:320 + 256]
                    @block.tensor
                    def _(t, ibh=ibh, cc=cc, blk=blk, mc=mc, b0ap=b0ap):
                        c.wait(t, tk[("vld", ibh, cc)], tk[("S0", ibh)] if cc == 0 else None,
                               tk.get(("sbf", ibh, cc - 1)))
                        c.wait(t, tk.get(("uadd", ibh, cc - 1, blk, 6)))
                        if blk == 1:
                            c.wait(t, tk[("uadd", ibh, cc, 0, 6)])
                        t.matmul(pu[blk][:, :], identb[:, :], vrow[:, blk * 128:(blk + 1) * 128],
                                 start=True, stop=False)
                        last = t.matmul(pu[blk][:, :], kexp[:, mc + blk * 128: mc + (blk + 1) * 128],
                                        negSb[:, :], start=False, stop=(blk == 0))
                        if blk == 1:
                            last = t.matmul(pu[1][:, :], msm[:, 128:256], u1f[:, :],
                                            start=False, stop=True)
                        tk[("rhsmm", ibh, cc, blk)] = c.inc("tensor", last)

                    @block.vector
                    def _(v, ibh=ibh, cc=cc, blk=blk):
                        c.wait(v, tk[("rhsmm", ibh, cc, blk)])
                        tk[("rhsev", ibh, cc, blk)] = c.inc("vector", v.tensor_copy(
                            uB[0][:, :], pu[blk][:, :]))

                    for lvl in range(1, 7):
                        aprev = (aB[(lvl - 1) % 2][:, :] if lvl > 1 else a0s[blk][:, :])
                        bprev = (bL[lvl - 1][:, :] if lvl > 1 else b0ap)

                        @block.tensor
                        def _(t, ibh=ibh, cc=cc, blk=blk, lvl=lvl, aprev=aprev, bprev=bprev):
                            if lvl == 1:
                                c.wait(t, tk[("trev0" if blk == 0 else "trev1", ibh, cc)])
                                c.wait(t, tk.get(("chend", ibh, cc, blk - 1)) or
                                       tk.get(("chend", ibh, cc - 1, 1)) or
                                       tk.get(("chend_last", ibh - 1)))
                            else:
                                c.wait(t, tk[("chev_a", ibh, cc, blk, lvl - 1)],
                                       tk[("chev_b", ibh, cc, blk, lvl - 1)])
                            if lvl < 6:
                                t1 = t.matmul(psq[:, :], bprev, aprev, start=True, stop=True)
                            else:
                                t1 = t.matmul(psq[:, :], aprev, bprev, start=True, stop=True)
                            tk[("chmm_a", ibh, cc, blk, lvl)] = c.inc("tensor", t1)

                        @block.vector
                        def _(v, ibh=ibh, cc=cc, blk=blk, lvl=lvl):
                            c.wait(v, tk[("chmm_a", ibh, cc, blk, lvl)])
                            dst = aB[lvl % 2] if lvl < 6 else bL[6]
                            tk[("chev_a", ibh, cc, blk, lvl)] = c.inc("vector", v.tensor_copy(
                                dst[:, :], psq[:, :]))

                        if lvl < 6:
                            @block.tensor
                            def _(t, ibh=ibh, cc=cc, blk=blk, lvl=lvl, aprev=aprev, bprev=bprev):
                                c.wait(t, tk[("chev_a", ibh, cc, blk, lvl)])
                                t1 = t.matmul(psq[:, :], aprev, bprev, start=True, stop=True)
                                tk[("chmm_b", ibh, cc, blk, lvl)] = c.inc("tensor", t1)

                            @block.vector
                            def _(v, ibh=ibh, cc=cc, blk=blk, lvl=lvl):
                                c.wait(v, tk[("chmm_b", ibh, cc, blk, lvl)])
                                tk[("chev_b", ibh, cc, blk, lvl)] = c.inc("vector", v.tensor_copy(
                                    bL[lvl][:, :], psq[:, :]))
                        else:
                            tk[("chev_b", ibh, cc, blk, lvl)] = tk[("chev_a", ibh, cc, blk, lvl)]
                            tk[("chend", ibh, cc, blk)] = tk[("chev_a", ibh, cc, blk, lvl)]
                            if cc == NCH - 1 and blk == 1:
                                tk[("chend_last", ibh)] = tk[("chev_a", ibh, cc, blk, lvl)]

                    for ai, lvl in enumerate(range(6, -1, -1)):
                        @block.tensor
                        def _(t, ibh=ibh, cc=cc, blk=blk, lvl=lvl, ai=ai, b0ap=b0ap):
                            bl = b0ap if lvl == 0 else bL[lvl][:, :]
                            ucur = uB[ai % 2][:, :]
                            if ai == 0:
                                c.wait(t, tk[("rhsev", ibh, cc, blk)],
                                       tk[("chev_a", ibh, cc, blk, 6)])
                            else:
                                c.wait(t, tk[("uadd", ibh, cc, blk, ai - 1)])
                                if lvl >= 1:
                                    c.wait(t, tk[("chev_b", ibh, cc, blk, lvl)])
                            tk[("apmm", ibh, cc, blk, ai)] = c.inc("tensor", t.matmul(
                                pu[blk][:, :], bl, ucur, start=True, stop=True))

                        @block.vector
                        def _(v, ibh=ibh, cc=cc, blk=blk, ai=ai):
                            c.wait(v, tk[("apmm", ibh, cc, blk, ai)])
                            unxt = uB[(ai + 1) % 2] if ai < 6 else (u1f if blk == 0 else uB[1])
                            ucur = uB[ai % 2]
                            tk[("uadd", ibh, cc, blk, ai)] = c.inc("vector", v.tensor_add(
                                unxt[:, :], pu[blk][:, :], ucur[:, :]))

                chain_block(0)
                chain_block(1)

                @block.tensor
                def _(t, ibh=ibh, cc=cc, tc=tc):
                    c.wait(t, tk[("uadd", ibh, cc, 0, 6)], tk[("uadd", ibh, cc, 1, 6)])
                    c.wait(t, tk.get(("oev", ibh, cc - 1)))
                    t.matmul(po[:, :], msm[0:128, 256:320], u1f[:, :], start=True, stop=False)
                    t.matmul(po[:, :], msm[0:128, 320 + 256:320 + 320], uB[1][:, :],
                             start=False, stop=False)
                    last = t.matmul(po[:, :], qgn[:, tc:tc + 64], negSb[:, :],
                                    start=False, stop=True)
                    tk[("omm", ibh, cc)] = c.inc("tensor", last)

                @block.scalar
                def _(sc, ibh=ibh, cc=cc):
                    c.wait(sc, tk[("omm", ibh, cc)])
                    if cc <= 1:
                        c.wait(sc, tk.get(("ost", ibh - 1)))
                    tk[("oev", ibh, cc)] = c.inc("scalar", sc.mul(
                        obh[64 * (cc % 2):64 * (cc % 2) + 64, (cc // 2) * 128:(cc // 2 + 1) * 128],
                        po[:, :], MKSCALE))

                @block.tensor
                def _(t, ibh=ibh, cc=cc):
                    c.wait(t, tk[("trev2", ibh, cc)], tk[("trev3", ibh, cc)],
                           tk[("uadd", ibh, cc, 0, 6)], tk[("uadd", ibh, cc, 1, 6)],
                           tk[("omm", ibh, cc)])
                    c.wait(t, tk.get(("sup", ibh, cc - 1)))
                    t.matmul(psS[:, :], kdT[:, 0:128], u1f[:, :], start=True, stop=False)
                    last = t.matmul(psS[:, :], kdT[:, 128:256], uB[1][:, :],
                                    start=False, stop=True)
                    tk[("smm", ibh, cc)] = c.inc("tensor", last)

                @block.vector
                def _(v, ibh=ibh, cc=cc):
                    c.wait(v, tk[("smm", ibh, cc)])
                    tk[("sup", ibh, cc)] = c.inc("vector", v.scalar_tensor_tensor(
                        negS[:, :], negS[:, :], glr[:, NCH + cc:NCH + cc + 1], psS[:, :],
                        ALU.mult, ALU.add))

                @block.scalar
                def _(sc, ibh=ibh, cc=cc):
                    c.wait(sc, tk[("sup", ibh, cc)])
                    tk[("sbf", ibh, cc)] = c.inc("scalar", sc.copy(negSb[:, :], negS[:, :]))
                    if cc == NCH - 1:
                        tk[("sup_last", ibh)] = tk[("sbf", ibh, cc)]

                tk[("vcons", ibh, cc)] = [tk[("rhsmm", ibh, cc, 0)], tk[("rhsmm", ibh, cc, 1)]]
                tk[("chain_done", ibh, cc)] = [tk[("uadd", ibh, cc, 1, 6)], tk[("omm", ibh, cc)],
                                               tk[("smm", ibh, cc)]]
                tk[("trev", ibh, cc)] = tk[("trev", ibh, cc)]

            @block.sync
            def _(sy, ibh=ibh, b=b, h=h):
                c.wait(sy, [tk[("oev", ibh, cc)] for cc in range(NCH)])
                tk[("ost", ibh)] = c.dinc(pfx + "ost", sy.dma_start(
                    out=o_dr[b, h].rearrange("(cc p) dv -> p cc dv", p=128), in_=obh.ap()
                    .rearrange("p (cc dv) -> p cc dv", dv=128)))

        @block.sync
        def _(sy):
            c.wait(sy, tk[("ost", len(bh_list) - 1)])
    return



def emit_l0out(nc, c, st, pfx, o_dr, gate_dr, onR_dr, obarT_dr, identb, cmap, ctok):
    NH = 16
    with contextlib.ExitStack() as stA:
        obh = stA.enter_context(nc.sbuf_tensor(pfx + "obh", [128, 16 * 128], BF16))
        grow = stA.enter_context(nc.sbuf_tensor(pfx + "grow", [128, 16 * 128], BF16))
        sq = stA.enter_context(nc.sbuf_tensor(pfx + "sq", [128, 2048], F32))
        ssb = stA.enter_context(nc.sbuf_tensor(pfx + "ssb", [128, 3 * 16], F32))
        onR = stA.enter_context(nc.sbuf_tensor(pfx + "onR", [128, 128], BF16))
        obT = stA.enter_context(nc.sbuf_tensor(pfx + "obT", [128, 16 * 128], BF16))
        ps_tr = stA.enter_context(nc.psum_tensor(pfx + "pstr", [128, 128], BF16))

        tk = {}
        with nc.Block() as block:
            @block.gpsimd
            def _(g):
                tk["onR"] = c.dinc(pfx + "onR", g.dma_start(out=onR[:, :], in_=onR_dr[:, :]))

            for ibh in range(2 * NH):
                b, h = ibh // NH, ibh % NH

                @block.gpsimd
                def _(g, b=b, h=h, ibh=ibh):
                    c.wait(g, tk.get(("trl", ibh - 1)))
                    tk[("lo", ibh)] = c.dinc(pfx + "lo", g.dma_start(
                        out=obh.ap().rearrange("p (cc dv) -> p cc dv", dv=128),
                        in_=o_dr[b, h].rearrange("(cc p) dv -> p cc dv", p=128)))

                @block.scalar
                def _(sc, b=b, h=h, ibh=ibh):
                    c.wait(sc, tk.get(("gated", ibh - 1)))
                    ts = []
                    for j in range(16):
                        ts.append(c.dinc(pfx + "lg", sc.dma_start(
                            out=grow[:, j * 128:(j + 1) * 128],
                            in_=gate_dr[h * 128:(h + 1) * 128,
                                        b * T + j * 128: b * T + (j + 1) * 128],
                            transpose=True)))
                    tk[("lg", ibh)] = Tok("D_" + pfx + "lg", c.cnt["D_" + pfx + "lg"])

                @block.scalar
                def _(sc, ibh=ibh):
                    c.wait(sc, tk[("lo", ibh)])
                    c.wait(sc, tk.get(("rstd", ibh - 1)))
                    tk[("sq", ibh)] = c.inc("scalar", sc.activation(sq[:, :], obh[:, :], AF.Square))

                @block.vector
                def _(v, ibh=ibh):
                    c.wait(v, tk[("sq", ibh)])
                    for j in range(16):
                        c.inc("vector", v.reduce_sum(ssb[:, j:j + 1], sq[:, j * 128:(j + 1) * 128], AX.X))
                    tk[("ss", ibh)] = c.mark("vector")

                @block.scalar
                def _(sc, ibh=ibh):
                    c.wait(sc, tk[("ss", ibh)])
                    tk[("srt", ibh)] = c.inc("scalar", sc.activation(
                        ssb[:, 16:32], ssb[:, 0:16], AF.Sqrt, scale=1.0 / 128.0, bias=cmap[1e-5]))

                @block.vector
                def _(v, ibh=ibh):
                    c.wait(v, tk[("srt", ibh)])
                    tk[("rstd", ibh)] = c.inc("vector", v.reciprocal(ssb[:, 32:48], ssb[:, 16:32]))

                @block.scalar
                def _(sc, ibh=ibh):
                    c.wait(sc, tk[("lg", ibh)])
                    c.wait(sc, tk.get(("gated", ibh - 1)))
                    tk[("sig", ibh)] = c.inc("scalar", sc.activation(grow[:, :], grow[:, :], AF.Sigmoid))

                @block.vector
                def _(v, ibh=ibh):
                    c.wait(v, tk[("rstd", ibh)], tk[("sig", ibh)], tk["onR"])
                    c.wait(v, tk.get(("trl", ibh - 1)))
                    for j in range(16):
                        t1 = c.inc("vector", v.tensor_scalar(
                            obh[:, j * 128:(j + 1) * 128], obh[:, j * 128:(j + 1) * 128],
                            ssb[:, 32 + j:33 + j], None, ALU.mult))
                        c.wait(v, t1)
                        t2 = c.inc("vector", v.tensor_mul(
                            obh[:, j * 128:(j + 1) * 128], obh[:, j * 128:(j + 1) * 128], onR[:, :]))
                        c.wait(v, t2)
                        c.inc("vector", v.tensor_mul(
                            obh[:, j * 128:(j + 1) * 128], obh[:, j * 128:(j + 1) * 128],
                            grow[:, j * 128:(j + 1) * 128]))
                    tk[("gated", ibh)] = c.mark("vector")

                for j in range(16):
                    @block.tensor
                    def _(t, ibh=ibh, j=j):
                        c.wait(t, tk[("gated", ibh)], ctok)
                        c.wait(t, tk.get(("trev", ibh, j - 1)) or tk.get(("trev", ibh - 1, 15)))
                        tk[("tr", ibh, j)] = c.inc("tensor", t.transpose(
                            ps_tr[:, :], obh[:, j * 128:(j + 1) * 128], identb[:, :]))
                        if j == 15:
                            tk[("trl", ibh)] = tk[("tr", ibh, j)]

                    @block.vector
                    def _(v, ibh=ibh, j=j):
                        c.wait(v, tk[("tr", ibh, j)])
                        if j == 0:
                            c.wait(v, tk.get(("st", ibh - 1)))
                        tk[("trev", ibh, j)] = c.inc("vector", v.tensor_copy(
                            obT[:, j * 128:(j + 1) * 128], ps_tr[:, :]))

                @block.sync
                def _(sy, b=b, h=h, ibh=ibh):
                    c.wait(sy, [tk[("trev", ibh, j)] for j in range(16)])
                    tk[("st", ibh)] = c.dinc(pfx + "st", sy.dma_start(
                        out=obarT_dr[h * 128:(h + 1) * 128, b * T:(b + 1) * T], in_=obT[:, :]))

            @block.sync
            def _(sy):
                c.wait(sy, tk[("st", 2 * NH - 1)])
    return



def _newnc():
    return bass.Bass("TRN2", target_bir_lowering=False, debug=False)


def _din(nc, name, shape, dt):
    return nc.dram_tensor(name, shape, dt, kind="ExternalInput").ap()


def _dout(nc, name, shape, dt):
    return nc.dram_tensor(name, shape, dt, kind="ExternalOutput").ap()


def _run(nc, in_maps):
    return run_bass_kernel_spmd(nc, in_maps, core_ids=list(range(8)))


def _ln_phase(in_maps, out32=False):
    nc = _newnc()
    x = _din(nc, "x", [D, NTOK], F32)
    o = _dout(nc, "o", [D, NTOK], BF16)
    o32 = _dout(nc, "o32", [D, NTOK], F32) if out32 else None
    c = Ctx(nc)
    with contextlib.ExitStack() as st:
        c.make_sems(st)
        onescol, onesrow, ident, cmap, ctok = emit_consts(nc, c, st, values=[1e-5, 1.0, 1e-24])
        emit_ln(nc, c, st, "ln", x, o, onescol, onesrow, cmap, ctok, out32_dr=o32)
    return _run(nc, in_maps)


_DBG = {}


def kernel_unfused(**inputs):
    ins = {k: np.asarray(v) for k, v in inputs.items()}
    x0 = ins["L0_attn_hidden_states"]

    TT = lambda a: np.ascontiguousarray(np.asarray(a).T)
    wqT = TT(ins["L0_attn_Wq"]); wkT = TT(ins["L0_attn_Wk"]); wvT = TT(ins["L0_attn_Wv"])
    wf1T = TT(ins["L0_attn_Wf1"]); wf2T = TT(ins["L0_attn_Wf2"]); wbT = TT(ins["L0_attn_Wb"])
    wg1T = TT(ins["L0_attn_Wg1"]); wg2T = TT(ins["L0_attn_Wg2"]); woT = TT(ins["L0_attn_Wo"])
    dtbT = np.ascontiguousarray(ins["L0_attn_dt_bias"].reshape(16, 128).T)
    bg2T = np.ascontiguousarray(ins["L0_attn_bg2"].reshape(16, 128).T)
    bg1 = np.ascontiguousarray(ins["L0_attn_bg1"])
    onR = np.ascontiguousarray(np.tile(ins["L0_attn_onorm_w"][None, :], (128, 1)))
    sgT = np.ascontiguousarray(ins["L0_ffn_s_gate"].reshape(64, 128).T)
    suT = np.ascontiguousarray(ins["L0_ffn_s_up"].reshape(64, 128).T)
    sdT = np.ascontiguousarray(ins["L0_ffn_s_down"].reshape(16, 128).T)
    wqkvT = TT(ins["L1_attn_w_qkv"]); wo1T = TT(ins["L1_attn_w_o"])
    fc1bT = np.ascontiguousarray(ins["L1_ffn_fc1_b"].reshape(8, 16, 128).transpose(0, 2, 1))
    alphaR = np.ascontiguousarray(np.tile(ins["L1_ffn_alpha"][None, :], (128, 1)))
    p_ = np.arange(64)
    mskK = np.tile(np.where(np.arange(64)[None, :] > p_[:, None], -1.0, 0.0), (2, 1)) \
        .astype(ml_dtypes.bfloat16)
    mskQ = np.tile(np.where(np.arange(16)[None, :] >= (p_ // 4)[:, None], 1.0, 0.0), (2, 1)) \
        .astype(ml_dtypes.bfloat16)

    shared = dict(wqT=wqT, wkT=wkT, wvT=wvT, wf1T=wf1T, wf2T=wf2T, wbT=wbT,
                  wg1T=wg1T, wg2T=wg2T, woT=woT, dtbT=dtbT, bg2T=bg2T, bg1=bg1,
                  onR=onR, sgT=sgT, suT=suT, sdT=sdT, wqkvT=wqkvT, wo1T=wo1T,
                  fc1bT=fc1bT, alphaR=alphaR, mskK=mskK, mskQ=mskQ,
                  convq=ins["L0_attn_conv_q"], convk=ins["L0_attn_conv_k"],
                  convv=ins["L0_attn_conv_v"], alog=ins["L0_attn_A_log"],
                  wgate=ins["L0_ffn_w_gate"], wup=ins["L0_ffn_w_up"],
                  wdown=ins["L0_ffn_w_down"], gatew=ins["L1_ffn_gate_w"],
                  fc1=ins["L1_ffn_fc1_w"], fc2=ins["L1_ffn_fc2_w"],
                  fc2b=ins["L1_ffn_fc2_b"])

    xT = [np.ascontiguousarray(
        x0[2 * cid:2 * cid + 2].transpose(2, 0, 1).reshape(D, NTOK)) for cid in range(8)]

    st_ = {}

    res = _ln_phase([{"x": xT[i]} for i in range(8)])
    ln0 = [r["o"] for r in res.results]

    nc = _newnc()
    a = {"xln": _din(nc, "xln", [D, NTOK], BF16),
         "wqT": _din(nc, "wqT", [2048, 2048], F32), "wkT": _din(nc, "wkT", [2048, 8192], F32),
         "wvT": _din(nc, "wvT", [2048, 8192], F32),
         "convq": _din(nc, "convq", [2048, 4], F32), "convk": _din(nc, "convk", [8192, 4], F32),
         "convv": _din(nc, "convv", [8192, 4], F32),
         "wf1T": _din(nc, "wf1T", [2048, 128], F32), "wf2T": _din(nc, "wf2T", [128, 2048], F32),
         "dtbT": _din(nc, "dtbT", [128, 16], F32), "alog": _din(nc, "alog", [16], F32),
         "wbT": _din(nc, "wbT", [2048, 64], F32), "wg1T": _din(nc, "wg1T", [2048, 128], F32),
         "bg1": _din(nc, "bg1", [128], F32), "wg2T": _din(nc, "wg2T", [128, 2048], F32),
         "bg2T": _din(nc, "bg2T", [128, 16], F32)}
    o = {"qn": _dout(nc, "qn", [16, 128, NTOK], BF16),
         "kni": _dout(nc, "kni", [2, 16, 128, 8192], BF16),
         "vni": _dout(nc, "vni", [2, 16, 128, 8192], BF16),
         "la": _dout(nc, "la", [16, 128, NTOK], F32),
         "beta": _dout(nc, "beta", [64, NTOK], BF16),
         "gate": _dout(nc, "gate", [D, NTOK], BF16)}
    c = Ctx(nc)
    with contextlib.ExitStack() as stk:
        c.make_sems(stk)
        onescol, onesrow, ident, cmap, ctok = emit_consts(nc, c, stk, values=[1e-5, 1.0, 1e-24])
        emit_l0proj(nc, c, stk, "p2", a["xln"], a["wqT"], a["wkT"], a["wvT"],
                    a["convq"], a["convk"], a["convv"], a["wf1T"], a["wf2T"],
                    a["dtbT"], a["alog"], a["wbT"], a["wg1T"], a["bg1"], a["wg2T"], a["bg2T"],
                    o["qn"], o["kni"], o["vni"], o["la"], o["beta"], o["gate"],
                    onescol, onesrow, cmap, ctok)
    res = _run(nc, [{"xln": ln0[i], "wqT": wqT, "wkT": wkT, "wvT": wvT,
                     "convq": shared["convq"], "convk": shared["convk"], "convv": shared["convv"],
                     "wf1T": wf1T, "wf2T": wf2T, "dtbT": dtbT, "alog": shared["alog"],
                     "wbT": wbT, "wg1T": wg1T, "bg1": bg1, "wg2T": wg2T, "bg2T": bg2T}
                    for i in range(8)])
    qn = [r["qn"] for r in res.results]; kni = [r["kni"] for r in res.results]
    vni = [r["vni"] for r in res.results]; la = [r["la"] for r in res.results]
    beta = [r["beta"] for r in res.results]; gate = [r["gate"] for r in res.results]

    nc = _newnc()
    a = {"qn": _din(nc, "qn", [16, 128, NTOK], BF16),
         "kni": _din(nc, "kni", [2, 16, 128, 8192], BF16),
         "vni": _din(nc, "vni", [2, 16, 128, 8192], BF16),
         "la": _din(nc, "la", [16, 128, NTOK], F32),
         "beta": _din(nc, "beta", [64, NTOK], BF16),
         "mskK": _din(nc, "mskK", [128, 64], BF16),
         "mskQ": _din(nc, "mskQ", [128, 16], BF16)}
    od = _dout(nc, "o", [2, 16, 2048, 128], BF16)
    c = Ctx(nc)
    with contextlib.ExitStack() as stk:
        c.make_sems(stk)
        onescol, onesrow, ident, cmap, ctok = emit_consts(nc, c, stk, values=[1e-5, 1.0, 1e-24])
        emit_mkda(nc, c, stk, "mk", a["qn"], a["kni"], a["vni"], a["la"], a["beta"], od,
                  a["mskK"], a["mskQ"], onesrow, ident, ctok)
    res = _run(nc, [{"qn": qn[i], "kni": kni[i], "vni": vni[i], "la": la[i],
                     "beta": beta[i], "mskK": mskK, "mskQ": mskQ} for i in range(8)])
    o_scr = [r["o"] for r in res.results]

    nc = _newnc()
    a = {"o": _din(nc, "o", [2, 16, 2048, 128], BF16),
         "gate": _din(nc, "gate", [D, NTOK], BF16),
         "onR": _din(nc, "onR", [128, 128], F32),
         "woT": _din(nc, "woT", [2048, 2048], F32),
         "x0": _din(nc, "x0", [D, NTOK], F32)}
    obarT = nc.dram_tensor("obarT", [2048, NTOK], BF16, kind="Internal").ap()
    x1d = _dout(nc, "x1", [D, NTOK], F32)
    c = Ctx(nc)
    with contextlib.ExitStack() as stk:
        c.make_sems(stk)
        onescol, onesrow, ident, cmap, ctok = emit_consts(nc, c, stk, values=[1e-5, 1.0, 1e-24])
        emit_l0out(nc, c, stk, "p4", a["o"], a["gate"], a["onR"], obarT, ident, cmap, ctok)
        emit_barrier(nc, c)
        emit_gemm(nc, c, stk, "p4g", a["woT"], 2048, 2048, obarT, x1d, residual_dr=a["x0"])
    res = _run(nc, [{"o": o_scr[i], "gate": gate[i], "onR": onR, "woT": woT,
                     "x0": xT[i]} for i in range(8)])
    x1 = [r["x1"] for r in res.results]
    _DBG["x1"] = x1; _DBG["ln0"] = ln0; _DBG["o_scr"] = o_scr; _DBG["gate"] = gate; _DBG["qn"] = qn; _DBG["kni"] = kni; _DBG["vni"] = vni; _DBG["la"] = la; _DBG["beta"] = beta

    res = _ln_phase([{"x": x1[i]} for i in range(8)])
    ln1 = [r["o"] for r in res.results]
    nc = _newnc()
    a = {"xln": _din(nc, "xln", [D, NTOK], BF16), "xres": _din(nc, "xres", [D, NTOK], F32),
         "wg": _din(nc, "wg", [2048, 8192], I8), "wu": _din(nc, "wu", [2048, 8192], I8),
         "wd": _din(nc, "wd", [8192, 2048], I8),
         "sgT": _din(nc, "sgT", [128, 64], F32), "suT": _din(nc, "suT", [128, 64], F32),
         "sdT": _din(nc, "sdT", [128, 16], F32)}
    out2 = _dout(nc, "out", [D, NTOK], F32)
    fuse = nc.dram_tensor("fuse", [8192, NTOK], BF16, kind="Internal").ap()
    c = Ctx(nc)
    with contextlib.ExitStack() as stk:
        c.make_sems(stk)
        emit_ffn0(nc, c, stk, "f0", a["xln"], a["wg"], a["wu"], a["wd"],
                  a["sgT"], a["suT"], a["sdT"], a["xres"], out2, fuse)
    res = _run(nc, [{"xln": ln1[i], "xres": x1[i], "wg": shared["wgate"],
                     "wu": shared["wup"], "wd": shared["wdown"],
                     "sgT": sgT, "suT": suT, "sdT": sdT} for i in range(8)])
    x2 = [r["out"] for r in res.results]
    _DBG["x2"] = x2

    res = _ln_phase([{"x": x2[i]} for i in range(8)])
    ln2 = [r["o"] for r in res.results]
    nc = _newnc()
    a = {"xln": _din(nc, "xln", [D, NTOK], BF16),
         "wqkvT": _din(nc, "wqkvT", [2048, 6144], F32)}
    qkvd = _dout(nc, "qkv", [6144, NTOK], BF16)
    c = Ctx(nc)
    with contextlib.ExitStack() as stk:
        c.make_sems(stk)
        emit_gemm(nc, c, stk, "p6", a["wqkvT"], 2048, 6144, a["xln"], qkvd,
                  scale_fn=lambda mt: (float(128.0 ** -0.5) if mt < 16 else 1.0))
    res = _run(nc, [{"xln": ln2[i], "wqkvT": wqkvT} for i in range(8)])
    qkv = [r["qkv"] for r in res.results]

    nc = _newnc()
    a = {"qkv": _din(nc, "qkv", [6144, NTOK], BF16)}
    attnd = _dout(nc, "attn", [2048, NTOK], BF16)
    c = Ctx(nc)
    with contextlib.ExitStack() as stk:
        c.make_sems(stk)
        onescol, onesrow, ident, cmap, ctok = emit_consts(nc, c, stk, values=[1e-5, 1.0, 1e-24])
        emit_attn(nc, c, stk, "at", a["qkv"], attnd, ident, ctok)
    res = _run(nc, [{"qkv": qkv[i]} for i in range(8)])
    attn = [r["attn"] for r in res.results]

    nc = _newnc()
    a = {"attn": _din(nc, "attn", [2048, NTOK], BF16),
         "wo1T": _din(nc, "wo1T", [2048, 2048], F32),
         "x2": _din(nc, "x2", [D, NTOK], F32)}
    x3d = _dout(nc, "x3", [D, NTOK], F32)
    c = Ctx(nc)
    with contextlib.ExitStack() as stk:
        c.make_sems(stk)
        emit_gemm(nc, c, stk, "p8", a["wo1T"], 2048, 2048, a["attn"], x3d, residual_dr=a["x2"])
    res = _run(nc, [{"attn": attn[i], "wo1T": wo1T, "x2": x2[i]} for i in range(8)])
    x3 = [r["x3"] for r in res.results]
    _DBG["x3"] = x3; _DBG["qkv"] = qkv; _DBG["attn"] = attn

    res = _ln_phase([{"x": x3[i]} for i in range(8)], out32=True)
    ln3 = [r["o"] for r in res.results]
    ln3_32 = [r["o32"] for r in res.results]
    nc = _newnc()
    a = {"xln": _din(nc, "xln", [D, NTOK], BF16), "xln32": _din(nc, "xln32", [D, NTOK], F32),
         "xres": _din(nc, "xres", [D, NTOK], F32),
         "gw": _din(nc, "gw", [2048, 8], F32), "fc1": _din(nc, "fc1", [8, 2048, 2048], F32),
         "fc1bT": _din(nc, "fc1bT", [8, 128, 16], F32),
         "fc2": _din(nc, "fc2", [8, 2048, 2048], F32), "fc2b": _din(nc, "fc2b", [8, 2048], F32),
         "alphaR": _din(nc, "alphaR", [128, 8], F32)}
    outd = _dout(nc, "out", [D, NTOK], F32)
    c = Ctx(nc)
    with contextlib.ExitStack() as stk:
        c.make_sems(stk)
        onescol, onesrow, ident, cmap, ctok = emit_consts(nc, c, stk, values=[1e-5, 1.0, 1e-24])
        emit_moe(nc, c, stk, "mo", a["xln"], a["xln32"], a["gw"], a["fc1"], a["fc1bT"],
                 a["fc2"], a["fc2b"], a["alphaR"], a["xres"], outd, onesrow, ident, ctok)
    res = _run(nc, [{"xln": ln3[i], "xln32": ln3_32[i], "xres": x3[i],
                     "gw": shared["gatew"], "fc1": shared["fc1"], "fc1bT": fc1bT,
                     "fc2": shared["fc2"], "fc2b": shared["fc2b"], "alphaR": alphaR}
                    for i in range(8)])
    outT = [r["out"] for r in res.results]

    full = np.empty((16, 2048, 2048), np.float32)
    for i in range(8):
        full[2 * i:2 * i + 2] = outT[i].reshape(D, 2, T).transpose(1, 2, 0)
    return full



def _build_fused():
    nc = _newnc()
    a = {}
    a["x0"] = _din(nc, "x0", [D, NTOK], F32)
    a["wqT"] = _din(nc, "wqT", [2048, 2048], F32)
    a["wkT"] = _din(nc, "wkT", [2048, 8192], F32)
    a["wvT"] = _din(nc, "wvT", [2048, 8192], F32)
    a["convq"] = _din(nc, "convq", [2048, 4], F32)
    a["convk"] = _din(nc, "convk", [8192, 4], F32)
    a["convv"] = _din(nc, "convv", [8192, 4], F32)
    a["wf1T"] = _din(nc, "wf1T", [2048, 128], F32)
    a["wf2T"] = _din(nc, "wf2T", [128, 2048], F32)
    a["dtbT"] = _din(nc, "dtbT", [128, 16], F32)
    a["alog"] = _din(nc, "alog", [16], F32)
    a["wbT"] = _din(nc, "wbT", [2048, 64], F32)
    a["wg1T"] = _din(nc, "wg1T", [2048, 128], F32)
    a["bg1"] = _din(nc, "bg1", [128], F32)
    a["wg2T"] = _din(nc, "wg2T", [128, 2048], F32)
    a["bg2T"] = _din(nc, "bg2T", [128, 16], F32)
    a["onR"] = _din(nc, "onR", [128, 128], F32)
    a["woT"] = _din(nc, "woT", [2048, 2048], F32)
    a["mskK"] = _din(nc, "mskK", [128, 64], BF16)
    a["mskQ"] = _din(nc, "mskQ", [128, 16], BF16)
    a["wg"] = _din(nc, "wg", [2048, 8192], I8)
    a["wu"] = _din(nc, "wu", [2048, 8192], I8)
    a["wd"] = _din(nc, "wd", [8192, 2048], I8)
    a["sgT"] = _din(nc, "sgT", [128, 64], F32)
    a["suT"] = _din(nc, "suT", [128, 64], F32)
    a["sdT"] = _din(nc, "sdT", [128, 16], F32)
    a["wqkvT"] = _din(nc, "wqkvT", [2048, 6144], F32)
    a["wo1T"] = _din(nc, "wo1T", [2048, 2048], F32)
    a["gw"] = _din(nc, "gw", [2048, 8], F32)
    a["fc1"] = _din(nc, "fc1", [8, 2048, 2048], F32)
    a["fc1bT"] = _din(nc, "fc1bT", [8, 128, 16], F32)
    a["fc2"] = _din(nc, "fc2", [8, 2048, 2048], F32)
    a["fc2b"] = _din(nc, "fc2b", [8, 2048], F32)
    a["alphaR"] = _din(nc, "alphaR", [128, 8], F32)
    out = _dout(nc, "out", [D, NTOK], F32)

    def scr(name, shape, dt):
        return nc.dram_tensor(name, shape, dt, kind="Internal").ap()
    ln0 = scr("s_ln0", [D, NTOK], BF16)
    qn = scr("s_qn", [16, 128, NTOK], BF16)
    kni = scr("s_kni", [2, 16, 128, 8192], BF16)
    vni = scr("s_vni", [2, 16, 128, 8192], BF16)
    la = scr("s_la", [16, 128, NTOK], F32)
    beta = scr("s_beta", [64, NTOK], BF16)
    gate = scr("s_gate", [D, NTOK], BF16)
    o_scr = scr("s_o", [2, 16, 2048, 128], BF16)
    obarT = scr("s_obarT", [2048, NTOK], BF16)
    x1 = scr("s_x1", [D, NTOK], F32)
    ln1 = scr("s_ln1", [D, NTOK], BF16)
    fuse = scr("s_fuse", [8192, NTOK], BF16)
    x2 = scr("s_x2", [D, NTOK], F32)
    ln2 = scr("s_ln2", [D, NTOK], BF16)
    qkv = scr("s_qkv", [6144, NTOK], BF16)
    attn = scr("s_attn", [2048, NTOK], BF16)
    x3 = scr("s_x3", [D, NTOK], F32)
    ln3 = scr("s_ln3", [D, NTOK], BF16)
    ln3_32 = scr("s_ln3_32", [D, NTOK], F32)

    c = Ctx(nc)
    with contextlib.ExitStack() as st:
        c.make_sems(st)
        onescol, onesrow, ident, cmap, ctok = emit_consts(nc, c, st, values=[1e-5, 1.0, 1e-24])

        with contextlib.ExitStack() as ph:
            emit_ln(nc, c, ph, "lnA", a["x0"], ln0, onescol, onesrow, cmap, ctok)
        emit_barrier(nc, c)
        with contextlib.ExitStack() as ph:
            emit_l0proj(nc, c, ph, "p2", ln0, a["wqT"], a["wkT"], a["wvT"],
                        a["convq"], a["convk"], a["convv"], a["wf1T"], a["wf2T"],
                        a["dtbT"], a["alog"], a["wbT"], a["wg1T"], a["bg1"], a["wg2T"],
                        a["bg2T"], qn, kni, vni, la, beta, gate,
                        onescol, onesrow, cmap, ctok, identb=ident)
        emit_barrier(nc, c)
        with contextlib.ExitStack() as ph:
            emit_mkda3(nc, c, ph, "mk", qn, kni, vni, la, beta, o_scr,
                       a["mskK"], a["mskQ"], onesrow, ident, ctok)
        emit_barrier(nc, c)
        with contextlib.ExitStack() as ph:
            emit_l0out(nc, c, ph, "p4", o_scr, gate, a["onR"], obarT, ident, cmap, ctok)
        emit_barrier(nc, c)
        with contextlib.ExitStack() as ph:
            emit_gemm(nc, c, ph, "p4g", a["woT"], 2048, 2048, obarT, x1, residual_dr=a["x0"])
        emit_barrier(nc, c)
        with contextlib.ExitStack() as ph:
            emit_ln(nc, c, ph, "lnB", x1, ln1, onescol, onesrow, cmap, ctok)
        emit_barrier(nc, c)
        with contextlib.ExitStack() as ph:
            emit_ffn0(nc, c, ph, "f0", ln1, a["wg"], a["wu"], a["wd"],
                      a["sgT"], a["suT"], a["sdT"], x1, x2, fuse)
        emit_barrier(nc, c)
        with contextlib.ExitStack() as ph:
            emit_ln(nc, c, ph, "lnC", x2, ln2, onescol, onesrow, cmap, ctok)
        emit_barrier(nc, c)
        with contextlib.ExitStack() as ph:
            emit_gemm(nc, c, ph, "p6", a["wqkvT"], 2048, 6144, ln2, qkv,
                      scale_fn=lambda mt: (float(128.0 ** -0.5) if mt < 16 else 1.0))
        emit_barrier(nc, c)
        with contextlib.ExitStack() as ph:
            emit_attn(nc, c, ph, "at", qkv, attn, ident, ctok)
        emit_barrier(nc, c)
        with contextlib.ExitStack() as ph:
            emit_gemm(nc, c, ph, "p8", a["wo1T"], 2048, 2048, attn, x3, residual_dr=x2)
        emit_barrier(nc, c)
        with contextlib.ExitStack() as ph:
            emit_ln(nc, c, ph, "lnD", x3, ln3, onescol, onesrow, cmap, ctok, out32_dr=ln3_32)
        emit_barrier(nc, c)
        with contextlib.ExitStack() as ph:
            emit_moe(nc, c, ph, "mo", ln3, ln3_32, a["gw"], a["fc1"], a["fc1bT"],
                     a["fc2"], a["fc2b"], a["alphaR"], x3, out, onesrow, ident, ctok)
    return nc


def kernel_fused(**inputs):
    ins = {k: np.asarray(v) for k, v in inputs.items()}
    x0 = ins["L0_attn_hidden_states"]
    TT = lambda arr: np.ascontiguousarray(np.asarray(arr).T)
    p_ = np.arange(64)
    host = dict(
        wqT=TT(ins["L0_attn_Wq"]), wkT=TT(ins["L0_attn_Wk"]), wvT=TT(ins["L0_attn_Wv"]),
        convq=ins["L0_attn_conv_q"], convk=ins["L0_attn_conv_k"], convv=ins["L0_attn_conv_v"],
        wf1T=TT(ins["L0_attn_Wf1"]), wf2T=TT(ins["L0_attn_Wf2"]),
        dtbT=np.ascontiguousarray(ins["L0_attn_dt_bias"].reshape(16, 128).T),
        alog=ins["L0_attn_A_log"],
        wbT=TT(ins["L0_attn_Wb"]), wg1T=TT(ins["L0_attn_Wg1"]),
        bg1=np.ascontiguousarray(ins["L0_attn_bg1"]), wg2T=TT(ins["L0_attn_Wg2"]),
        bg2T=np.ascontiguousarray(ins["L0_attn_bg2"].reshape(16, 128).T),
        onR=np.ascontiguousarray(np.tile(ins["L0_attn_onorm_w"][None, :], (128, 1))),
        woT=TT(ins["L0_attn_Wo"]),
        mskK=np.tile(np.where(np.arange(64)[None, :] > p_[:, None], -1.0, 0.0), (2, 1))
            .astype(ml_dtypes.bfloat16),
        mskQ=np.tile(np.where(np.arange(16)[None, :] >= (p_ // 4)[:, None], 1.0, 0.0), (2, 1))
            .astype(ml_dtypes.bfloat16),
        wg=ins["L0_ffn_w_gate"], wu=ins["L0_ffn_w_up"], wd=ins["L0_ffn_w_down"],
        sgT=np.ascontiguousarray(ins["L0_ffn_s_gate"].reshape(64, 128).T),
        suT=np.ascontiguousarray(ins["L0_ffn_s_up"].reshape(64, 128).T),
        sdT=np.ascontiguousarray(ins["L0_ffn_s_down"].reshape(16, 128).T),
        wqkvT=TT(ins["L1_attn_w_qkv"]), wo1T=TT(ins["L1_attn_w_o"]),
        gw=ins["L1_ffn_gate_w"], fc1=ins["L1_ffn_fc1_w"],
        fc1bT=np.ascontiguousarray(ins["L1_ffn_fc1_b"].reshape(8, 16, 128).transpose(0, 2, 1)),
        fc2=ins["L1_ffn_fc2_w"], fc2b=ins["L1_ffn_fc2_b"],
        alphaR=np.ascontiguousarray(np.tile(ins["L1_ffn_alpha"][None, :], (128, 1))))
    nc = _build_fused()
    in_maps = []
    for cid in range(8):
        m = dict(host)
        m["x0"] = np.ascontiguousarray(
            x0[2 * cid:2 * cid + 2].transpose(2, 0, 1).reshape(D, NTOK))
        in_maps.append(m)
    res = _run(nc, in_maps)
    full = np.empty((16, 2048, 2048), np.float32)
    for i in range(8):
        full[2 * i:2 * i + 2] = res.results[i]["out"].reshape(D, 2, T).transpose(1, 2, 0)
    return full


def kernel(**inputs):
    return kernel_fused(**inputs)



class _BlockProxy:

    def __init__(self):
        self.steps = []
        self._pend = 0

    def begin_atomic(self, n):
        self._pend = n

    def _mk(self, eng):
        def deco(fn):
            if self._pend > 0:
                if self._pend == self._first:
                    self.steps.append([(eng, fn)])
                else:
                    self.steps[-1].append((eng, fn))
                self._pend -= 1
            else:
                self.steps.append([(eng, fn)])
            return fn
        return deco

    def begin_atomic(self, n):
        self._pend = n
        self._first = n

    @property
    def tensor(self):
        return self._mk("tensor")

    @property
    def vector(self):
        return self._mk("vector")

    @property
    def scalar(self):
        return self._mk("scalar")

    @property
    def gpsimd(self):
        return self._mk("gpsimd")

    @property
    def sync(self):
        return self._mk("sync")


def _replay(block, steps):
    for grp in steps:
        for eng, fn in grp:
            getattr(block, eng)(fn)


def emit_mkda3(nc, c, st, pfx, qn_dr, kni_dr, vni_dr, la_dr, beta_dr, o_dr,
               mskK_dr, mskQ_dr, onesrow, identb, ctok, bh_list=None):
    NCH = T // 64
    if bh_list is None:
        bh_list = [(b, h) for b in range(BPC) for h in range(16)]
    kI = st.enter_context(nc.sbuf_tensor(pfx + "kI", [128, 4 * T], BF16))
    kexp = kI
    qnh = st.enter_context(nc.sbuf_tensor(pfx + "qnh", [128, T], BF16))
    qgn = qnh
    laG = st.enter_context(nc.sbuf_tensor(pfx + "laG", [128, T], F32))
    G = st.enter_context(nc.sbuf_tensor(pfx + "G", [128, T], F32))
    tmpB = st.enter_context(nc.sbuf_tensor(pfx + "tmpB", [128, 32 * 48], F32))
    brep = st.enter_context(nc.sbuf_tensor(pfx + "brep", [128, 4 * T], BF16))
    w_ = st.enter_context(nc.sbuf_tensor(pfx + "w", [128, 4 * T], BF16))
    khA = st.enter_context(nc.sbuf_tensor(pfx + "khA", [128, 4 * T], BF16))
    kdec = st.enter_context(nc.sbuf_tensor(pfx + "kdec", [128, 4 * T], BF16))
    whA = st.enter_context(nc.sbuf_tensor(pfx + "whA", [128, 4 * T], BF16))
    whB = [None] + [st.enter_context(nc.sbuf_tensor(pfx + f"whB{i}", [128, NCH * 64 * i], BF16))
                    for i in (1, 2, 3)]
    ecB = [None] + [st.enter_context(nc.sbuf_tensor(pfx + f"ecB{i}", [128, NCH * 16 * i], BF16))
                    for i in (1, 2, 3)]
    qhA = st.enter_context(nc.sbuf_tensor(pfx + "qhA", [128, T], BF16))
    e_r = st.enter_context(nc.sbuf_tensor(pfx + "e_r", [128, T], BF16))
    e_c = st.enter_context(nc.sbuf_tensor(pfx + "e_c", [128, T], BF16))
    e_g = st.enter_context(nc.sbuf_tensor(pfx + "e_g", [128, T], BF16))
    e_d = st.enter_context(nc.sbuf_tensor(pfx + "e_d", [128, T], BF16))
    glr = st.enter_context(nc.sbuf_tensor(pfx + "glr", [128, 2 * NCH], F32))
    negS = st.enter_context(nc.sbuf_tensor(pfx + "negS", [128, 128], F32))
    negSb = st.enter_context(nc.sbuf_tensor(pfx + "negSb", [128, 128], BF16))
    mskK = st.enter_context(nc.sbuf_tensor(pfx + "mskK", [128, 64], BF16))
    mskQ = st.enter_context(nc.sbuf_tensor(pfx + "mskQ", [128, 16], BF16))
    msm2 = [st.enter_context(nc.sbuf_tensor(pfx + f"msm{i}", [128, 640], BF16)) for i in range(2)]
    kdT2 = [st.enter_context(nc.sbuf_tensor(pfx + f"kdT{i}", [128, 256], BF16)) for i in range(2)]
    Wt2 = [st.enter_context(nc.sbuf_tensor(pfx + f"Wt{i}", [128, 256], BF16)) for i in range(2)]
    uv2 = [st.enter_context(nc.sbuf_tensor(pfx + f"uv{i}", [128, 256], BF16)) for i in range(2)]
    usb2 = [st.enter_context(nc.sbuf_tensor(pfx + f"usb{i}", [128, 256], BF16)) for i in range(2)]
    vrow2 = [st.enter_context(nc.sbuf_tensor(pfx + f"vr{i}", [128, 256], BF16)) for i in range(2)]
    aB4 = [[[st.enter_context(nc.sbuf_tensor(pfx + f"aB{pp}{b_}{i}", [128, 128], BF16))
             for i in range(2)] for b_ in range(2)] for pp in range(2)]
    bB4 = [[[st.enter_context(nc.sbuf_tensor(pfx + f"bB{pp}{b_}{i}", [128, 128], BF16))
             for i in range(2)] for b_ in range(2)] for pp in range(2)]
    Dnv = [[st.enter_context(nc.sbuf_tensor(pfx + f"Dnv{pp}{i}", [128, 128], BF16))
            for i in range(2)] for pp in range(2)]
    tmpw = [st.enter_context(nc.sbuf_tensor(pfx + f"tmpw{pp}", [128, 256], BF16)) for pp in range(2)]
    tmpu = [[st.enter_context(nc.sbuf_tensor(pfx + f"tmpu{pp}{i}", [128, 128], BF16))
             for i in range(2)] for pp in range(2)]
    Wrow = [[st.enter_context(nc.sbuf_tensor(pfx + f"Wrow{pp}{i}", [128, 128], BF16))
             for i in range(2)] for pp in range(2)]
    a0s2 = [[st.enter_context(nc.sbuf_tensor(pfx + f"a0s{pp}{i}", [128, 128], BF16))
             for i in range(2)] for pp in range(2)]
    obh = st.enter_context(nc.sbuf_tensor(pfx + "obh", [128, 16 * 128], BF16))

    pmT = st.enter_context(nc.psum_tensor(pfx + "pmT", [128, 640], F32))
    pm2 = [pmT.ap(), pmT.ap()]
    pstrT = st.enter_context(nc.psum_tensor(pfx + "pstrT", [128, 128], BF16))
    ps_tr2 = [pstrT.ap(), pstrT.ap()]
    psq2 = [st.enter_context(nc.psum_tensor(pfx + f"psqP{pp}", [128, 256], F32))
            for pp in range(2)]
    pVW2 = [st.enter_context(nc.psum_tensor(pfx + f"pVW{pp}", [128, 512], F32))
            for pp in range(2)]
    pV2 = [pVW2[0][:, 0:256], pVW2[1][:, 0:256]]
    pW2 = [pVW2[0][:, 256:512], pVW2[1][:, 256:512]]
    puS = st.enter_context(nc.psum_tensor(pfx + "puS", [128, 512], F32))
    pu = puS[:, 0:256]
    poS = puS[:, 256:512]

    def Gv(tsz):
        return G.ap().rearrange("p (c t) -> p c t", t=tsz)

    tk = {}
    with nc.Block() as block:
        @block.gpsimd
        def _(g):
            c.dinc(pfx + "msk", g.dma_start(out=mskK[:, :], in_=mskK_dr[:, :]))
            c.dinc(pfx + "msk", g.dma_start(out=mskQ[:, :], in_=mskQ_dr[:, :]))
            tk["msk"] = Tok("D_" + pfx + "msk", c.cnt["D_" + pfx + "msk"])

        for ibh, (b, h) in enumerate(bh_list):
            @block.gpsimd
            def _(g, b=b, h=h, ibh=ibh):
                c.wait(g, tk.get(("ost", ibh - 1)))
                c.dinc(pfx + "in", g.dma_start(out=kI[:, :], in_=kni_dr[b, h, :, :]))
                c.dinc(pfx + "in", g.dma_start(out=qnh[:, :], in_=qn_dr[h, :, b * T:(b + 1) * T]))
                c.dinc(pfx + "in", g.dma_start(out=laG[:, :], in_=la_dr[h, :, b * T:(b + 1) * T]))
                for i in range(4):
                    c.dinc(pfx + "in", g.dma_start(
                        out=brep[:, i * T:(i + 1) * T],
                        in_=beta_dr[h * 4 + i:h * 4 + i + 1,
                                    b * T:(b + 1) * T].partition_broadcast(128)))
                tk[("in", ibh)] = Tok("D_" + pfx + "in", c.cnt["D_" + pfx + "in"])

            @block.vector
            def _(v, ibh=ibh):
                c.wait(v, tk[("in", ibh)])
                for cc in range(NCH):
                    c.inc("vector", v.tensor_tensor_scan(
                        G[:, cc * 64:(cc + 1) * 64], laG[:, cc * 64:(cc + 1) * 64],
                        laG[:, cc * 64:(cc + 1) * 64], 0.0, ALU.add, ALU.bypass))
                tk[("G", ibh)] = c.mark("vector")
                c.wait(v, tk[("G", ibh)])
                t1 = c.inc("vector", v.tensor_copy(
                    glr[:, 0:NCH].rearrange("p (c o) -> p c o", o=1), Gv(64)[:, :, 63:64]))
                t2 = c.inc("vector", v.tensor_sub(
                    laG.ap().rearrange("p (blk t) -> p blk t", t=16),
                    G.ap().rearrange("p (blk t) -> p blk t", t=16),
                    G.ap().rearrange("p (blk t) -> p blk t", t=16)[:, :, 0:1]
                     .broadcast_to([128, 128, 16])))
                tk[("dpre", ibh)] = [t1, t2]

            @block.scalar
            def _(sc, ibh=ibh):
                c.wait(sc, tk[("dpre", ibh)])
                c.inc("scalar", sc.activation(e_r[:, :], laG[:, :], AF.Exp))
                c.inc("scalar", sc.activation(e_c[:, :], laG[:, :], AF.Exp, scale=-1.0))
                c.inc("scalar", sc.activation(e_g[:, :], G[:, :], AF.Exp))
                c.inc("scalar", sc.activation(glr[:, NCH:2 * NCH], glr[:, 0:NCH], AF.Exp))
                tk[("exp1", ibh)] = c.mark("scalar")

            @block.vector
            def _(v, ibh=ibh):
                c.wait(v, tk[("exp1", ibh)])
                tk[("dd", ibh)] = c.inc("vector", v.tensor_sub(
                    laG.ap().rearrange("p (c t) -> p c t", t=64),
                    Gv(64)[:, :, 63:64].broadcast_to([128, NCH, 64]),
                    Gv(64)))

            @block.scalar
            def _(sc, ibh=ibh):
                c.wait(sc, tk[("dd", ibh)])
                tk[("e_dx", ibh)] = c.inc("scalar", sc.activation(e_d[:, :], laG[:, :], AF.Exp))

            for I in (1, 2, 3):
                @block.vector
                def _(v, ibh=ibh, I=I):
                    c.wait(v, tk[("G", ibh)])
                    if I > 1:
                        c.wait(v, tk[("ecBx", ibh, I - 1)])
                    tk[("dB", ibh, I)] = c.inc("vector", v.tensor_sub(
                        tmpB[:, 0:NCH * 16 * I].rearrange("p (cc s) -> p cc s", s=16 * I),
                        Gv(64)[:, :, 16 * I:16 * I + 1].broadcast_to([128, NCH, 16 * I]),
                        Gv(64)[:, :, 0:16 * I]))

                @block.scalar
                def _(sc, ibh=ibh, I=I):
                    c.wait(sc, tk[("dB", ibh, I)])
                    tk[("ecBx", ibh, I)] = c.inc("scalar", sc.activation(
                        ecB[I][:, :], tmpB[:, 0:NCH * 16 * I], AF.Exp))

            @block.vector
            def _(v, ibh=ibh):
                c.wait(v, tk[("exp1", ibh)], tk[("in", ibh)])
                t1 = c.inc("vector", v.tensor_mul(
                    w_.ap().rearrange("p (t i) -> p t i", i=4),
                    kI.ap().rearrange("p (t i) -> p t i", i=4),
                    brep.ap().rearrange("p (i t) -> p t i", t=T)))
                t2 = c.inc("vector", v.tensor_mul(
                    khA.ap().rearrange("p (t i) -> p t i", i=4),
                    kI.ap().rearrange("p (t i) -> p t i", i=4),
                    e_r.ap().rearrange("p (t o) -> p t o", o=1).broadcast_to([128, T, 4])))
                c.wait(v, t1)
                t3 = c.inc("vector", v.tensor_mul(
                    whA.ap().rearrange("p (t i) -> p t i", i=4),
                    w_.ap().rearrange("p (t i) -> p t i", i=4),
                    e_c.ap().rearrange("p (t o) -> p t o", o=1).broadcast_to([128, T, 4])))
                c.wait(v, tk[("e_dx", ibh)])
                t4 = c.inc("vector", v.tensor_mul(
                    kdec.ap().rearrange("p (t i) -> p t i", i=4),
                    w_.ap().rearrange("p (t i) -> p t i", i=4),
                    e_d.ap().rearrange("p (t o) -> p t o", o=1).broadcast_to([128, T, 4])))
                t5 = c.inc("vector", v.tensor_mul(qhA[:, :], qnh[:, :], e_r[:, :]))
                c.wait(v, t5)
                t6 = c.inc("vector", v.tensor_mul(qgn[:, :], qnh[:, :], e_g[:, :]))
                c.wait(v, t6)
                t7 = c.inc("vector", v.tensor_scalar(qgn[:, :], qgn[:, :], -1.0, None, ALU.mult))
                tk[("modv", ibh)] = [t2, t3, t4, t5, t7]

            @block.gpsimd
            def _(g, ibh=ibh):
                c.wait(g, tk[("exp1", ibh)], tk[("modv", ibh)])
                tk[("modg", ibh)] = c.inc("gpsimd", g.tensor_mul(
                    kexp.ap().rearrange("p (t i) -> p t i", i=4),
                    kI.ap().rearrange("p (t i) -> p t i", i=4),
                    e_g.ap().rearrange("p (t o) -> p t o", o=1).broadcast_to([128, T, 4])))

            for I in (1, 2, 3):
                @block.gpsimd
                def _(g, ibh=ibh, I=I):
                    c.wait(g, tk[("ecBx", ibh, I)], tk[("modv", ibh)])
                    tk[("modB", ibh, I)] = c.inc("gpsimd", g.tensor_mul(
                        whB[I].ap().rearrange("p (cc s i) -> p cc s i", s=16 * I, i=4),
                        w_.ap().rearrange("p (cc m) -> p cc m", m=256)[:, :, 0:64 * I]
                           .rearrange("p cc (s i) -> p cc s i", i=4),
                        ecB[I].ap().rearrange("p (cc s o) -> p cc s o", s=16 * I, o=1)
                              .broadcast_to([128, NCH, 16 * I, 4])))

            @block.vector
            def _(v, ibh=ibh):
                c.wait(v, tk[("in", ibh)], tk.get(("sbf", ibh - 1, NCH - 1)))
                t1 = c.inc("vector", v.memset(negS[:, :], 0.0))
                c.wait(v, t1)
                tk[("S0", ibh)] = c.inc("vector", v.tensor_copy(negSb[:, :], negS[:, :]))
            tk[("sbf", ibh, -1)] = tk[("S0", ibh)]

            def stageP(cc, blk_o, ibh=ibh, b=b, h=h):
                mc, tc, pc = cc * 256, cc * 64, cc % 2
                msm, kdT, Wt, uv, vrow = msm2[pc], kdT2[pc], Wt2[pc], uv2[pc], vrow2[pc]

                @blk_o.sync
                def _(sy, cc=cc, mc=mc, pc=pc):
                    c.wait(sy, tk.get(("uvdone", ibh, cc - 2)))
                    c.dinc(pfx + f"vld{pc}", sy.dma_start(
                        out=vrow2[pc][:, 0:128], in_=vni_dr[b, h, :, mc:mc + 128], transpose=True))
                    c.dinc(pfx + f"vld{pc}", sy.dma_start(
                        out=vrow2[pc][:, 128:256], in_=vni_dr[b, h, :, mc + 128:mc + 256],
                        transpose=True))
                    tk[("vld", ibh, cc)] = Tok("D_" + pfx + f"vld{pc}",
                                               c.cnt["D_" + pfx + f"vld{pc}"])

                @blk_o.tensor
                def _(t, cc=cc, mc=mc, tc=tc):
                    c.wait(t, tk[("modv", ibh)], tk[("modg", ibh)],
                           [tk[("modB", ibh, I)] for I in (1, 2, 3)], tk["msk"])
                    c.wait(t, tk.get(("fevac", ibh, cc - 1)))
                    for j in range(4):
                        rt, base = j // 2, 64 * (j % 2)
                        t.matmul(pm2[pc][base:base + 64, rt * 320 + 64 * j:rt * 320 + 64 * j + 64],
                                 whA[:, mc + 64 * j: mc + 64 * j + 64],
                                 khA[:, mc + 64 * j: mc + 64 * j + 64], start=True, stop=True)
                        t.matmul(pm2[pc][base:base + 64, rt * 320 + 256 + 16 * j:rt * 320 + 256 + 16 * j + 16],
                                 whA[:, mc + 64 * j: mc + 64 * j + 64],
                                 qhA[:, tc + 16 * j: tc + 16 * j + 16], start=True, stop=True)
                    for I in (1, 2, 3):
                        wB = whB[I]
                        mb = cc * 64 * I
                        if I < 3:
                            t.matmul(pm2[pc][0:64 * I, 64 * I:64 * I + 64], wB[:, mb: mb + 64 * I],
                                     khA[:, mc + 64 * I: mc + 64 * I + 64], start=True, stop=True)
                            last = t.matmul(pm2[pc][0:64 * I, 256 + 16 * I:256 + 16 * I + 16],
                                            wB[:, mb: mb + 64 * I],
                                            qhA[:, tc + 16 * I: tc + 16 * I + 16],
                                            start=True, stop=True)
                        else:
                            t.matmul(pm2[pc][0:128, 192:256], wB[:, mb: mb + 128],
                                     khA[:, mc + 192: mc + 256], start=True, stop=True)
                            t.matmul(pm2[pc][0:64, 320 + 192:320 + 256], wB[:, mb + 128: mb + 192],
                                     khA[:, mc + 192: mc + 256], start=True, stop=True)
                            t.matmul(pm2[pc][0:128, 304:320], wB[:, mb: mb + 128],
                                     qhA[:, tc + 48: tc + 64], start=True, stop=True)
                            last = t.matmul(pm2[pc][0:64, 320 + 304:320 + 320], wB[:, mb + 128: mb + 192],
                                            qhA[:, tc + 48: tc + 64], start=True, stop=True)
                    tk[("fmm", ibh, cc)] = c.inc("tensor", last)

                @blk_o.gpsimd
                def _(g, cc=cc, pc=pc):
                    c.wait(g, tk.get(("omm", ibh, cc - 2)))
                    tk[("msz", ibh, cc)] = c.inc("gpsimd", g.memset(msm2[pc][:, :], 0.0))

                @blk_o.vector
                def _(v, cc=cc, pc=pc):
                    c.wait(v, tk[("fmm", ibh, cc)], tk[("msz", ibh, cc)])
                    ts = []
                    for j in range(4):
                        rt, base = j // 2, 64 * (j % 2)
                        ts.append(c.inc("vector", v.tensor_mul(
                            msm2[pc][base:base + 64, rt * 320 + 64 * j: rt * 320 + 64 * j + 64],
                            pm2[pc][base:base + 64, rt * 320 + 64 * j:rt * 320 + 64 * j + 64],
                            mskK[base:base + 64, :])))
                        ts.append(c.inc("vector", v.tensor_mul(
                            msm2[pc][base:base + 64,
                                     rt * 320 + 256 + 16 * j: rt * 320 + 256 + 16 * j + 16],
                            pm2[pc][base:base + 64, rt * 320 + 256 + 16 * j:rt * 320 + 256 + 16 * j + 16],
                            mskQ[base:base + 64, :])))
                    tk[("evA", ibh, cc)] = ts

                @blk_o.vector
                def _(v, cc=cc, pc=pc):
                    c.wait(v, tk[("fmm", ibh, cc)], tk[("msz", ibh, cc)])
                    msm = msm2[pc]
                    t1 = c.inc("vector", v.tensor_scalar(
                        msm[0:64, 64:256], pm2[pc][0:64, 64:256], -1.0, None, ALU.mult))
                    t2 = c.inc("vector", v.tensor_scalar(
                        msm[64:128, 128:256], pm2[pc][64:128, 128:256], -1.0, None, ALU.mult))
                    t3 = c.inc("vector", v.tensor_scalar(
                        msm[0:64, 320 + 192:320 + 256],
                        pm2[pc][0:64, 320 + 192:320 + 256], -1.0, None, ALU.mult))
                    t4 = c.inc("vector", v.tensor_copy(msm[0:64, 272:320], pm2[pc][0:64, 272:320]))
                    t5 = c.inc("vector", v.tensor_copy(msm[64:128, 288:320], pm2[pc][64:128, 288:320]))
                    t6 = c.inc("vector", v.tensor_copy(msm[0:64, 320 + 304:320 + 320],
                                                       pm2[pc][0:64, 320 + 304:320 + 320]))
                    tk[("evB", ibh, cc)] = [t1, t2, t3, t4, t5, t6]
                    tk[("fevac", ibh, cc)] = [tk[("evA", ibh, cc)], tk[("evB", ibh, cc)]]

                trs = [("s0", lambda msm=msm2[pc]: msm[:, 0:128], a0s2[pc][0], 1.0, "vector"),
                       ("s1", lambda msm=msm2[pc]: msm[:, 320 + 128:320 + 256], a0s2[pc][1], 1.0, "vector"),
                       ("k0", lambda mc=mc: kdec[:, mc:mc + 128], (kdT2[pc], 0), -1.0, "scalar"),
                       ("k1", lambda mc=mc: kdec[:, mc + 128:mc + 256], (kdT2[pc], 1), -1.0, "scalar")]
                for ti, (nm, src, dst, sgn, eng) in enumerate(trs):
                    blk_o.begin_atomic(2)
                    @blk_o.tensor
                    def _(t, cc=cc, ti=ti, nm=nm, src=src):
                        if nm in ("s0", "s1"):
                            c.wait(t, tk[("fevac", ibh, cc)], ctok)
                        else:
                            c.wait(t, tk[("modv", ibh)], ctok)
                        c.wait(t, tk.get(("trchain", ibh)))
                        if nm.startswith("k"):
                            c.wait(t, tk.get(("smm", ibh, cc - 2)))
                        tk[("tr", ibh, cc, ti)] = c.inc("tensor", t.transpose(
                            ps_tr2[pc][:, :], src(), identb[:, :]))

                    if eng == "vector":
                        @blk_o.vector
                        def _(v, cc=cc, ti=ti, dst=dst):
                            c.wait(v, tk[("tr", ibh, cc, ti)])
                            c.wait(v, tk.get(("ch1", ibh, cc - 2)))
                            tk[("trev", ibh, cc, ti)] = c.inc("vector", v.tensor_copy(
                                dst[:, :], ps_tr2[pc][:, :]))
                            tk[("trchain", ibh)] = tk[("trev", ibh, cc, ti)]
                    else:
                        @blk_o.vector
                        def _(v, cc=cc, ti=ti, dst=dst):
                            c.wait(v, tk[("tr", ibh, cc, ti)])
                            buf, half = dst
                            tk[("trev", ibh, cc, ti)] = c.inc("vector", v.tensor_scalar(
                                buf[:, half * 128:(half + 1) * 128], ps_tr2[pc][:, :],
                                -1.0, None, ALU.mult))
                            tk[("trchain", ibh)] = tk[("trev", ibh, cc, ti)]

                for blk in range(2):
                    @blk_o.vector
                    def _(v, cc=cc, blk=blk, pc=pc):
                        msm = msm2[pc]
                        b0aps = [msm[:, 0:128], msm[:, 320 + 128:320 + 256]]
                        c.wait(v, tk[("fevac", ibh, cc)], ctok)
                        c.wait(v, tk.get(("dapl", ibh, cc - 2)))
                        tk[("dx0", ibh, cc, blk)] = c.inc("vector", v.tensor_add(
                            tmpu[pc][blk][:, :], b0aps[blk], identb[:, :]))

                for lvl in range(1, 7):
                    @blk_o.tensor
                    def _(t, cc=cc, lvl=lvl, pc=pc):
                        msm = msm2[pc]
                        b0aps = [msm[:, 0:128], msm[:, 320 + 128:320 + 256]]
                        if lvl == 1:
                            c.wait(t, tk[("trev", ibh, cc, 0)], tk[("trev", ibh, cc, 1)])
                            c.wait(t, tk.get(("chevA", ibh, cc - 2, 6)))
                        else:
                            c.wait(t, tk[("chevB", ibh, cc, lvl - 1)])
                        last = None
                        for blk in range(2):
                            aprev = (aB4[pc][blk][(lvl - 1) % 2][:, :] if lvl > 1 else a0s2[pc][blk][:, :])
                            bprev = (bB4[pc][blk][(lvl - 1) % 2][:, :] if lvl > 1 else b0aps[blk])
                            last = t.matmul(psq2[pc][:, blk * 128:(blk + 1) * 128],
                                            bprev, aprev, start=True, stop=True)
                        tk[("chmmA", ibh, cc, lvl)] = c.inc("tensor", last)

                    @blk_o.vector
                    def _(v, cc=cc, lvl=lvl):
                        c.wait(v, tk[("chmmA", ibh, cc, lvl)])
                        last = None
                        for blk in range(2):
                            last = c.inc("vector", v.tensor_copy(
                                aB4[pc][blk][lvl % 2][:, :], psq2[pc][:, blk * 128:(blk + 1) * 128]))
                        tk[("chevA", ibh, cc, lvl)] = last

                    @blk_o.tensor
                    def _(t, cc=cc, lvl=lvl, pc=pc):
                        msm = msm2[pc]
                        b0aps = [msm[:, 0:128], msm[:, 320 + 128:320 + 256]]
                        c.wait(t, tk[("chevA", ibh, cc, lvl)])
                        if lvl == 1:
                            c.wait(t, tk[("dx0", ibh, cc, 0)], tk[("dx0", ibh, cc, 1)])
                            c.wait(t, tk.get(("pVfree", ibh, cc - 2)))
                        else:
                            c.wait(t, tk[("dadd", ibh, cc, lvl - 1)])
                        last = None
                        lastpv = None
                        for blk in range(2):
                            aprev = (aB4[pc][blk][(lvl - 1) % 2][:, :] if lvl > 1 else a0s2[pc][blk][:, :])
                            bprev = (bB4[pc][blk][(lvl - 1) % 2][:, :] if lvl > 1 else b0aps[blk])
                            lastpv = t.matmul(pV2[pc][:, blk * 128:(blk + 1) * 128],
                                              aB4[pc][blk][lvl % 2][:, :], tmpu[pc][blk][:, :],
                                              start=True, stop=True)
                            if lvl < 6:
                                last = t.matmul(psq2[pc][:, blk * 128:(blk + 1) * 128],
                                                aprev, bprev, start=True, stop=True)
                        tk[("chmmB", ibh, cc, lvl)] = c.inc("tensor", last if lvl < 6 else lastpv)
                        if lvl == 1:
                            tk[("ch1", ibh, cc)] = tk[("chmmB", ibh, cc, lvl)]

                    @blk_o.vector
                    def _(v, cc=cc, lvl=lvl):
                        c.wait(v, tk[("chmmB", ibh, cc, lvl)])
                        last = None
                        for blk in range(2):
                            dst = tmpu[pc][blk][:, :] if lvl < 6 else Dnv[pc][blk][:, :]
                            last = c.inc("vector", v.tensor_add(
                                dst, pV2[pc][:, blk * 128:(blk + 1) * 128], tmpu[pc][blk][:, :]))
                        tk[("dadd", ibh, cc, lvl)] = last
                        if lvl == 6:
                            pass
                        if lvl < 6:
                            last2 = None
                            for blk in range(2):
                                last2 = c.inc("vector", v.tensor_copy(
                                    bB4[pc][blk][lvl % 2][:, :], psq2[pc][:, blk * 128:(blk + 1) * 128]))
                            tk[("chevB", ibh, cc, lvl)] = last2
                        else:
                            tk[("Dfin", ibh, cc)] = last
                            tk[("dapl", ibh, cc)] = last

                @blk_o.tensor
                def _(t, cc=cc, pc=pc, mc=mc):
                    msm = msm2[pc]
                    c.wait(t, tk[("Dfin", ibh, cc)], tk[("vld", ibh, cc)])
                    c.wait(t, tk.get(("slv_ev", ibh, cc - 2)))
                    t.matmul(pV2[pc][:, 0:128], Dnv[pc][0][:, :], vrow2[pc][:, 0:128], start=True, stop=True)
                    last = t.matmul(pW2[pc][:, 0:128], kexp[:, mc:mc + 128], identb[:, :],
                                    start=True, stop=True)
                    tk[("slv1", ibh, cc)] = c.inc("tensor", last)

                @blk_o.vector
                def _(v, cc=cc, pc=pc):
                    c.wait(v, tk[("slv1", ibh, cc)])
                    c.inc("vector", v.tensor_copy(uv2[pc][:, 0:128], pV2[pc][:, 0:128]))
                    tk[("slv1ev", ibh, cc)] = c.inc("vector", v.tensor_copy(
                        tmpw[pc][:, 0:128], pW2[pc][:, 0:128]))

                @blk_o.tensor
                def _(t, cc=cc, pc=pc, mc=mc):
                    msm = msm2[pc]
                    c.wait(t, tk[("slv1ev", ibh, cc)])
                    t.matmul(pV2[pc][:, 128:256], identb[:, :], vrow2[pc][:, 128:256],
                             start=True, stop=False)
                    t.matmul(pV2[pc][:, 128:256], msm[:, 128:256], uv2[pc][:, 0:128],
                             start=False, stop=True)
                    last = t.matmul(pW2[pc][:, 128:256], Dnv[pc][0][:, :], tmpw[pc][:, 0:128],
                                    start=True, stop=True)
                    tk[("slv2", ibh, cc)] = c.inc("tensor", last)

                @blk_o.vector
                def _(v, cc=cc, pc=pc):
                    c.wait(v, tk[("slv2", ibh, cc)])
                    c.inc("vector", v.tensor_copy(tmpw[pc][:, 128:256], pV2[pc][:, 128:256]))
                    tk[("slv2ev", ibh, cc)] = c.inc("vector", v.tensor_copy(
                        Wrow[pc][0][:, :], pW2[pc][:, 128:256]))

                @blk_o.tensor
                def _(t, cc=cc, pc=pc, mc=mc):
                    msm = msm2[pc]
                    c.wait(t, tk[("slv2ev", ibh, cc)])
                    t.matmul(pV2[pc][:, 0:128], Dnv[pc][1][:, :], tmpw[pc][:, 128:256], start=True, stop=True)
                    t.matmul(pW2[pc][:, 0:128], kexp[:, mc + 128:mc + 256], identb[:, :],
                             start=True, stop=False)
                    last = t.matmul(pW2[pc][:, 0:128], msm[:, 128:256], Wrow[pc][0][:, :],
                                    start=False, stop=True)
                    tk[("slv3", ibh, cc)] = c.inc("tensor", last)

                @blk_o.vector
                def _(v, cc=cc, pc=pc):
                    c.wait(v, tk[("slv3", ibh, cc)])
                    c.inc("vector", v.tensor_copy(uv2[pc][:, 128:256], pV2[pc][:, 0:128]))
                    tk[("slv3ev", ibh, cc)] = c.inc("vector", v.tensor_copy(
                        tmpw[pc][:, 0:128], pW2[pc][:, 0:128]))

                blk_o.begin_atomic(2)
                @blk_o.tensor
                def _(t, cc=cc, pc=pc):
                    c.wait(t, tk[("slv3ev", ibh, cc)])
                    t.matmul(pW2[pc][:, 128:256], Dnv[pc][1][:, :], tmpw[pc][:, 0:128], start=True, stop=True)
                    c.wait(t, tk.get(("trchain", ibh)))
                    last = t.transpose(ps_tr2[pc][:, :], Wrow[pc][0][:, :], identb[:, :])
                    tk[("slv4", ibh, cc)] = c.inc("tensor", last)

                @blk_o.vector
                def _(v, cc=cc, pc=pc):
                    c.wait(v, tk[("slv4", ibh, cc)])
                    c.wait(v, tk.get(("umm", ibh, cc - 2)))
                    tk[("trevW", ibh, cc, 0)] = c.inc("vector", v.tensor_copy(
                        Wt2[pc][:, 0:128], ps_tr2[pc][:, :]))
                    tk[("trchain", ibh)] = tk[("trevW", ibh, cc, 0)]

                @blk_o.vector
                def _(v, cc=cc, pc=pc):
                    c.wait(v, tk[("slv4", ibh, cc)], tk.get(("slv5", ibh, cc - 2)))
                    tk[("slv4ev", ibh, cc)] = c.inc("vector", v.tensor_copy(
                        Wrow[pc][1][:, :], pW2[pc][:, 128:256]))

                blk_o.begin_atomic(2)
                @blk_o.tensor
                def _(t, cc=cc, pc=pc):
                    c.wait(t, tk[("slv4ev", ibh, cc)], tk[("trevW", ibh, cc, 0)])
                    c.wait(t, tk.get(("trchain", ibh)))
                    tk[("slv5", ibh, cc)] = c.inc("tensor", t.transpose(
                        ps_tr2[pc][:, :], Wrow[pc][1][:, :], identb[:, :]))

                @blk_o.vector
                def _(v, cc=cc, pc=pc):
                    c.wait(v, tk[("slv5", ibh, cc)])
                    tk[("trevW", ibh, cc, 1)] = c.inc("vector", v.tensor_copy(
                        Wt2[pc][:, 128:256], ps_tr2[pc][:, :]))
                    tk[("trchain", ibh)] = tk[("trevW", ibh, cc, 1)]
                    tk[("slv_ev", ibh, cc)] = [tk[("slv4ev", ibh, cc)], tk[("slv3ev", ibh, cc)]]
                    tk[("pVfree", ibh, cc)] = [tk[("slv2ev", ibh, cc)], tk[("slv3ev", ibh, cc)], tk[("slv4ev", ibh, cc)]]
                    tk[("Vfinev", ibh, cc, 1)] = tk[("slv3ev", ibh, cc)]
                    tk[("uvdone", ibh, cc)] = tk[("slv3", ibh, cc)]



            def serial(cc, ibh=ibh, b=b, h=h):
                mc, tc, pc = cc * 256, cc * 64, cc % 2
                msm, kdT, Wt, uv = msm2[pc], kdT2[pc], Wt2[pc], uv2[pc]

                @block.tensor
                def _(t, cc=cc, pc=pc, tc=tc):
                    c.wait(t, tk[("Vfinev", ibh, cc, 1)], tk[("trevW", ibh, cc, 1)],
                           tk[("sbf", ibh, cc - 1)])
                    c.wait(t, tk.get(("usbev", ibh, cc - 2)))
                    t.matmul(pu[:, 0:128], Wt2[pc][:, 0:128], negSb[:, :], start=True, stop=True)
                    last = t.matmul(pu[:, 128:256], Wt2[pc][:, 128:256], negSb[:, :],
                                    start=True, stop=True)
                    tk[("umm", ibh, cc)] = c.inc("tensor", last)

                @block.vector
                def _(v, cc=cc, pc=pc):
                    c.wait(v, tk[("umm", ibh, cc)])
                    c.wait(v, tk.get(("smm", ibh, cc - 1)))
                    c.inc("vector", v.tensor_add(usb2[pc][:, 0:128], pu[:, 0:128],
                                                 uv2[pc][:, 0:128]))
                    tk[("usbev", ibh, cc)] = c.inc("vector", v.tensor_add(
                        usb2[pc][:, 128:256], pu[:, 128:256], uv2[pc][:, 128:256]))

                @block.tensor
                def _(t, cc=cc, pc=pc, tc=tc):
                    msm = msm2[pc]
                    c.wait(t, tk[("usbev", ibh, cc)])
                    c.wait(t, tk.get(("oev", ibh, cc - 1)))
                    t.matmul(poS[0:64, 0:128], msm[0:128, 256:320], usb2[pc][:, 0:128],
                             start=True, stop=False)
                    t.matmul(poS[0:64, 0:128], msm[0:128, 320 + 256:320 + 320], usb2[pc][:, 128:256],
                             start=False, stop=False)
                    tk[("omm", ibh, cc)] = c.inc("tensor", t.matmul(
                        poS[0:64, 0:128], qgn[:, tc:tc + 64], negSb[:, :], start=False, stop=True))

                @block.vector
                def _(v, cc=cc):
                    c.wait(v, tk[("omm", ibh, cc)])
                    if cc <= 1:
                        c.wait(v, tk.get(("ost", ibh - 1)))
                    tk[("oev", ibh, cc)] = c.inc("vector", v.tensor_scalar(
                        obh[64 * (cc % 2):64 * (cc % 2) + 64, (cc // 2) * 128:(cc // 2 + 1) * 128],
                        poS[0:64, 0:128], MKSCALE, None, ALU.mult))

                @block.tensor
                def _(t, cc=cc, pc=pc):
                    c.wait(t, tk.get(("sup", ibh, cc - 1)), tk[("oev", ibh, cc)])
                    t.matmul(poS[:, 128:256], kdT2[pc][:, 0:128], usb2[pc][:, 0:128],
                             start=True, stop=False)
                    tk[("smm", ibh, cc)] = c.inc("tensor", t.matmul(
                        poS[:, 128:256], kdT2[pc][:, 128:256], usb2[pc][:, 128:256],
                        start=False, stop=True))

                @block.vector
                def _(v, cc=cc):
                    c.wait(v, tk[("smm", ibh, cc)])
                    tk[("sup", ibh, cc)] = c.inc("vector", v.scalar_tensor_tensor(
                        negS[:, :], negS[:, :], glr[:, NCH + cc:NCH + cc + 1], poS[:, 128:256],
                        ALU.mult, ALU.add))

                @block.vector
                def _(v, cc=cc):
                    c.wait(v, tk[("sup", ibh, cc)])
                    tk[("sbf", ibh, cc)] = c.inc("vector", v.tensor_copy(negSb[:, :], negS[:, :]))

            STAG = 5
            for p0 in range(0, NCH, 2):
                pa, pb = _BlockProxy(), _BlockProxy()
                stageP(p0, pa)
                stageP(p0 + 1, pb)
                _replay(block, pa.steps[:STAG])
                sa, sb = pa.steps[STAG:], pb.steps
                n = max(len(sa), len(sb))
                for i in range(n):
                    if i < len(sa):
                        _replay(block, [sa[i]])
                    if i < len(sb):
                        _replay(block, [sb[i]])
                serial(p0)
                serial(p0 + 1)

            @block.sync
            def _(sy, ibh=ibh, b=b, h=h):
                c.wait(sy, [tk[("oev", ibh, cc)] for cc in range(NCH)])
                tk[("ost", ibh)] = c.dinc(pfx + "ost", sy.dma_start(
                    out=o_dr[b, h].rearrange("(cc p) dv -> p cc dv", p=128),
                    in_=obh.ap().rearrange("p (cc dv) -> p cc dv", dv=128)))

        @block.sync
        def _(sy):
            c.wait(sy, tk[("ost", len(bh_list) - 1)])
    return
